# version 06
# baseline (speedup 1.0000x reference)
import numpy as np
import ml_dtypes
import concourse.bass as bass
import concourse.mybir as mybir
from concourse.bass_utils import run_bass_kernel_spmd
from concourse.bass_core_rust import IndirectOffsetOnAxis
from contextlib import ExitStack

F32 = mybir.dt.float32
BF16 = mybir.dt.bfloat16
F16 = mybir.dt.float16
I32 = mybir.dt.int32
AF = mybir.ActivationFunctionType
ALU = mybir.AluOpType
AX = mybir.AxisListType

T, C, H, D = 4096, 1024, 16, 64
NJ = 8
CH = 128
SLAB = 512
NSLAB = T // SLAB
CPS = SLAB // CH
NPAIR = NJ
NI = NSLAB * CPS * NPAIR
IPS = CPS * NPAIR
TPS = SLAB // 128
NE, DF = 8, 4096
NFT = DF // 128
TB = 256
NTB = T // TB
NTBG = NE * NTB
NTI = T // 128

DEBUG = set()
MAX_PHASE = 4
SPARSE = True
NB = 6
UPE = 2 * NB
NBG = NE * NB
PADPOS = T
RET_MODE = 0


def host_tables():
    half = np.arange(0, D, 2, dtype=np.float64)
    inv_freq = 1.0 / (10000.0 ** (half / D))
    t = np.arange(T, dtype=np.float64)
    freqs = np.repeat(t[:, None] * inv_freq[None, :], 2, axis=-1)
    scale_vec = (half + 0.4 * D) / (1.4 * D)
    power = (t - T // 2) / 512.0
    scale = np.repeat(scale_vec[None, :] ** power[:, None], 2, axis=-1)
    cos, sin = np.cos(freqs), np.sin(freqs)

    def dup(a):
        a = np.ascontiguousarray(a.T.astype(np.float32))
        return np.concatenate([a, a], axis=0)

    tabs = {"csq": dup(cos * scale), "ssq": dup(sin * scale),
            "csk": dup(cos / scale), "ssk": dup(sin / scale)}
    gamma = 1.0 - 2.0 ** (-5.0 - np.arange(H, dtype=np.float64))
    ii = np.arange(CH)
    diff = ii[None, :] - ii[:, None]
    m = np.where(diff >= 0, gamma[:, None, None] ** np.maximum(diff, 0)[None], 0.0)
    m = m * (D ** -0.5)
    tabs["maskr"] = np.ascontiguousarray(m.transpose(1, 0, 2).astype(np.float32))
    gp = np.zeros((128, NJ, CH), np.float64)
    kg = np.zeros((128, NJ, 128), np.float64)
    for p in range(NJ):
        for par in range(2):
            gp[par * 64:(par + 1) * 64, p, :] = gamma[2 * p + par] ** (ii + 1.0) * (D ** -0.5)
            kg[:, p, par * 64:(par + 1) * 64] = (gamma[2 * p + par] ** (CH - 1.0 - ii))[:, None]
    tabs["gampair"] = gp.astype(np.float32)
    tabs["kgampair"] = kg.astype(np.float32)
    tabs["gam_c"] = [float(g ** CH) for g in gamma]
    P = np.zeros((64, 64), np.float32)
    for i in range(32):
        P[2 * i, 2 * i + 1] = -1.0
        P[2 * i + 1, 2 * i] = 1.0
    P2 = np.zeros((128, 128), np.float32)
    P2[0:64, 0:64] = P
    P2[64:128, 64:128] = P
    tabs["p2t"] = np.ascontiguousarray(P2.T)
    tabs["ident"] = np.eye(128, dtype=np.float32)
    h2 = np.zeros((128, 2), np.float32)
    h2[0:64, 0] = 1.0
    h2[64:128, 1] = 1.0
    tabs["h2"] = h2
    tabs["g2"] = np.ascontiguousarray(h2.T)
    idc = np.zeros((128, T // 128), np.float32)
    for ti in range(T // 128):
        idc[:, ti] = ti * 128 + np.arange(128)
    tabs["idcol"] = idc
    return tabs


def build_program(gam_c):
    nc = bass.Bass("TRN2", target_bir_lowering=False, debug=False)

    def din(name, shape):
        return nc.dram_tensor(name, shape, F32, kind="ExternalInput").ap()

    x_d = din("x", [T, C])
    wqkvT_d = din("wqkvT", [C, 3 * C])
    wgT_d = din("wgT", [C, C])
    wpT_d = din("wpT", [C, C])
    gnw_d = din("gnw_r", [128, NJ])
    gnb_d = din("gnb_r", [128, NJ])
    Wg_d = din("Wg", [C, NE])
    bg_d = din("bg", [1, NE])
    b1_d = din("b1r", [NE, 128, NFT])
    b2_d = din("b2", [NE, C])
    W1_d = din("W1", [NE, C, DF])
    W2_d = din("W2", [NE, DF, C])
    csq_d = din("csq", [128, T]); ssq_d = din("ssq", [128, T])
    csk_d = din("csk", [128, T]); ssk_d = din("ssk", [128, T])
    maskr_d = din("maskr", [128, H, CH])
    gampair_d = din("gampair", [128, NJ, CH])
    kgam_d = din("kgampair", [128, NJ, 128])
    p2t_d = din("p2t", [128, 128])
    ident_d = din("ident", [128, 128])
    h2_d = din("h2", [128, 2])
    g2_d = din("g2", [2, 128])
    idcol_d = din("idcol", [128, NTI])

    out_d = nc.dram_tensor("out", [T, C], F32, kind="ExternalOutput").ap()
    ytk = "ExternalOutput" if "ph2" in DEBUG else "Internal"
    yT_hbm = nc.dram_tensor("yT_hbm", [128, NJ, T], BF16, kind=ytk).ap()
    acck = "ExternalOutput" if "ph3" in DEBUG else "Internal"
    acc_hbm = nc.dram_tensor("acc_hbm", [T + 128, C], F16, kind=acck).ap()
    xn_hbm = nc.dram_tensor("xn_hbm", [T + 128, C], BF16, kind="Internal").ap()
    xnt_hbm = nc.dram_tensor("xnt_hbm", [128, NJ, T], BF16, kind="Internal").ap()
    pair_hbm = nc.dram_tensor("pair_hbm", [NE, T + 128, 2], F32, kind="Internal").ap()
    wt_hbm = nc.dram_tensor("wt_hbm", [NE, T], BF16, kind="Internal").ap()
    if "ph1" in DEBUG:
        dbg_xnT = nc.dram_tensor("dbg_xnT", [128, NJ, T], BF16, kind="ExternalOutput").ap()
        dbg_w = nc.dram_tensor("dbg_w", [128, NTI, NE], F32, kind="ExternalOutput").ap()
    if "ph2" in DEBUG:
        dbg_st = nc.dram_tensor("dbg_st", [128, 2 * NJ], F32, kind="ExternalOutput").ap()

    ctx = ExitStack()
    sbp = lambda name, shape, dt: ctx.enter_context(nc.sbuf_tensor(name, shape, dt))
    if not SPARSE:
        xnT = sbp("xnT", [128, NJ, T], BF16)
    w_nat = sbp("w_nat", [128, NTI, NE], F32)
    b1_sb = sbp("b1_sb", [128, NE, NFT], F32)
    S1acc = sbp("S1acc", [128, NJ], F32)
    S2acc = sbp("S2acc", [128, NJ], F32)
    identb = sbp("identb", [128, 128], BF16)
    ident32 = sbp("ident32", [128, 128], F32)

    sems = {}
    def S(name):
        sems[name] = Sem = nc.alloc_semaphore(name)
        return Sem
    pre = S("pre"); pre_w = S("pre_w"); pre_v = S("pre_v")
    ph = S("ph"); vch = S("vch")
    PH = [0]
    _vchn = [0]

    def VC(v, inst):
        inst.then_inc(vch, 1)
        _vchn[0] += 1
        v.wait_ge(vch, _vchn[0])

    def W(e, sem, val):
        if val > 0:
            e.wait_ge(sem, val)

    sch = S("sch")
    _schn = [0]

    def SC(eng, inst):
        inst.then_inc(sch, 1)
        _schn[0] += 1
        eng.wait_ge(sch, _schn[0])

    PRE_TOT = 16 * (5 if SPARSE else 4)
    ctx12 = ExitStack()
    sb12 = lambda name, shape, dt: ctx12.enter_context(nc.sbuf_tensor(name, shape, dt))
    wq_bf = sb12("wq_bf", [128, NJ, 3 * C], BF16)
    p2tb = sb12("p2tb", [128, 128], BF16)
    if SPARSE:
        offn = sb12("offn", [128, NTI, NE], F32)
        offi = sb12("offi", [128, NTI, NE], I32)
        idcol_sb = sb12("idcol_sb", [128, NTI], F32)
        pairs_all = sb12("pairs_all", [128, NTI, NE, 2], F32)
        ones8 = sb12("ones8", [128, NE], F32)

    ld_x2 = [S("ld_x0"), S("ld_x1")]
    s_sq = S("s_sq"); v_va = S("v_va"); s_rstd = S("s_rstd")
    s_xn = S("s_xn"); t_tr = S("t_tr"); s_tre = S("s_tre")
    v_tre = S("v_tre"); t_gate = S("t_gate"); v_g = S("v_g"); v_sm1 = S("v_sm1")
    s_e = S("s_e"); v_w = S("v_w"); t_wt = S("t_wt"); v_wt = S("v_wt")
    st_wt = S("st_wt")
    sp_x = [S("sp_x0"), S("sp_x1")]
    st_xnt = S("st_xnt"); ppf = S("ppf"); z_a = S("z_a")
    t_ot = S("t_ot"); v_ote = S("v_ote"); v_cmp = S("v_cmp")

    ctx1 = ExitStack()
    sb1 = lambda name, shape, dt: ctx1.enter_context(nc.sbuf_tensor(name, shape, dt))
    xt = [sb1(f"xt{i}", [128, C], F32) for i in range(2)]
    if SPARSE:
        xnT = sb1("xnT", [128, NJ, T], BF16)
    sqscr = sb1("sqscr", [128, C], F32)
    xnf = [sb1(f"xnf{i}", [128, C], F32) for i in range(2)]
    xnb = [sb1(f"xnb{i}", [128, C], BF16) for i in range(2)]
    sx = [sb1(f"sx{i}", [128, 1], F32) for i in range(2)]
    sxx = [sb1(f"sxx{i}", [128, 1], F32) for i in range(2)]
    msc = [sb1(f"msc{i}", [128, 1], F32) for i in range(2)]
    msq = sb1("msq", [128, 1], F32)
    veps = [sb1(f"veps{i}", [128, 1], F32) for i in range(2)]
    rstdc = [sb1(f"rstdc{i}", [128, 1], F32) for i in range(2)]
    nbc = [sb1(f"nbc{i}", [128, 1], F32) for i in range(2)]
    Wg_sb = sb1("Wg_sb", [128, NJ, NE], F32)
    Wg_bf = sb1("Wg_bf", [128, NJ, NE], BF16)
    bg_sb = sb1("bg_sb", [1, NE], F32)
    bg_bf = sb1("bg_bf", [1, NE], BF16)
    ones1 = sb1("ones1", [1, 128], BF16)
    lg = [sb1(f"lg{i}", [128, NE], F32) for i in range(2)]
    pe_ = [sb1(f"pe{i}", [128, NE], F32) for i in range(2)]
    lmn = [sb1(f"lmn{i}", [128, 1], F32) for i in range(2)]
    sm = sb1("sm", [128, NE], F32)
    sm2 = sb1("sm2", [128, NE], F32)
    sme = sb1("sme", [128, NE], F32)
    smw = sb1("smw", [128, NE], F32)
    col_a = sb1("col_a", [128, 1], F32)
    col_b = sb1("col_b", [128, 1], F32)
    wT_sb = sb1("wT_sb", [NE, T], BF16)
    if SPARSE:
        mskT = sb1("mskT", [NE, T], F32)
        scanT = sb1("scanT", [NE, T], F32)
        tmp8 = sb1("tmp8", [128, NE], F32)
        pft = sb1("pft", [128, UPE, 2], F32)
        zt16 = sb1("zt16", [128, C], F16)
        zbf = sb1("zbf", [128, C], BF16)

    ctxp1 = ExitStack()
    pt1 = lambda name, shape, dt: ctxp1.enter_context(nc.psum_tensor(name, shape, dt))
    ps_tr = [pt1(f"ps_tr{i}", [128, 128], BF16) for i in range(2)]
    ps_tro = [pt1(f"ps_tro{i}", [128, NE], F32) for i in range(2)]
    ps_gate = [pt1(f"ps_gate{i}", [128, NE], F32) for i in range(2)]
    ps_wt = [pt1(f"ps_wt{i}", [NE, 128], F32) for i in range(2)]

    with nc.Block() as b1blk:
        @b1blk.tensor
        def _(e):
            e.wait_ge(pre, PRE_TOT)
            e.wait_ge(pre_v, 1)
            e.wait_ge(pre_w, 48)
            for ti in range(NTI):
                e.wait_ge(s_xn, ti + 1)
                for c in range(NJ):
                    g = NJ * ti + c
                    W(e, v_tre, g - 1)
                    e.transpose(ps_tr[g % 2][:], xnb[ti % 2][:, c * 128:(c + 1) * 128],
                                identb[:]).then_inc(t_tr, 1)
                W(e, v_g, ti - 1)
                e.wait_ge(v_tre, NJ * ti + NJ)
                for c in range(NJ):
                    e.matmul(ps_gate[ti % 2][:], xnT[:, c, ti * 128:(ti + 1) * 128],
                             Wg_bf[:, c, :], start=(c == 0), stop=False)
                e.matmul(ps_gate[ti % 2][:], ones1[:], bg_bf[:],
                         start=False, stop=True).then_inc(t_gate, 1)
            for ti in range(NTI):
                e.wait_ge(v_w, ti + 1)
                W(e, v_wt, ti - 1)
                e.transpose(ps_wt[ti % 2][:], w_nat[:, ti, :], ident32[:]).then_inc(t_wt, 1)
            e.wait_ge(v_wt, NTI)
            if SPARSE:
                e.wait_ge(v_cmp, 2)
                e.wait_ge(v_tre, NJ * NTI)
                for ti in range(NTI):
                    W(e, v_ote, ti - 1)
                    e.transpose(ps_tro[ti % 2][:], scanT[:, ti * 128:(ti + 1) * 128],
                                ident32[0:NE, 0:NE]).then_inc(t_ot, 1)
                e.wait_ge(v_cmp, 3)
            e.nop().then_inc(ph, 1)

        @b1blk.scalar
        def _(e):
            for ti in range(NTI):
                e.wait_ge(ld_x2[ti % 2], 16 * (ti // 2 + 1))
                e.activation(sqscr[:], xt[ti % 2][:], AF.Square,
                             accum_out=sxx[ti % 2][:]).then_inc(s_sq, 1)
                e.wait_ge(v_va, ti + 1)
                SC(e, e.activation(veps[ti % 2][:], veps[ti % 2][:], AF.Ln))
                e.activation(rstdc[ti % 2][:], veps[ti % 2][:], AF.Exp,
                             scale=-0.5).then_inc(s_rstd, 1)
                e.wait_ge(v_sm1, ti + 1)
                e.activation(pe_[ti % 2][:], lg[ti % 2][:], AF.Exp,
                             bias=lmn[ti % 2][:]).then_inc(s_e, 1)
            e.wait_ge(v_w, NTI)
            e.nop().then_inc(ph, 1)

        @b1blk.vector
        def _(e):
            e.wait_ge(pre, PRE_TOT)
            e.memset(ones1[:], 1.0)
            e.wait_ge(pre, PRE_TOT)
            e.tensor_copy(Wg_bf[:], Wg_sb[:])
            VC(e, e.tensor_copy(bg_bf[:], bg_sb[:]))
            e.memset(S1acc[:], 0.0)
            e.memset(S2acc[:], 0.0).then_inc(pre_v, 1)
            if SPARSE:
                e.memset(pft[:, :, 0:1], float(PADPOS))
                e.memset(pft[:, :, 1:2], 0.0)
                e.memset(zt16[:], 0.0)
                e.memset(zbf[:], 0.0).then_inc(v_cmp, 1)
            for ti in range(NTI):
                e.wait_ge(ld_x2[ti % 2], 16 * (ti // 2 + 1))
                VC(e, e.reduce_sum(sx[ti % 2][:], xt[ti % 2][:], AX.X))
                VC(e, e.tensor_scalar_mul(msc[ti % 2][:], sx[ti % 2][:], 1.0 / C))
                e.wait_ge(s_sq, ti + 1)
                VC(e, e.tensor_mul(msq[:], msc[ti % 2][:], msc[ti % 2][:]))
                VC(e, e.tensor_scalar_mul(veps[ti % 2][:], sxx[ti % 2][:], 1.0 / C))
                VC(e, e.tensor_sub(veps[ti % 2][:], veps[ti % 2][:], msq[:]))
                VC(e, e.tensor_scalar_add(veps[ti % 2][:], veps[ti % 2][:], 1e-5))
                e.nop().then_inc(v_va, 1)
                e.wait_ge(s_rstd, ti + 1)
                VC(e, e.scalar_tensor_tensor(nbc[ti % 2][:], msc[ti % 2][:], -1.0,
                                             rstdc[ti % 2][:], ALU.mult, ALU.mult))
                if SPARSE:
                    W(e, sp_x[ti % 2], 16 * (ti // 2))
                W(e, t_tr, NJ * (ti - 1))
                VC(e, e.tensor_scalar(xnf[ti % 2][:], xt[ti % 2][:], rstdc[ti % 2][:],
                                      nbc[ti % 2][:], ALU.mult, ALU.add))
                e.tensor_copy(xnb[ti % 2][:], xnf[ti % 2][:]).then_inc(s_xn, 1)
                for c in range(NJ):
                    g = NJ * ti + c
                    e.wait_ge(t_tr, g + 1)
                    e.tensor_copy(xnT[:, c, ti * 128:(ti + 1) * 128],
                                  ps_tr[g % 2][:]).then_inc(v_tre, 1)
                e.wait_ge(t_gate, ti + 1)
                VC(e, e.tensor_copy(lg[ti % 2][:], ps_gate[ti % 2][:]))
                e.nop().then_inc(v_g, 1)
                VC(e, e.reduce_max(col_a[:], lg[ti % 2][:], AX.X))
                e.tensor_scalar_mul(lmn[ti % 2][:], col_a[:], -1.0).then_inc(v_sm1, 1)
                e.wait_ge(s_e, ti + 1)
                VC(e, e.reduce_sum(col_a[:], pe_[ti % 2][:], AX.X))
                VC(e, e.reciprocal(col_b[:], col_a[:]))
                VC(e, e.tensor_scalar_mul(sm[:], pe_[ti % 2][:], col_b[:]))
                VC(e, e.reduce_max(col_a[:], sm[:], AX.X))
                VC(e, e.tensor_scalar(sme[:], sm[:], col_a[:], None, ALU.is_equal))
                VC(e, e.scalar_tensor_tensor(sm2[:], sme[:], -1e9, sm[:], ALU.mult, ALU.add))
                VC(e, e.reduce_max(col_a[:], sm2[:], AX.X))
                VC(e, e.tensor_scalar(sme[:], sm[:], col_a[:], None, ALU.is_ge))
                VC(e, e.tensor_mul(smw[:], sm[:], sme[:]))
                VC(e, e.reduce_sum(col_a[:], smw[:], AX.X))
                VC(e, e.reciprocal(col_b[:], col_a[:]))
                e.tensor_scalar_mul(w_nat[:, ti, :], smw[:], col_b[:]).then_inc(v_w, 1)
            for ti in range(NTI):
                e.wait_ge(t_wt, ti + 1)
                iw = e.tensor_copy(wT_sb[:, ti * 128:(ti + 1) * 128],
                                   ps_wt[ti % 2][:])
                if ti == NTI - 1:
                    VC(e, iw)
                    e.nop().then_inc(v_wt, 1)
                else:
                    iw.then_inc(v_wt, 1)
            if SPARSE:
                VC(e, e.tensor_single_scalar(mskT[:], wT_sb[:], 0.0, ALU.is_gt))
                VC(e, e.tensor_tensor_scan(scanT[:], mskT[:], mskT[:], 0.0,
                                           ALU.add, ALU.bypass))
                VC(e, e.tensor_scalar_add(scanT[:], scanT[:], -1.0))
                VC(e, e.tensor_mul(scanT[:], scanT[:], mskT[:]))
                VC(e, e.tensor_scalar(mskT[:], mskT[:], -float(PADPOS), float(PADPOS),
                                      ALU.mult, ALU.add))
                VC(e, e.tensor_add(scanT[:], scanT[:], mskT[:]))
                e.nop().then_inc(v_cmp, 1)
                for ti in range(NTI):
                    e.wait_ge(t_ot, ti + 1)
                    VC(e, e.tensor_copy(offn[:, ti, :], ps_tro[ti % 2][:]))
                    if ti > 0:
                        VC(e, e.tensor_single_scalar(tmp8[:], offn[:, ti, :],
                                                     float(PADPOS) - 0.5, ALU.is_ge))
                        VC(e, e.scalar_tensor_tensor(offn[:, ti, :], tmp8[:], float(ti),
                                                     offn[:, ti, :], ALU.mult, ALU.add))
                    e.nop().then_inc(v_ote, 1)
                VC(e, e.tensor_copy(offi[:], offn[:]))
                e.nop().then_inc(v_cmp, 1)

            e.wait_ge(st_wt, 16)
            e.nop().then_inc(ph, 1)

        @b1blk.gpsimd
        def _(e):
            e.dma_start(out=wq_bf[:],
                        in_=wqkvT_d.rearrange("(a p) j -> p a j", p=128)).then_inc(pre_w, 16)
            e.dma_start(out=p2tb[:], in_=p2t_d).then_inc(pre_w, 16)
            e.dma_start(out=identb[:], in_=ident_d).then_inc(pre_w, 16)
            if SPARSE:
                for ti in range(NTI):
                    e.wait_ge(s_xn, ti + 1)
                    e.dma_start(out=xn_hbm[ti * 128:(ti + 1) * 128, :],
                                in_=xnb[ti % 2][:]).then_inc(sp_x[ti % 2], 16)
                e.wait_ge(sp_x[0], 16 * (NTI // 2))
                e.wait_ge(sp_x[1], 16 * (NTI // 2))
            e.nop().then_inc(ph, 1)

        @b1blk.sync
        def _(e):
            e.dma_start(out=Wg_sb[:],
                        in_=Wg_d.rearrange("(a p) e -> p a e", p=128)).then_inc(pre, 16)
            e.dma_start(out=bg_sb[:], in_=bg_d).then_inc(pre, 16)
            e.dma_start(out=b1_sb[:], in_=b1_d.rearrange("e p a -> p e a")).then_inc(pre, 16)
            e.dma_start(out=ident32[:], in_=ident_d).then_inc(pre, 16)
            if SPARSE:
                e.dma_start(out=idcol_sb[:], in_=idcol_d).then_inc(pre, 16)
                e.wait_ge(v_cmp, 1)
                for ee in range(NE):
                    e.dma_start(out=pair_hbm[ee].rearrange("(a p) c -> p a c", p=128)[:, 0:UPE, :],
                                in_=pft[:]).then_inc(ppf, 16)
                for zi in range((T + 128) // 128):
                    e.dma_start(out=acc_hbm[zi * 128:(zi + 1) * 128, :],
                                in_=zt16[:]).then_inc(z_a, 16)
                e.dma_start(out=xn_hbm[T:T + 128, :], in_=zbf[:]).then_inc(z_a, 16)
            for ti in range(NTI):
                W(e, s_xn, ti - 1)
                e.dma_start(out=xt[ti % 2][:],
                            in_=x_d[ti * 128:(ti + 1) * 128, :]).then_inc(ld_x2[ti % 2], 16)
            if SPARSE:
                e.wait_ge(v_tre, NJ * NTI)
                e.dma_start(out=xnt_hbm, in_=xnT[:]).then_inc(st_xnt, 16)
                e.wait_ge(ppf, 16 * NE)
                e.wait_ge(z_a, 16 * ((T + 128) // 128 + 1))
                e.wait_ge(st_xnt, 16)
            e.wait_ge(v_wt, NTI)
            e.dma_start(out=wt_hbm, in_=wT_sb[:]).then_inc(st_wt, 16)
            nst = 16
            if "ph1" in DEBUG:
                e.wait_ge(v_tre, NJ * NTI)
                e.dma_start(out=dbg_xnT, in_=xnT[:]).then_inc(st_wt, 16)
                e.wait_ge(v_w, NTI)
                e.dma_start(out=dbg_w, in_=w_nat[:]).then_inc(st_wt, 16)
                nst = 48
            e.wait_ge(st_wt, nst)
            e.nop().then_inc(ph, 1)

    ctxp1.close()
    ctx1.close()
    PH[0] = 5

    if MAX_PHASE >= 2:
        tab2 = [S("tab0"), S("tab1")]
        t_qkv = S("t_qkv"); s_raw = S("s_raw"); t_rot = S("t_rot")
        v_qkc = S("v_qkc"); t_at = S("t_at"); v_mask = S("v_mask")
        t_y = S("t_y"); s_ye = S("s_ye"); t_ktr = S("t_ktr"); v_kg = S("v_kg")
        t_s = S("t_s"); v_s = S("v_s"); s_st = S("s_st"); v_st = S("v_st")
        sty2 = [S("st_y0"), S("st_y1")]
        ldxt2 = [S("ld_xt0"), S("ld_xt1")]
        v_pb = S("v_pb"); g_scp2 = [S("g_scp0"), S("g_scp1")]

        ctx2 = ExitStack()
        sb2 = lambda name, shape, dt: ctx2.enter_context(nc.sbuf_tensor(name, shape, dt))
        tab_sb = [sb2(f"tab{i}", [128, 4, SLAB], F32) for i in range(2)]
        xnTs = [sb2(f"xnTs{i}", [128, NJ, SLAB], BF16) for i in range(2)]
        qT_sl = [sb2(f"qT{i}", [128, NJ, SLAB], BF16) for i in range(2)]
        kT_sl = [sb2(f"kT{i}", [128, NJ, SLAB], BF16) for i in range(2)]
        v_sl = [sb2(f"v{i}", [128, CPS, C], BF16) for i in range(2)]
        yT_sl = [sb2(f"yT{i}", [128, NJ, SLAB], BF16) for i in range(2)]
        maskr_bf = sb2("maskr_bf", [128, H, CH], BF16)
        gamp_bf = sb2("gamp_bf", [128, NJ, CH], BF16)
        kgam_bf = sb2("kgam_bf", [128, NJ, 128], BF16)
        S_f32 = sb2("S_f32", [128, NJ, 64], F32)
        S_bf = sb2("S_bf", [128, NJ, 128], BF16)
        qraw = [sb2(f"qraw{i}", [128, SLAB], BF16) for i in range(2)]
        rtmp1 = sb2("rtmp1", [128, SLAB], BF16)
        rtmp2 = sb2("rtmp2", [128, SLAB], BF16)
        at_bf = [sb2(f"at_bf{i}", [128, 2, CH], BF16) for i in range(2)]
        qg_sb = [sb2(f"qg{i}", [128, CH], BF16) for i in range(2)]
        kg_sb = [sb2(f"kg{i}", [128, 128], BF16) for i in range(2)]
        S1tmp = sb2("S1tmp", [128, NJ, 1], F32)
        S2col = [sb2(f"S2col{i}", [128, NJ], F32) for i in range(2)]
        sqscr2 = sb2("sqscr2", [128, SLAB], F32)

        ctxp2 = ExitStack()
        pt2 = lambda name, shape, dt: ctxp2.enter_context(nc.psum_tensor(name, shape, dt))
        ps_qkv = pt2("ps_qkv", [128, 512], F32)
        ps_rot = pt2("ps_rot", [128, SLAB], F32)
        ps_at = [pt2("ps_at_e", [128, 128], F32), pt2("ps_at_o", [128, 128], F32)]
        ps_y = [pt2(f"ps_y{i}", [128, 256], F32) for i in range(2)]
        ps_ktr = pt2("ps_ktr", [128, 128], BF16)
        ps_s = pt2("ps_s", [128, 128], F32)

        NG = 16 + 2 * CPS
        TABS_D = [csq_d, ssq_d, csk_d, ssk_d]

        def prod_groups(s):
            gs = []
            for part in range(2):
                for jt in range(NJ):
                    gs.append(("qk", part, jt))
            for ts_ in range(CPS):
                for dmh in range(2):
                    gs.append(("v", ts_, dmh))
            return gs

        with nc.Block() as b2blk:
            def gi_sp(j):
                sj = j // IPS
                ck, p = (j - sj * IPS) // 8, (j - sj * IPS) % 8
                return sj, ck, p

            @b2blk.tensor
            def _(e):
                e.wait_ge(ph, PH[0])
                e.wait_ge(pre_w, 96)

                def emit_prod(s):
                    for gl, (kind, a1, a2) in enumerate(prod_groups(s)):
                        m = s * NG + gl
                        W(e, s_raw, m)
                        if gl == 0:
                            e.wait_ge(ldxt2[s % 2], 16 * (s // 2 + 1))
                        if kind == "qk":
                            part, jt = a1, a2
                            joff = part * C + jt * 128
                            for c in range(NJ):
                                im = e.matmul(ps_qkv[:, 0:SLAB], wq_bf[:, c, joff:joff + 128],
                                              xnTs[s % 2][:, c, :], start=(c == 0),
                                              stop=(c == NJ - 1))
                            im.then_inc(t_qkv, 1)
                            r = s * 16 + part * 8 + jt
                            e.wait_ge(s_raw, m + 1)
                            W(e, v_qkc, r)
                            e.matmul(ps_rot[:, 0:SLAB], p2tb[:], qraw[m % 2][:],
                                     start=True, stop=True).then_inc(t_rot, 1)
                        else:
                            ts_, dmh = a1, a2
                            for c in range(NJ):
                                im = e.matmul(ps_qkv[:, 0:512],
                                              xnTs[s % 2][:, c, ts_ * 128:(ts_ + 1) * 128],
                                              wq_bf[:, c, 2 * C + dmh * 512:2 * C + (dmh + 1) * 512],
                                              start=(c == 0), stop=(c == NJ - 1))
                            im.then_inc(t_qkv, 1)

                def emit_at(j):
                    sj, ck, p = gi_sp(j)
                    ccols = slice(ck * CH, (ck + 1) * CH)
                    e.wait_ge(v_qkc, sj * 16 + 8 + p + 1)
                    W(e, v_mask, j)
                    e.matmul(ps_at[0][:], kT_sl[sj % 2][0:64, p, ccols],
                             qT_sl[sj % 2][0:64, p, ccols], start=True, stop=True)
                    e.matmul(ps_at[1][:], kT_sl[sj % 2][64:128, p, ccols],
                             qT_sl[sj % 2][64:128, p, ccols], start=True,
                             stop=True).then_inc(t_at, 1)

                def emit_supd(j):
                    if j < 0:
                        return
                    sj, ck, p = gi_sp(j)
                    e.wait_ge(v_kg, j + 1)
                    W(e, v_s, j - 1)
                    e.matmul(ps_s[:, 0:64], kg_sb[j % 2][:],
                             v_sl[sj % 2][:, ck, p * 128:p * 128 + 64],
                             start=True, stop=True)
                    e.matmul(ps_s[:, 64:128], kg_sb[j % 2][:],
                             v_sl[sj % 2][:, ck, p * 128 + 64:p * 128 + 128],
                             start=True, stop=True).then_inc(t_s, 1)

                def emit_ret(s):
                    i0 = s * IPS
                    e.wait_ge(s_raw, (s + 1) * NG)
                    emit_at(i0)
                    emit_at(i0 + 1)
                    for i in range(i0, i0 + IPS):
                        sj, ck, p = gi_sp(i)
                        e.wait_ge(v_mask, i + 1)
                        W(e, s_ye, i - 1)
                        W(e, v_s, i - 7)
                        vpair = v_sl[s % 2][:, ck, p * 128:(p + 1) * 128]
                        e.matmul(ps_y[i % 2][:, 0:128], vpair, at_bf[i % 2][:, 0, :],
                                 start=True, stop=False)
                        e.matmul(ps_y[i % 2][:, 0:128], S_bf[:, p, :], qg_sb[i % 2][:],
                                 start=False, stop=True)
                        e.matmul(ps_y[i % 2][:, 128:256], vpair, at_bf[i % 2][:, 1, :],
                                 start=True, stop=False)
                        e.matmul(ps_y[i % 2][:, 128:256], S_bf[:, p, :], qg_sb[i % 2][:],
                                 start=False, stop=True).then_inc(t_y, 1)
                        W(e, v_kg, i)
                        e.transpose(ps_ktr[:], kT_sl[s % 2][:, p, ck * CH:(ck + 1) * CH],
                                    identb[:]).then_inc(t_ktr, 1)
                        if i > i0:
                            emit_supd(i - 1)
                        if i + 2 < i0 + IPS:
                            emit_at(i + 2)
                    emit_supd(i0 + IPS - 1)

                emit_prod(0)
                for s in range(1, NSLAB + 1):
                    if s < NSLAB:
                        emit_prod(s)
                    emit_ret(s - 1)
                e.wait_ge(v_s, NI)
                e.nop().then_inc(ph, 1)

            @b2blk.vector
            def _(e):
                e.wait_ge(ph, PH[0])
                e.wait_ge(pre_w, 96)
                e.memset(S_f32[:], 0.0)
                e.memset(S_bf[:], 0.0)

                def emit_prodv(s):
                    for gl, (kind, a1, a2) in enumerate(prod_groups(s)):
                        m = s * NG + gl
                        if kind == "qk":
                            part, jt = a1, a2
                            r = s * 16 + part * 8 + jt
                            e.wait_ge(s_raw, m + 1)
                            e.wait_ge(t_rot, r + 1)
                            e.wait_ge(tab2[s % 2], (s // 2 + 1) * 64)
                            if s >= 2 and gl == 0:
                                e.wait_ge(t_at, (s - 1) * IPS)
                                e.wait_ge(t_ktr, (s - 1) * IPS)
                                e.wait_ge(t_y, (s - 1) * IPS)
                                e.wait_ge(t_s, (s - 1) * IPS)
                            ci = 0 if part == 0 else 2
                            e.tensor_mul(rtmp1[:], qraw[m % 2][:], tab_sb[s % 2][:, ci, :])
                            VC(e, e.tensor_mul(rtmp2[:], ps_rot[:], tab_sb[s % 2][:, ci + 1, :]))
                            dst = (qT_sl if part == 0 else kT_sl)[s % 2][:, jt, :]
                            e.tensor_add(dst, rtmp1[:], rtmp2[:]).then_inc(v_qkc, 1)

                def emit_maskqg(j):
                    sj, ck, p = gi_sp(j)
                    ccols = slice(ck * CH, (ck + 1) * CH)
                    e.wait_ge(t_at, j + 1)
                    e.tensor_mul(at_bf[j % 2][:, 0, :], ps_at[0][:],
                                 maskr_bf[:, 2 * p, :])
                    e.tensor_mul(at_bf[j % 2][:, 1, :], ps_at[1][:],
                                 maskr_bf[:, 2 * p + 1, :])
                    e.tensor_mul(qg_sb[j % 2][:], qT_sl[sj % 2][:, p, ccols],
                                 gamp_bf[:, p, :]).then_inc(v_mask, 1)

                def emit_tail(j):
                    if j < 0:
                        return
                    sj, ck, p = gi_sp(j)
                    e.wait_ge(t_ktr, j + 1)
                    e.tensor_mul(kg_sb[j % 2][:], ps_ktr[:],
                                 kgam_bf[:, p, :]).then_inc(v_kg, 1)
                    e.wait_ge(t_s, j + 1)
                    ii_ = e.scalar_tensor_tensor(S_f32[0:64, p, :], S_f32[0:64, p, :],
                                                 gam_c[2 * p], ps_s[0:64, 0:64],
                                                 ALU.mult, ALU.add)
                    ii_ = e.scalar_tensor_tensor(S_f32[64:128, p, :], S_f32[64:128, p, :],
                                                 gam_c[2 * p + 1], ps_s[64:128, 64:128],
                                                 ALU.mult, ALU.add)
                    VC(e, ii_)
                    e.tensor_copy(S_bf[0:64, p, 0:64], S_f32[0:64, p, :])
                    e.tensor_copy(S_bf[64:128, p, 64:128],
                                  S_f32[64:128, p, :]).then_inc(v_s, 1)

                def emit_retv(s):
                    i0 = s * IPS
                    for i in range(i0, i0 + IPS):
                        emit_maskqg(i)
                        if i > i0:
                            emit_tail(i - 1)
                    emit_tail(i0 + IPS - 1)
                    e.wait_ge(s_ye, IPS * (s + 1))
                    VC(e, e.tensor_reduce(S1tmp[:], yT_sl[s % 2][:], AX.X, ALU.add))
                    e.tensor_add(S1acc[:], S1acc[:], S1tmp[:, :, 0])
                    e.wait_ge(s_st, s + 1)
                    i2 = e.tensor_add(S2acc[:], S2acc[:], S2col[s % 2][:])
                    if s == NSLAB - 1:
                        VC(e, i2)
                        e.nop().then_inc(v_st, 1)
                    else:
                        i2.then_inc(v_st, 1)

                emit_prodv(0)
                if SPARSE:
                    e.wait_ge(pre, PRE_TOT)
                    VC(e, e.memset(ones8[:], 1.0))
                    for ti in range(NTI):
                        e.tensor_scalar_mul(pairs_all[:, ti, :, 0], ones8[:],
                                            idcol_sb[:, ti:ti + 1])
                    iv = e.tensor_copy(pairs_all[:, :, :, 1], w_nat[:, :, :])
                    VC(e, iv)
                    e.nop().then_inc(v_pb, 1)
                for s in range(1, NSLAB + 1):
                    if s < NSLAB:
                        emit_prodv(s)
                    emit_retv(s - 1)
                e.nop().then_inc(ph, 1)

            @b2blk.scalar
            def _(e):
                e.wait_ge(ph, PH[0])

                def emit_prods(s):
                    for gl, (kind, a1, a2) in enumerate(prod_groups(s)):
                        m = s * NG + gl
                        e.wait_ge(t_qkv, m + 1)
                        if kind == "qk":
                            e.activation(qraw[m % 2][:], ps_qkv[:, 0:SLAB],
                                         AF.Copy).then_inc(s_raw, 1)
                        else:
                            ts_, dmh = a1, a2
                            e.activation(v_sl[s % 2][:, ts_, dmh * 512:(dmh + 1) * 512],
                                         ps_qkv[:, 0:512], AF.Copy).then_inc(s_raw, 1)

                def emit_rets(s):
                    i0 = s * IPS
                    for i in range(i0, i0 + IPS):
                        sj, ck, p = gi_sp(i)
                        ccols = slice(ck * CH, (ck + 1) * CH)
                        e.wait_ge(t_y, i + 1)
                        if s >= 2 and i == i0:
                            e.wait_ge(sty2[s % 2], 16 * (s // 2))
                            e.wait_ge(v_st, s - 1)
                        e.activation(yT_sl[s % 2][0:64, p, ccols],
                                     ps_y[i % 2][0:64, 0:128], AF.Copy)
                        iy = e.activation(yT_sl[s % 2][64:128, p, ccols],
                                          ps_y[i % 2][64:128, 128:256], AF.Copy)
                        if i == i0 + 15:
                            SC(e, iy)
                            e.nop().then_inc(s_ye, 1)
                        else:
                            iy.then_inc(s_ye, 1)
                    W(e, v_st, s - 1)
                    for j in range(NJ):
                        ii_ = e.activation(sqscr2[:], yT_sl[s % 2][:, j, :], AF.Square,
                                           accum_out=S2col[s % 2][:, j:j + 1])
                        if j < NJ - 1:
                            SC(e, ii_)
                    ii_.then_inc(s_st, 1)

                emit_prods(0)
                for s in range(1, NSLAB + 1):
                    if s < NSLAB:
                        emit_prods(s)
                    emit_rets(s - 1)
                e.wait_ge(v_st, NSLAB)
                e.nop().then_inc(ph, 1)

            @b2blk.gpsimd
            def _(e):
                e.wait_ge(ph, PH[0])
                e.dma_start(out=maskr_bf[:], in_=maskr_d).then_inc(pre_w, 16)
                e.dma_start(out=gamp_bf[:], in_=gampair_d).then_inc(pre_w, 16)
                e.dma_start(out=kgam_bf[:], in_=kgam_d).then_inc(pre_w, 16)
                if SPARSE:
                    e.wait_ge(ppf, 16 * NE)
                    e.wait_ge(v_pb, 1)
                    for j in range(NTI * NE):
                        ti, ee = j // NE, j % NE
                        e.indirect_dma_start(
                            out=pair_hbm.rearrange("e r c -> (e r) c"),
                            out_offset=IndirectOffsetOnAxis(offi[:, ti, ee:ee + 1], 0),
                            in_=pairs_all[:, ti, ee, :], in_offset=None,
                            element_offset=ee * (T + 128) * 2,
                        ).then_inc(g_scp2[j % 2], 16)
                    e.wait_ge(g_scp2[0], 16 * NTI * NE // 2)
                    e.wait_ge(g_scp2[1], 16 * NTI * NE // 2)
                e.wait_ge(pre_w, 96)
                e.nop().then_inc(ph, 1)

            @b2blk.sync
            def _(e):
                e.wait_ge(ph, PH[0])
                for s in range(2):
                    e.dma_start(out=xnTs[s][:],
                                in_=xnt_hbm[:, :, s * SLAB:(s + 1) * SLAB]).then_inc(ldxt2[s % 2], 16)
                    for k in range(4):
                        e.dma_start(out=tab_sb[s][:, k, :],
                                    in_=TABS_D[k][:, s * SLAB:(s + 1) * SLAB]).then_inc(tab2[s % 2], 16)
                for s in range(NSLAB):
                    e.wait_ge(s_ye, IPS * (s + 1))
                    e.dma_start(out=yT_hbm[:, :, s * SLAB:(s + 1) * SLAB],
                                in_=yT_sl[s % 2][:]).then_inc(sty2[s % 2], 16)
                    if s + 2 < NSLAB:
                        e.wait_ge(t_qkv, (s + 1) * NG)
                        e.dma_start(out=xnTs[s % 2][:],
                                    in_=xnt_hbm[:, :, (s + 2) * SLAB:(s + 3) * SLAB]).then_inc(ldxt2[s % 2], 16)
                        e.wait_ge(v_qkc, (s + 1) * 16)
                        for k in range(4):
                            e.dma_start(out=tab_sb[s % 2][:, k, :],
                                        in_=TABS_D[k][:, (s + 2) * SLAB:(s + 3) * SLAB]).then_inc(tab2[s % 2], 16)
                e.wait_ge(sty2[0], 16 * (NSLAB // 2))
                e.wait_ge(sty2[1], 16 * (NSLAB // 2))
                if "ph2" in DEBUG:
                    e.wait_ge(v_st, NSLAB)
                    e.dma_start(out=dbg_st[:, 0:NJ], in_=S1acc[:]).then_inc(sty2[0], 16)
                    e.dma_start(out=dbg_st[:, NJ:2 * NJ], in_=S2acc[:]).then_inc(sty2[1], 16)
                    e.wait_ge(sty2[0], 16 * (NSLAB // 2) + 16)
                    e.wait_ge(sty2[1], 16 * (NSLAB // 2) + 16)
                e.nop().then_inc(ph, 1)

        ctxp2.close()
        ctx2.close()
        PH[0] += 5
    ctx12.close()

    if MAX_PHASE >= 3 and SPARSE:
        w1g4 = [S(f"w1g{i}") for i in range(4)]
        w2g4 = [S(f"w2g{i}") for i in range(4)]
        t_h = S("t_h"); s_g = S("s_g")
        t_y3 = S("t_y3"); v_e3 = S("v_e3")
        ldpr2 = [S("ld_pr0"), S("ld_pr1")]
        v_ix = S("v_ix"); g_g2 = [S("g_g0"), S("g_g1")]
        t_tp = S("t_tp"); s_tpe = S("s_tpe")
        g_sc = S("g_sc")

        ctx3 = ExitStack()
        sb3 = lambda name, shape, dt: ctx3.enter_context(nc.sbuf_tensor(name, shape, dt))
        W1sb = sb3("W1sb", [128, NJ, DF], BF16)
        W2sb = sb3("W2sb", [128, NFT, C], BF16)
        hTg = [sb3(f"hTg{i}", [128, TB], BF16) for i in range(2)]
        stage = [sb3(f"stage{i}", [128, 2, C], F16) for i in range(2)]
        pairs_sb = [sb3(f"pairs{i}", [128, UPE, 2], F32) for i in range(2)]
        idxi_sb = [sb3(f"idxi{i}", [128, UPE], I32) for i in range(2)]
        xg = [sb3(f"xg{i}", [128, C], BF16) for i in range(2)]
        xgT = [sb3(f"xgT{i}", [128, NJ, TB], BF16) for i in range(2)]

        ctxp3 = ExitStack()
        pt3 = lambda name, shape, dt: ctxp3.enter_context(nc.psum_tensor(name, shape, dt))
        ps_h = [pt3(f"ps_h{i}", [128, TB], F32) for i in range(2)]
        ps_y3 = [pt3(f"ps_y3{i}", [128, C], F32) for i in range(2)]
        ps_tp = [pt3(f"ps_tp{i}", [128, 128], BF16) for i in range(2)]

        with nc.Block() as b3blk:
            @b3blk.tensor
            def _(e):
                e.wait_ge(ph, PH[0])

                def y_mms(ee, k, bg_, ft):
                    idx = bg_ * NFT + ft
                    e.wait_ge(s_g, idx + 1)
                    if ft == 0:
                        W(e, v_e3, bg_)
                        if k == 0:
                            e.wait_ge(w2g4[0], (ee + 1) * 32)
                    elif k == 0 and ft % 4 == 0:
                        e.wait_ge(w2g4[(ft // 4) // 2], (ee + 1) * 32)
                    im = None
                    for a in range(2):
                        for dmh in range(2):
                            im = e.matmul(ps_y3[a][:, dmh * 512:(dmh + 1) * 512],
                                          hTg[ft % 2][:, a * 128:(a + 1) * 128],
                                          W2sb[:, ft, dmh * 512:(dmh + 1) * 512],
                                          start=(ft == 0), stop=(ft == NFT - 1))
                    return im

                for ee in range(NE):
                    for k in range(NB):
                        bg_ = ee * NB + k
                        for uu in range(2):
                            u = 2 * k + uu
                            ug = ee * UPE + u
                            e.wait_ge(g_g2[ug % 2], 16 * (ug // 2 + 1))
                            if bg_ >= 2 and uu == 0:
                                e.wait_ge(s_g, (bg_ - 1) * NFT)
                            for c in range(NJ):
                                tc = ug * NJ + c
                                W(e, s_tpe, tc - 1)
                                e.transpose(ps_tp[tc % 2][:],
                                            xg[u % 2][:, c * 128:(c + 1) * 128],
                                            identb[:]).then_inc(t_tp, 1)
                        for ft in range(NFT):
                            idx = bg_ * NFT + ft
                            if k == 0:
                                e.wait_ge(w1g4[(ft // 4) // 2], (ee + 1) * 32)
                            W(e, s_g, idx - 1)
                            if ft == 0:
                                e.wait_ge(s_tpe, 16 * (bg_ + 1))
                            for c in range(NJ):
                                im = e.matmul(ps_h[ft % 2][:],
                                              W1sb[:, c, ft * 128:(ft + 1) * 128],
                                              xgT[k % 2][:, c, :], start=(c == 0),
                                              stop=(c == NJ - 1))
                            im.then_inc(t_h, 1)
                            if ft >= 1:
                                y_mms(ee, k, bg_, ft - 1)
                        y_mms(ee, k, bg_, NFT - 1).then_inc(t_y3, 1)
                e.wait_ge(v_e3, NBG)
                e.nop().then_inc(ph, 1)

            @b3blk.scalar
            def _(e):
                e.wait_ge(ph, PH[0])
                for ee in range(NE):
                    for k in range(NB):
                        bg_ = ee * NB + k
                        for uu in range(2):
                            u = 2 * k + uu
                            ug = ee * UPE + u
                            for c in range(NJ):
                                tc = ug * NJ + c
                                e.wait_ge(t_tp, tc + 1)
                                e.activation(xgT[k % 2][:, c, uu * 128:(uu + 1) * 128],
                                             ps_tp[tc % 2][:],
                                             AF.Copy).then_inc(s_tpe, 1)
                        for ft in range(NFT):
                            idx = bg_ * NFT + ft
                            e.wait_ge(t_h, idx + 1)
                            e.activation(hTg[ft % 2][:], ps_h[ft % 2][:], AF.Gelu,
                                         bias=b1_sb[:, ee, ft:ft + 1]).then_inc(s_g, 1)
                e.wait_ge(t_y3, NBG)
                e.nop().then_inc(ph, 1)

            @b3blk.vector
            def _(e):
                e.wait_ge(ph, PH[0])
                for ee in range(NE):
                    e.wait_ge(ldpr2[ee % 2], 16 * (ee // 2 + 1))
                    if ee >= 2:
                        e.wait_ge(g_g2[0], 16 * (ee - 1) * UPE // 2)
                        e.wait_ge(g_g2[1], 16 * (ee - 1) * UPE // 2)
                        e.wait_ge(g_sc, 32 * (ee - 1) * NB)
                    VC(e, e.tensor_copy(idxi_sb[ee % 2][:], pairs_sb[ee % 2][:, :, 0]))
                    e.nop().then_inc(v_ix, 1)
                    for k in range(NB):
                        bg_ = ee * NB + k
                        e.wait_ge(t_y3, bg_ + 1)
                        W(e, g_sc, 32 * (bg_ - 1))
                        ii_ = None
                        for a in range(2):
                            for dmh in range(2):
                                ii_ = e.tensor_scalar(
                                    stage[bg_ % 2][:, a, dmh * 512:(dmh + 1) * 512],
                                    ps_y3[a][:, dmh * 512:(dmh + 1) * 512],
                                    pairs_sb[ee % 2][:, 2 * k + a, 1:2], None, ALU.mult)
                        ii_.then_inc(v_e3, 1)
                e.nop().then_inc(ph, 1)

            @b3blk.gpsimd
            def _(e):
                e.wait_ge(ph, PH[0])

                def loadW(ee):
                    for gi in range(8):
                        e.dma_start(
                            out=W1sb[:, :, gi * 512:(gi + 1) * 512],
                            in_=W1_d[ee].rearrange("(a p) f -> p a f", p=128)[:, :, gi * 512:(gi + 1) * 512],
                        ).then_inc(w1g4[gi // 2], 16)
                        e.dma_start(
                            out=W2sb[:, gi * 4:(gi + 1) * 4, :],
                            in_=W2_d[ee].rearrange("(g p) d -> p g d", p=128)[:, gi * 4:(gi + 1) * 4, :],
                        ).then_inc(w2g4[gi // 2], 16)

                loadW(0)
                events = []
                for ee in range(NE):
                    for k in range(NB):
                        events.append(("g", ee, k))
                ev2 = []
                for idx3, (kind, ee, k) in enumerate(events):
                    ev2.append(("gather", ee, k))
                    if k == 0 and ee > 0:
                        ev2.append(("loadW", ee, 0))
                    if idx3 >= 2:
                        pee, pk = events[idx3 - 2][1], events[idx3 - 2][2]
                        ev2.append(("scatter", pee, pk))
                ev2.append(("scatter", NE - 1, NB - 2))
                ev2.append(("scatter", NE - 1, NB - 1))
                for kind, ee, k in ev2:
                    bg_ = ee * NB + k
                    if kind == "gather":
                        e.wait_ge(v_ix, ee + 1)
                        for uu in range(2):
                            u = 2 * k + uu
                            ug = ee * UPE + u
                            if ug >= 2:
                                e.wait_ge(t_tp, NJ * (ug - 1))
                            e.indirect_dma_start(
                                out=xg[u % 2][:], out_offset=None,
                                in_=xn_hbm,
                                in_offset=IndirectOffsetOnAxis(
                                    idxi_sb[ee % 2][:, u:u + 1], 0),
                            ).then_inc(g_g2[ug % 2], 16)
                    elif kind == "scatter":
                        e.wait_ge(v_e3, bg_ + 1)
                        for a in range(2):
                            W(e, g_sc, 16 * (2 * bg_ + a))
                            e.indirect_dma_start(
                                out=acc_hbm,
                                out_offset=IndirectOffsetOnAxis(
                                    idxi_sb[ee % 2][:, 2 * k + a:2 * k + a + 1], 0),
                                in_=stage[bg_ % 2][:, a, :], in_offset=None,
                                compute_op=ALU.add,
                            ).then_inc(g_sc, 16)
                    else:
                        e.wait_ge(t_h, ee * NB * NFT)
                        e.wait_ge(t_y3, ee * NB)
                        loadW(ee)
                e.wait_ge(g_sc, 16 * 2 * NBG)
                e.nop().then_inc(ph, 1)

            @b3blk.sync
            def _(e):
                e.wait_ge(ph, PH[0])
                for ee in range(NE):
                    W(e, v_ix, ee - 1)
                    W(e, v_e3, (ee - 1) * NB)
                    e.dma_start(
                        out=pairs_sb[ee % 2][:],
                        in_=pair_hbm[ee].rearrange("(a p) c -> p a c", p=128)[:, 0:UPE, :],
                    ).then_inc(ldpr2[ee % 2], 16)
                e.nop().then_inc(ph, 1)

        ctxp3.close()
        ctx3.close()
        PH[0] += 5

    if MAX_PHASE >= 3 and not SPARSE:
        w1g8 = [S(f"w1g{i}") for i in range(8)]
        w2g8 = [S(f"w2g{i}") for i in range(8)]
        t_h = S("t_h"); s_g = S("s_g")
        t_y3 = S("t_y3"); v_e3 = S("v_e3"); st_c = S("st_c")

        ctx3 = ExitStack()
        sb3 = lambda name, shape, dt: ctx3.enter_context(nc.sbuf_tensor(name, shape, dt))
        W1sb = sb3("W1sb", [128, NJ, DF], BF16)
        W2sb = sb3("W2sb", [128, NFT, C], BF16)
        hTg = [sb3(f"hTg{i}", [128, TB], BF16) for i in range(2)]
        stage = sb3("stage", [128, 2, C], F16)

        ctxp3 = ExitStack()
        pt3 = lambda name, shape, dt: ctxp3.enter_context(nc.psum_tensor(name, shape, dt))
        ps_h = [pt3(f"ps_h{i}", [128, TB], F32) for i in range(2)]
        ps_y3 = [pt3(f"ps_y3{i}", [128, C], F32) for i in range(2)]

        with nc.Block() as b3blk:
            @b3blk.tensor
            def _(e):
                e.wait_ge(ph, PH[0])
                def y_mms(ee, tb, tbg, ft):
                    idx = tbg * NFT + ft
                    e.wait_ge(s_g, idx + 1)
                    if ft == 0:
                        W(e, v_e3, tbg)
                        if tb == 0:
                            e.wait_ge(w2g8[0], (ee + 1) * 16)
                    elif tb == 0 and ft % 4 == 0:
                        e.wait_ge(w2g8[ft // 4], (ee + 1) * 16)
                    im = None
                    for a in range(2):
                        for dmh in range(2):
                            im = e.matmul(ps_y3[a][:, dmh * 512:(dmh + 1) * 512],
                                          hTg[ft % 2][:, a * 128:(a + 1) * 128],
                                          W2sb[:, ft, dmh * 512:(dmh + 1) * 512],
                                          start=(ft == 0), stop=(ft == NFT - 1))
                    return im

                for ee in range(NE):
                    for tb in range(NTB):
                        tbg = ee * NTB + tb
                        tcols = slice(tb * TB, (tb + 1) * TB)
                        for ft in range(NFT):
                            idx = tbg * NFT + ft
                            if tb == 0:
                                e.wait_ge(w1g8[ft // 4], (ee + 1) * 16)
                            W(e, s_g, idx - 1)
                            for c in range(NJ):
                                im = e.matmul(ps_h[ft % 2][:], W1sb[:, c, ft * 128:(ft + 1) * 128],
                                              xnT[:, c, tcols], start=(c == 0),
                                              stop=(c == NJ - 1))
                            im.then_inc(t_h, 1)
                            if ft >= 1:
                                y_mms(ee, tb, tbg, ft - 1)
                        y_mms(ee, tb, tbg, NFT - 1).then_inc(t_y3, 1)
                e.wait_ge(v_e3, NTBG)
                e.nop().then_inc(ph, 1)

            @b3blk.scalar
            def _(e):
                e.wait_ge(ph, PH[0])
                for ee in range(NE):
                    for tb in range(NTB):
                        for ft in range(NFT):
                            idx = (ee * NTB + tb) * NFT + ft
                            e.wait_ge(t_h, idx + 1)
                            e.activation(hTg[ft % 2][:], ps_h[ft % 2][:], AF.Gelu,
                                         bias=b1_sb[:, ee, ft:ft + 1]).then_inc(s_g, 1)
                e.wait_ge(t_y3, NTBG)
                e.nop().then_inc(ph, 1)

            @b3blk.vector
            def _(e):
                e.wait_ge(ph, PH[0])
                for ee in range(NE):
                    for tb in range(NTB):
                        tbg = ee * NTB + tb
                        e.wait_ge(t_y3, tbg + 1)
                        W(e, st_c, 16 * tbg)
                        ii_ = None
                        for a in range(2):
                            for dmh in range(2):
                                ii_ = e.tensor_scalar(
                                    stage[:, a, dmh * 512:(dmh + 1) * 512],
                                    ps_y3[a][:, dmh * 512:(dmh + 1) * 512],
                                    w_nat[:, tb * 2 + a, ee:ee + 1], None, ALU.mult)
                        ii_.then_inc(v_e3, 1)
                e.wait_ge(st_c, 16 * NTBG)
                e.nop().then_inc(ph, 1)

            @b3blk.gpsimd
            def _(e):
                e.wait_ge(ph, PH[0])

                def loadW(ee):
                    for gi in range(8):
                        e.dma_start(
                            out=W1sb[:, :, gi * 512:(gi + 1) * 512],
                            in_=W1_d[ee].rearrange("(a p) f -> p a f", p=128)[:, :, gi * 512:(gi + 1) * 512],
                        ).then_inc(w1g4[gi // 2], 16)
                        e.dma_start(
                            out=W2sb[:, gi * 4:(gi + 1) * 4, :],
                            in_=W2_d[ee].rearrange("(g p) d -> p g d", p=128)[:, gi * 4:(gi + 1) * 4, :],
                        ).then_inc(w2g4[gi // 2], 16)

                loadW(0)
                for ee in range(NE):
                    for tb in range(NTB):
                        tbg = ee * NTB + tb
                        e.wait_ge(v_e3, tbg + 1)
                        e.dma_start(
                            out=acc_hbm[tb * TB:(tb + 1) * TB, :].rearrange("(a p) d -> p a d", p=128),
                            in_=stage[:],
                            accum_op=(ALU.add if ee > 0 else ALU.bypass),
                        ).then_inc(st_c, 16)
                    if ee + 1 < NE:
                        e.wait_ge(t_h, (ee + 1) * NTB * NFT)
                        e.wait_ge(t_y3, (ee + 1) * NTB)
                        loadW(ee + 1)
                e.wait_ge(st_c, 16 * NTBG)
                e.nop().then_inc(ph, 1)

        ctxp3.close()
        ctx3.close()
        PH[0] += 4

    if MAX_PHASE >= 4:
        g4w = S("g4w"); ld4 = S("ld4"); t_s4 = S("t_s4")
        s_st4 = S("s_st4"); v_m4 = S("v_m4")
        ld_xnt = S("ld_xnt")
        ldy2 = [S("ld_y40"), S("ld_y41")]
        t_g4 = S("t_g4"); s_si = S("s_si"); v_gy = S("v_gy")
        t_pr = S("t_pr")
        ldxa2 = [S("ld_xa0"), S("ld_xa1")]
        v_mrg = S("v_mrg")
        sto2 = [S("st_o0"), S("st_o1")]

        ctx4 = ExitStack()
        sb4 = lambda name, shape, dt: ctx4.enter_context(nc.sbuf_tensor(name, shape, dt))
        if SPARSE:
            xnT = sb4("xnT4", [128, NJ, T], BF16)
        wgT_bf = sb4("wgT_bf", [128, NJ, C], BF16)
        wpT_bf = sb4("wpT_bf", [128, NJ, C], BF16)
        yt4 = [sb4(f"yt4_{i}", [128, NJ, SLAB], BF16) for i in range(2)]
        gy = [sb4(f"gy{i}", [128, NJ, SLAB], BF16) for i in range(2)]
        sil = [sb4(f"sil{i}", [128, SLAB], BF16) for i in range(2)]
        tmp_yn = sb4("tmp_yn", [128, SLAB], BF16)
        xt4 = [sb4(f"xt4_{i}", [128, C], F32) for i in range(2)]
        at4 = [sb4(f"at4_{i}", [128, C], F16) for i in range(2)]
        ot4 = [sb4(f"ot4_{i}", [128, C], F32) for i in range(2)]
        wT4 = sb4("wT4", [NE, T], BF16)
        b2f = sb4("b2f", [NE, C], F32)
        b2b = sb4("b2b", [NE, C], BF16)
        gnw_sb = sb4("gnw_sb", [128, NJ], F32)
        gnb_sb = sb4("gnb_sb", [128, NJ], F32)
        h2_sb = sb4("h2_sb", [128, 2], F32)
        g2_sb = sb4("g2_sb", [2, 128], F32)
        sums4 = sb4("sums4", [2, 2 * NJ], F32)
        mh4 = sb4("mh4", [2, NJ], F32)
        rstd4 = sb4("rstd4", [2, NJ], F32)
        mr4 = sb4("mr4", [2, NJ], F32)
        alpha = sb4("alpha", [128, NJ], F32)
        beta = sb4("beta", [128, NJ], F32)
        btmp = sb4("btmp", [128, NJ], F32)

        ctxp4 = ExitStack()
        pt4 = lambda name, shape, dt: ctxp4.enter_context(nc.psum_tensor(name, shape, dt))
        ps_st4 = pt4("ps_st4", [128, 2 * NJ], F32)
        ps_bc = pt4("ps_bc", [128, 2 * NJ], F32)
        ps_g4 = [pt4(f"ps_g4{i}", [128, SLAB], F32) for i in range(2)]
        ps_o = [pt4(f"ps_o{i}", [128, C], F32) for i in range(2)]

        def emit_proj(e, ti):
            s = ti // TPS
            e.wait_ge(v_gy, NJ * (s + 1))
            W(e, v_mrg, ti - 1)
            tcols = slice((ti % TPS) * 128, (ti % TPS) * 128 + 128)
            for dmh in range(2):
                for c in range(NJ):
                    e.matmul(ps_o[ti % 2][:, dmh * 512:(dmh + 1) * 512],
                             gy[s % 2][:, c, tcols],
                             wpT_bf[:, c, dmh * 512:(dmh + 1) * 512],
                             start=(c == 0), stop=False)
                im = e.matmul(ps_o[ti % 2][:, dmh * 512:(dmh + 1) * 512],
                              wT4[:, ti * 128:(ti + 1) * 128],
                              b2b[:, dmh * 512:(dmh + 1) * 512],
                              start=False, stop=True)
            im.then_inc(t_pr, 1)

        with nc.Block() as b4blk:
            @b4blk.tensor
            def _(e):
                e.wait_ge(ph, PH[0])
                e.wait_ge(ld4, 16 * 6)
                e.matmul(ps_st4[0:2, 0:NJ], h2_sb[:], S1acc[:], start=True, stop=True)
                e.matmul(ps_st4[0:2, NJ:2 * NJ], h2_sb[:], S2acc[:], start=True,
                         stop=True).then_inc(t_s4, 1)
                e.wait_ge(v_m4, 2)
                e.matmul(ps_bc[:, 0:NJ], g2_sb[:], rstd4[:], start=True, stop=True)
                e.matmul(ps_bc[:, NJ:2 * NJ], g2_sb[:], mr4[:], start=True,
                         stop=True).then_inc(t_s4, 1)
                e.wait_ge(g4w, 32)
                if SPARSE:
                    e.wait_ge(ld_xnt, 16)
                for s in range(NSLAB):
                    cols = slice(s * SLAB, (s + 1) * SLAB)
                    for j in range(NJ):
                        gj = s * NJ + j
                        W(e, s_si, gj - 1)
                        for c in range(NJ):
                            im = e.matmul(ps_g4[gj % 2][:], wgT_bf[:, c, j * 128:(j + 1) * 128],
                                          xnT[:, c, cols], start=(c == 0), stop=(c == NJ - 1))
                        im.then_inc(t_g4, 1)
                    if s >= 1:
                        for ts_ in range(TPS):
                            ti = (s - 1) * TPS + ts_
                            emit_proj(e, ti)
                for ts_ in range(TPS):
                    emit_proj(e, (NSLAB - 1) * TPS + ts_)
                e.nop().then_inc(ph, 1)

            @b4blk.scalar
            def _(e):
                e.wait_ge(ph, PH[0])
                e.wait_ge(v_m4, 1)
                SC(e, e.activation(rstd4[:], sums4[0:2, NJ:2 * NJ], AF.Ln))
                e.activation(rstd4[:], rstd4[:], AF.Exp, scale=-0.5).then_inc(s_st4, 1)
                for s in range(NSLAB):
                    for j in range(NJ):
                        gj = s * NJ + j
                        e.wait_ge(t_g4, gj + 1)
                        W(e, v_gy, gj - 1)
                        e.activation(sil[gj % 2][:], ps_g4[gj % 2][:],
                                     AF.Silu).then_inc(s_si, 1)
                e.nop().then_inc(ph, 1)

            @b4blk.vector
            def _(e):
                e.wait_ge(ph, PH[0])
                e.wait_ge(ld4, 16 * 6)
                e.tensor_copy(b2b[:], b2f[:])
                e.wait_ge(t_s4, 1)
                VC(e, e.tensor_copy(sums4[:], ps_st4[0:2, :]))
                VC(e, e.tensor_scalar_mul(mh4[:], sums4[0:2, 0:NJ], 1.0 / (64.0 * T)))
                VC(e, e.tensor_mul(rstd4[:], mh4[:], mh4[:]))
                VC(e, e.tensor_scalar_mul(sums4[0:2, NJ:2 * NJ], sums4[0:2, NJ:2 * NJ],
                                          1.0 / (64.0 * T)))
                VC(e, e.tensor_sub(sums4[0:2, NJ:2 * NJ], sums4[0:2, NJ:2 * NJ], rstd4[:]))
                VC(e, e.tensor_scalar_add(sums4[0:2, NJ:2 * NJ], sums4[0:2, NJ:2 * NJ],
                                          1e-5))
                e.nop().then_inc(v_m4, 1)
                e.wait_ge(s_st4, 1)
                VC(e, e.tensor_mul(mr4[:], mh4[:], rstd4[:]))
                e.nop().then_inc(v_m4, 1)
                e.wait_ge(t_s4, 2)
                VC(e, e.tensor_mul(alpha[:], ps_bc[:, 0:NJ], gnw_sb[:]))
                VC(e, e.tensor_mul(btmp[:], ps_bc[:, NJ:2 * NJ], gnw_sb[:]))
                VC(e, e.tensor_sub(beta[:], gnb_sb[:], btmp[:]))
                e.nop().then_inc(v_m4, 1)
                def emit_gy(s):
                    for j in range(NJ):
                        gj = s * NJ + j
                        e.wait_ge(ldy2[s % 2], 16 * (s // 2 + 1))
                        e.wait_ge(s_si, gj + 1)
                        W(e, v_gy, gj)
                        VC(e, e.tensor_scalar(tmp_yn[:], yt4[s % 2][:, j, :],
                                              alpha[:, j:j + 1], beta[:, j:j + 1],
                                              ALU.mult, ALU.add))
                        e.tensor_mul(gy[s % 2][:, j, :], sil[gj % 2][:],
                                     tmp_yn[:]).then_inc(v_gy, 1)

                def emit_merge(sm):
                    for ts_ in range(TPS):
                        ti = sm * TPS + ts_
                        e.wait_ge(t_pr, ti + 1)
                        e.wait_ge(ldxa2[ti % 2], 32 * (ti // 2 + 1))
                        W(e, sto2[ti % 2], 16 * (ti // 2))
                        for dmh in range(2):
                            ia = e.tensor_add(ot4[ti % 2][:, dmh * 512:(dmh + 1) * 512],
                                              ps_o[ti % 2][:, dmh * 512:(dmh + 1) * 512],
                                              xt4[ti % 2][:, dmh * 512:(dmh + 1) * 512])
                        VC(e, ia)
                        e.tensor_add(ot4[ti % 2][:], ot4[ti % 2][:],
                                     at4[ti % 2][:]).then_inc(v_mrg, 1)

                emit_gy(0)
                for s in range(1, NSLAB + 1):
                    if s < NSLAB:
                        emit_gy(s)
                    emit_merge(s - 1)
                e.nop().then_inc(ph, 1)

            @b4blk.gpsimd
            def _(e):
                e.wait_ge(ph, PH[0])
                e.dma_start(out=wgT_bf[:],
                            in_=wgT_d.rearrange("(a p) j -> p a j", p=128)).then_inc(g4w, 16)
                e.dma_start(out=wpT_bf[:],
                            in_=wpT_d.rearrange("(a p) j -> p a j", p=128)).then_inc(g4w, 16)

            @b4blk.sync
            def _(e):
                e.wait_ge(ph, PH[0])
                if SPARSE:
                    e.dma_start(out=xnT[:], in_=xnt_hbm).then_inc(ld_xnt, 16)
                e.dma_start(out=wT4[:], in_=wt_hbm).then_inc(ld4, 16)
                e.dma_start(out=b2f[:], in_=b2_d).then_inc(ld4, 16)
                e.dma_start(out=gnw_sb[:], in_=gnw_d).then_inc(ld4, 16)
                e.dma_start(out=gnb_sb[:], in_=gnb_d).then_inc(ld4, 16)
                e.dma_start(out=h2_sb[:], in_=h2_d).then_inc(ld4, 16)
                e.dma_start(out=g2_sb[:], in_=g2_d).then_inc(ld4, 16)
                for s in range(2):
                    e.dma_start(out=yt4[s][:],
                                in_=yT_hbm[:, :, s * SLAB:(s + 1) * SLAB]).then_inc(ldy2[s % 2], 16)
                for ti in range(2):
                    e.dma_start(out=xt4[ti][:],
                                in_=x_d[ti * 128:(ti + 1) * 128, :]).then_inc(ldxa2[ti % 2], 16)
                    e.dma_start(out=at4[ti][:],
                                in_=acc_hbm[ti * 128:(ti + 1) * 128, :]).then_inc(ldxa2[ti % 2], 16)
                for ti in range(NTI):
                    e.wait_ge(v_mrg, ti + 1)
                    e.dma_start(out=out_d[ti * 128:(ti + 1) * 128, :],
                                in_=ot4[ti % 2][:]).then_inc(sto2[ti % 2], 16)
                    if ti + 2 < NTI:
                        e.dma_start(out=xt4[ti % 2][:],
                                    in_=x_d[(ti + 2) * 128:(ti + 3) * 128, :]).then_inc(ldxa2[ti % 2], 16)
                        e.dma_start(out=at4[ti % 2][:],
                                    in_=acc_hbm[(ti + 2) * 128:(ti + 3) * 128, :]).then_inc(ldxa2[ti % 2], 16)
                    if ti % TPS == TPS - 1:
                        s = ti // TPS
                        if s + 2 < NSLAB:
                            e.wait_ge(v_gy, NJ * (s + 1))
                            e.dma_start(out=yt4[s % 2][:],
                                        in_=yT_hbm[:, :, (s + 2) * SLAB:(s + 3) * SLAB]).then_inc(ldy2[s % 2], 16)
                e.wait_ge(sto2[0], 16 * (NTI // 2))
                e.wait_ge(sto2[1], 16 * (NTI // 2))

        ctxp4.close()
        ctx4.close()

    ctx.close()
    return nc


def kernel(**inputs):
    tabs = host_tables()
    x = np.ascontiguousarray(np.asarray(inputs["L0_attn_x"], np.float32))
    w_qkv = np.asarray(inputs["L0_attn_w_qkv"], np.float32)
    w_gated = np.asarray(inputs["L0_attn_w_gated"], np.float32)
    w_proj = np.asarray(inputs["L0_attn_w_proj"], np.float32)
    gnw = np.asarray(inputs["L0_attn_gn_weight"], np.float32)
    gnb = np.asarray(inputs["L0_attn_gn_bias"], np.float32)
    Wg = np.ascontiguousarray(np.asarray(inputs["L0_ffn_Wg"], np.float32))
    bg = np.asarray(inputs["L0_ffn_bg"], np.float32)
    W1 = np.ascontiguousarray(np.asarray(inputs["L0_ffn_W1"], np.float32))
    b1 = np.asarray(inputs["L0_ffn_b1"], np.float32)
    W2 = np.ascontiguousarray(np.asarray(inputs["L0_ffn_W2"], np.float32))
    b2 = np.ascontiguousarray(np.asarray(inputs["L0_ffn_b2"], np.float32))

    common = {
        "wqkvT": np.ascontiguousarray(w_qkv.T),
        "wgT": np.ascontiguousarray(w_gated.T),
        "wpT": np.ascontiguousarray(w_proj.T),
        "gnw_r": np.ascontiguousarray(gnw.reshape(NJ, 128).T),
        "gnb_r": np.ascontiguousarray(gnb.reshape(NJ, 128).T),
        "Wg": Wg,
        "bg": np.ascontiguousarray(bg.reshape(1, NE)),
        "b1r": np.ascontiguousarray(b1.reshape(NE, NFT, 128).transpose(0, 2, 1)),
        "b2": b2,
        "W1": W1,
        "W2": W2,
        "csq": tabs["csq"], "ssq": tabs["ssq"],
        "csk": tabs["csk"], "ssk": tabs["ssk"],
        "maskr": tabs["maskr"], "gampair": tabs["gampair"],
        "kgampair": tabs["kgampair"], "p2t": tabs["p2t"],
        "ident": tabs["ident"], "h2": tabs["h2"], "g2": tabs["g2"],
        "idcol": tabs["idcol"],
    }
    nc = build_program(tabs["gam_c"])
    in_maps = [dict(common, x=np.ascontiguousarray(x[bb])) for bb in range(8)]
    res = run_bass_kernel_spmd(nc, in_maps, core_ids=list(range(8)))
    if DEBUG:
        return res
    return np.stack([np.asarray(r["out"]) for r in res.results], axis=0)


# version 07
# speedup vs baseline: 1.0932x; 1.0932x over previous
import numpy as np
import ml_dtypes
import concourse.bass as bass
import concourse.mybir as mybir
from concourse.bass_utils import run_bass_kernel_spmd
from concourse.bass_core_rust import IndirectOffsetOnAxis
from contextlib import ExitStack

F32 = mybir.dt.float32
BF16 = mybir.dt.bfloat16
F16 = mybir.dt.float16
I32 = mybir.dt.int32
AF = mybir.ActivationFunctionType
ALU = mybir.AluOpType
AX = mybir.AxisListType

T, C, H, D = 4096, 1024, 16, 64
NJ = 8
CH = 128
SLAB = 512
NSLAB = T // SLAB
CPS = SLAB // CH
NPAIR = NJ
NI = NSLAB * CPS * NPAIR
IPS = CPS * NPAIR
TPS = SLAB // 128
NE, DF = 8, 4096
NFT = DF // 128
TB = 256
NTB = T // TB
NTBG = NE * NTB
NTI = T // 128

DEBUG = set()
MAX_PHASE = 4
SPARSE = True
NB = 6
UPE = 2 * NB
NBG = NE * NB
PADPOS = T
RET_MODE = 0


def host_tables():
    half = np.arange(0, D, 2, dtype=np.float64)
    inv_freq = 1.0 / (10000.0 ** (half / D))
    t = np.arange(T, dtype=np.float64)
    freqs = np.repeat(t[:, None] * inv_freq[None, :], 2, axis=-1)
    scale_vec = (half + 0.4 * D) / (1.4 * D)
    power = (t - T // 2) / 512.0
    scale = np.repeat(scale_vec[None, :] ** power[:, None], 2, axis=-1)
    cos, sin = np.cos(freqs), np.sin(freqs)

    def dup(a):
        a = np.ascontiguousarray(a.T.astype(np.float32))
        return np.concatenate([a, a], axis=0)

    tabs = {"csq": dup(cos * scale), "ssq": dup(sin * scale),
            "csk": dup(cos / scale), "ssk": dup(sin / scale)}
    gamma = 1.0 - 2.0 ** (-5.0 - np.arange(H, dtype=np.float64))
    ii = np.arange(CH)
    diff = ii[None, :] - ii[:, None]
    m = np.where(diff >= 0, gamma[:, None, None] ** np.maximum(diff, 0)[None], 0.0)
    m = m * (D ** -0.5)
    tabs["maskr"] = np.ascontiguousarray(m.transpose(1, 0, 2).astype(np.float32))
    gp = np.zeros((128, NJ, CH), np.float64)
    kg = np.zeros((128, NJ, 128), np.float64)
    for p in range(NJ):
        for par in range(2):
            gp[par * 64:(par + 1) * 64, p, :] = gamma[2 * p + par] ** (ii + 1.0) * (D ** -0.5)
            kg[:, p, par * 64:(par + 1) * 64] = (gamma[2 * p + par] ** (CH - 1.0 - ii))[:, None]
    tabs["gampair"] = gp.astype(np.float32)
    tabs["kgampair"] = kg.astype(np.float32)
    tabs["gam_c"] = [float(g ** CH) for g in gamma]
    P = np.zeros((64, 64), np.float32)
    for i in range(32):
        P[2 * i, 2 * i + 1] = -1.0
        P[2 * i + 1, 2 * i] = 1.0
    P2 = np.zeros((128, 128), np.float32)
    P2[0:64, 0:64] = P
    P2[64:128, 64:128] = P
    tabs["p2t"] = np.ascontiguousarray(P2.T)
    tabs["ident"] = np.eye(128, dtype=np.float32)
    h2 = np.zeros((128, 2), np.float32)
    h2[0:64, 0] = 1.0
    h2[64:128, 1] = 1.0
    tabs["h2"] = h2
    tabs["g2"] = np.ascontiguousarray(h2.T)
    idc = np.zeros((128, T // 128), np.float32)
    for ti in range(T // 128):
        idc[:, ti] = ti * 128 + np.arange(128)
    tabs["idcol"] = idc
    return tabs


def build_program(gam_c):
    nc = bass.Bass("TRN2", target_bir_lowering=False, debug=False)

    def din(name, shape):
        return nc.dram_tensor(name, shape, F32, kind="ExternalInput").ap()

    x_d = din("x", [T, C])
    wqkvT_d = din("wqkvT", [C, 3 * C])
    wgT_d = din("wgT", [C, C])
    wpT_d = din("wpT", [C, C])
    gnw_d = din("gnw_r", [128, NJ])
    gnb_d = din("gnb_r", [128, NJ])
    Wg_d = din("Wg", [C, NE])
    bg_d = din("bg", [1, NE])
    b1_d = din("b1r", [NE, 128, NFT])
    b2_d = din("b2", [NE, C])
    W1_d = din("W1", [NE, C, DF])
    W2_d = din("W2", [NE, DF, C])
    csq_d = din("csq", [128, T]); ssq_d = din("ssq", [128, T])
    csk_d = din("csk", [128, T]); ssk_d = din("ssk", [128, T])
    maskr_d = din("maskr", [128, H, CH])
    gampair_d = din("gampair", [128, NJ, CH])
    kgam_d = din("kgampair", [128, NJ, 128])
    p2t_d = din("p2t", [128, 128])
    ident_d = din("ident", [128, 128])
    h2_d = din("h2", [128, 2])
    g2_d = din("g2", [2, 128])
    idcol_d = din("idcol", [128, NTI])

    out_d = nc.dram_tensor("out", [T, C], F32, kind="ExternalOutput").ap()
    ytk = "ExternalOutput" if "ph2" in DEBUG else "Internal"
    yT_hbm = nc.dram_tensor("yT_hbm", [128, NJ, T], BF16, kind=ytk).ap()
    acck = "ExternalOutput" if "ph3" in DEBUG else "Internal"
    acc_hbm = nc.dram_tensor("acc_hbm", [T + 128, C], F16, kind=acck).ap()
    xn_hbm = nc.dram_tensor("xn_hbm", [T + 128, C], BF16, kind="Internal").ap()
    xnt_hbm = nc.dram_tensor("xnt_hbm", [128, NJ, T], BF16, kind="Internal").ap()
    pair_hbm = nc.dram_tensor("pair_hbm", [NE, T + 128, 2], F32, kind="Internal").ap()
    wt_hbm = nc.dram_tensor("wt_hbm", [NE, T], BF16, kind="Internal").ap()
    if "ph1" in DEBUG:
        dbg_xnT = nc.dram_tensor("dbg_xnT", [128, NJ, T], BF16, kind="ExternalOutput").ap()
        dbg_w = nc.dram_tensor("dbg_w", [128, NTI, NE], F32, kind="ExternalOutput").ap()
    if "ph2" in DEBUG:
        dbg_st = nc.dram_tensor("dbg_st", [128, 2 * NJ], F32, kind="ExternalOutput").ap()

    ctx = ExitStack()
    sbp = lambda name, shape, dt: ctx.enter_context(nc.sbuf_tensor(name, shape, dt))
    if not SPARSE:
        xnT = sbp("xnT", [128, NJ, T], BF16)
    w_nat = sbp("w_nat", [128, NTI, NE], F32)
    b1_sb = sbp("b1_sb", [128, NE, NFT], F32)
    S1acc = sbp("S1acc", [128, NJ], F32)
    S2acc = sbp("S2acc", [128, NJ], F32)
    identb = sbp("identb", [128, 128], BF16)
    ident32 = sbp("ident32", [128, 128], F32)

    sems = {}
    def S(name):
        sems[name] = Sem = nc.alloc_semaphore(name)
        return Sem
    pre = S("pre"); pre_w = S("pre_w"); pre_v = S("pre_v")
    ph = S("ph"); vch = S("vch")
    PH = [0]
    _vchn = [0]

    def VC(v, inst):
        inst.then_inc(vch, 1)
        _vchn[0] += 1
        v.wait_ge(vch, _vchn[0])

    def W(e, sem, val):
        if val > 0:
            e.wait_ge(sem, val)

    sch = S("sch")
    _schn = [0]

    def SC(eng, inst):
        inst.then_inc(sch, 1)
        _schn[0] += 1
        eng.wait_ge(sch, _schn[0])

    PRE_TOT = 16 * (5 if SPARSE else 4)
    ctx12 = ExitStack()
    sb12 = lambda name, shape, dt: ctx12.enter_context(nc.sbuf_tensor(name, shape, dt))
    wq_bf = sb12("wq_bf", [128, NJ, 3 * C], BF16)
    p2tb = sb12("p2tb", [128, 128], BF16)
    if SPARSE:
        offn = sb12("offn", [128, NTI, NE], F32)
        offi = sb12("offi", [128, NTI, NE], I32)
        idcol_sb = sb12("idcol_sb", [128, NTI], F32)
        pairs_all = sb12("pairs_all", [128, NTI, NE, 2], F32)
        ones8 = sb12("ones8", [128, NE], F32)

    ld_x4 = [S(f"ld_x{i}") for i in range(4)]
    s_sq = S("s_sq"); v_va = S("v_va"); s_rstd = S("s_rstd")
    s_xn = S("s_xn"); t_tr = S("t_tr"); s_tre = S("s_tre")
    v_tre = S("v_tre"); t_gate = S("t_gate"); v_g = S("v_g"); v_sm1 = S("v_sm1")
    s_e = S("s_e"); v_w = S("v_w"); t_wt = S("t_wt"); v_wt = S("v_wt")
    st_wt = S("st_wt")
    sp_x = [S("sp_x0"), S("sp_x1")]
    st_xnt = S("st_xnt"); ppf = S("ppf"); z_a = S("z_a")
    t_ot = S("t_ot"); v_ote = S("v_ote"); v_cmp = S("v_cmp")

    ctx1 = ExitStack()
    sb1 = lambda name, shape, dt: ctx1.enter_context(nc.sbuf_tensor(name, shape, dt))
    xt = [sb1(f"xt{i}", [128, C], F32) for i in range(4)]
    if SPARSE:
        xnT = sb1("xnT", [128, NJ, T], BF16)
    sqscr = [sb1(f"sqscr{i}", [128, C], F32) for i in range(2)]
    xnf = [sb1(f"xnf{i}", [128, C], F32) for i in range(2)]
    xnb = [sb1(f"xnb{i}", [128, C], BF16) for i in range(2)]
    sx = [sb1(f"sx{i}", [128, 1], F32) for i in range(2)]
    sxx = [sb1(f"sxx{i}", [128, 1], F32) for i in range(2)]
    msc = [sb1(f"msc{i}", [128, 1], F32) for i in range(4)]
    msq = sb1("msq", [128, 1], F32)
    veps = [sb1(f"veps{i}", [128, 1], F32) for i in range(2)]
    rstdc = [sb1(f"rstdc{i}", [128, 1], F32) for i in range(2)]
    nbc = [sb1(f"nbc{i}", [128, 1], F32) for i in range(2)]
    Wg_sb = sb1("Wg_sb", [128, NJ, NE], F32)
    Wg_bf = sb1("Wg_bf", [128, NJ, NE], BF16)
    bg_sb = sb1("bg_sb", [1, NE], F32)
    bg_bf = sb1("bg_bf", [1, NE], BF16)
    ones1 = sb1("ones1", [1, 128], BF16)
    lg = [sb1(f"lg{i}", [128, NE], F32) for i in range(2)]
    pe_ = [sb1(f"pe{i}", [128, NE], F32) for i in range(2)]
    lmn = [sb1(f"lmn{i}", [128, 1], F32) for i in range(2)]
    sm = sb1("sm", [128, NE], F32)
    sm2 = sb1("sm2", [128, NE], F32)
    sme = sb1("sme", [128, NE], F32)
    smw = sb1("smw", [128, NE], F32)
    col_a = sb1("col_a", [128, 1], F32)
    col_b = sb1("col_b", [128, 1], F32)
    wT_sb = sb1("wT_sb", [NE, T], BF16)
    if SPARSE:
        mskT = sb1("mskT", [NE, T], F32)
        scanT = sb1("scanT", [NE, T], F32)
        tmp8 = sb1("tmp8", [128, NE], F32)
        pft = sb1("pft", [128, UPE, 2], F32)
        zt16 = sb1("zt16", [128, C], F16)
        zbf = sb1("zbf", [128, C], BF16)

    ctxp1 = ExitStack()
    pt1 = lambda name, shape, dt: ctxp1.enter_context(nc.psum_tensor(name, shape, dt))
    ps_tr = [pt1(f"ps_tr{i}", [128, 128], BF16) for i in range(2)]
    ps_tro = [pt1(f"ps_tro{i}", [128, NE], F32) for i in range(2)]
    ps_gate = [pt1(f"ps_gate{i}", [128, NE], F32) for i in range(2)]
    ps_wt = [pt1(f"ps_wt{i}", [NE, 128], F32) for i in range(2)]

    with nc.Block() as b1blk:
        @b1blk.tensor
        def _(e):
            e.wait_ge(pre, PRE_TOT)
            e.wait_ge(pre_v, 1)
            e.wait_ge(pre_w, 48)

            def peD(t):
                e.wait_ge(s_xn, t + 1)
                for c in range(NJ):
                    g = NJ * t + c
                    W(e, s_tre, g - 1)
                    e.transpose(ps_tr[g % 2][:], xnb[t % 2][:, c * 128:(c + 1) * 128],
                                identb[:]).then_inc(t_tr, 1)

            def peE(t):
                W(e, v_g, t - 1)
                e.wait_ge(s_tre, NJ * t + NJ)
                for c in range(NJ):
                    e.matmul(ps_gate[t % 2][:], xnT[:, c, t * 128:(t + 1) * 128],
                             Wg_bf[:, c, :], start=(c == 0), stop=False)
                e.matmul(ps_gate[t % 2][:], ones1[:], bg_bf[:],
                         start=False, stop=True).then_inc(t_gate, 1)

            for k in range(NTI + 3):
                if 0 <= k - 2 < NTI:
                    peD(k - 2)
                if 0 <= k - 3 < NTI:
                    peE(k - 3)
            for ti in range(NTI):
                e.wait_ge(v_w, ti + 1)
                W(e, v_wt, ti - 1)
                e.transpose(ps_wt[ti % 2][:], w_nat[:, ti, :], ident32[:]).then_inc(t_wt, 1)
            e.wait_ge(v_wt, NTI)
            if SPARSE:
                e.wait_ge(v_cmp, 2)
                e.wait_ge(s_tre, NJ * NTI)
                for ti in range(NTI):
                    W(e, v_ote, ti - 1)
                    e.transpose(ps_tro[ti % 2][:], scanT[:, ti * 128:(ti + 1) * 128],
                                ident32[0:NE, 0:NE]).then_inc(t_ot, 1)
                e.wait_ge(v_cmp, 3)
            e.nop().then_inc(ph, 1)

        @b1blk.scalar
        def _(e):
            def sA(t):
                e.wait_ge(ld_x4[t % 4], 16 * (t // 4 + 1))
                W(e, v_va, t - 1)
                e.activation(sqscr[t % 2][:], xt[t % 4][:], AF.Square,
                             accum_out=sxx[t % 2][:]).then_inc(s_sq, 1)

            def sB(t):
                e.wait_ge(v_va, t + 1)
                W(e, s_xn, t - 1)
                SC(e, e.activation(veps[t % 2][:], veps[t % 2][:], AF.Ln))
                e.activation(rstdc[t % 2][:], veps[t % 2][:], AF.Exp,
                             scale=-0.5).then_inc(s_rstd, 1)

            def sD(t):
                for c in range(NJ):
                    g = NJ * t + c
                    e.wait_ge(t_tr, g + 1)
                    e.activation(xnT[:, c, t * 128:(t + 1) * 128],
                                 ps_tr[g % 2][:], AF.Copy).then_inc(s_tre, 1)

            def sC(t):
                e.wait_ge(v_sm1, t + 1)
                W(e, v_w, t - 1)
                e.activation(pe_[t % 2][:], lg[t % 2][:], AF.Exp,
                             bias=lmn[t % 2][:]).then_inc(s_e, 1)

            for k in range(NTI + 4):
                if 0 <= k < NTI:
                    sA(k)
                if 0 <= k - 1 < NTI:
                    sB(k - 1)
                if 0 <= k - 2 < NTI:
                    sD(k - 2)
                if 0 <= k - 4 < NTI:
                    sC(k - 4)
            e.wait_ge(v_w, NTI)
            e.nop().then_inc(ph, 1)

        @b1blk.vector
        def _(e):
            e.wait_ge(pre, PRE_TOT)
            e.memset(ones1[:], 1.0)
            e.wait_ge(pre, PRE_TOT)
            e.tensor_copy(Wg_bf[:], Wg_sb[:])
            VC(e, e.tensor_copy(bg_bf[:], bg_sb[:]))
            e.memset(S1acc[:], 0.0)
            e.memset(S2acc[:], 0.0).then_inc(pre_v, 1)
            if SPARSE:
                e.memset(pft[:, :, 0:1], float(PADPOS))
                e.memset(pft[:, :, 1:2], 0.0)
                e.memset(zt16[:], 0.0)
                e.memset(zbf[:], 0.0).then_inc(v_cmp, 1)
            def vA(t):
                e.wait_ge(ld_x4[t % 4], 16 * (t // 4 + 1))
                VC(e, e.reduce_sum(sx[t % 2][:], xt[t % 4][:], AX.X))
                VC(e, e.tensor_scalar_mul(msc[t % 4][:], sx[t % 2][:], 1.0 / C))

            def vB(t):
                e.wait_ge(s_sq, t + 1)
                W(e, s_rstd, t - 1)
                VC(e, e.tensor_mul(msq[:], msc[t % 4][:], msc[t % 4][:]))
                VC(e, e.tensor_scalar_mul(veps[t % 2][:], sxx[t % 2][:], 1.0 / C))
                VC(e, e.tensor_sub(veps[t % 2][:], veps[t % 2][:], msq[:]))
                VC(e, e.tensor_scalar_add(veps[t % 2][:], veps[t % 2][:], 1e-5))
                e.nop().then_inc(v_va, 1)

            def vCst(t):
                e.wait_ge(s_rstd, t + 1)
                VC(e, e.scalar_tensor_tensor(nbc[t % 2][:], msc[t % 4][:], -1.0,
                                             rstdc[t % 2][:], ALU.mult, ALU.mult))
                if SPARSE:
                    W(e, sp_x[t % 2], 16 * (t // 2))
                W(e, t_tr, NJ * (t - 1))
                VC(e, e.tensor_scalar(xnf[t % 2][:], xt[t % 4][:], rstdc[t % 2][:],
                                      nbc[t % 2][:], ALU.mult, ALU.add))
                e.tensor_copy(xnb[t % 2][:], xnf[t % 2][:]).then_inc(s_xn, 1)

            def vE(t):
                e.wait_ge(t_gate, t + 1)
                W(e, s_e, t - 1)
                VC(e, e.tensor_copy(lg[t % 2][:], ps_gate[t % 2][:]))
                e.nop().then_inc(v_g, 1)
                VC(e, e.reduce_max(col_a[:], lg[t % 2][:], AX.X))
                VC(e, e.tensor_scalar_mul(lmn[t % 2][:], col_a[:], -1.0))
                e.nop().then_inc(v_sm1, 1)

            def vF(t):
                e.wait_ge(s_e, t + 1)
                VC(e, e.reduce_sum(col_a[:], pe_[t % 2][:], AX.X))
                VC(e, e.reciprocal(col_b[:], col_a[:]))
                VC(e, e.tensor_scalar_mul(sm[:], pe_[t % 2][:], col_b[:]))
                VC(e, e.reduce_max(col_a[:], sm[:], AX.X))
                VC(e, e.tensor_scalar(sme[:], sm[:], col_a[:], None, ALU.is_equal))
                VC(e, e.scalar_tensor_tensor(sm2[:], sme[:], -1e9, sm[:], ALU.mult, ALU.add))
                VC(e, e.reduce_max(col_a[:], sm2[:], AX.X))
                VC(e, e.tensor_scalar(sme[:], sm[:], col_a[:], None, ALU.is_ge))
                VC(e, e.tensor_mul(smw[:], sm[:], sme[:]))
                VC(e, e.reduce_sum(col_a[:], smw[:], AX.X))
                VC(e, e.reciprocal(col_b[:], col_a[:]))
                e.tensor_scalar_mul(w_nat[:, t, :], smw[:], col_b[:]).then_inc(v_w, 1)

            for k in range(NTI + 6):
                if 0 <= k < NTI:
                    vA(k)
                if 0 <= k - 1 < NTI:
                    vB(k - 1)
                if 0 <= k - 2 < NTI:
                    vCst(k - 2)
                if 0 <= k - 5 < NTI:
                    vE(k - 5)
                if 0 <= k - 6 < NTI:
                    vF(k - 6)
            for ti in range(NTI):
                e.wait_ge(t_wt, ti + 1)
                iw = e.tensor_copy(wT_sb[:, ti * 128:(ti + 1) * 128],
                                   ps_wt[ti % 2][:])
                if ti == NTI - 1:
                    VC(e, iw)
                    e.nop().then_inc(v_wt, 1)
                else:
                    iw.then_inc(v_wt, 1)
            if SPARSE:
                VC(e, e.tensor_single_scalar(mskT[:], wT_sb[:], 0.0, ALU.is_gt))
                VC(e, e.tensor_tensor_scan(scanT[:], mskT[:], mskT[:], 0.0,
                                           ALU.add, ALU.bypass))
                VC(e, e.tensor_scalar_add(scanT[:], scanT[:], -1.0))
                VC(e, e.tensor_mul(scanT[:], scanT[:], mskT[:]))
                VC(e, e.tensor_scalar(mskT[:], mskT[:], -float(PADPOS), float(PADPOS),
                                      ALU.mult, ALU.add))
                VC(e, e.tensor_add(scanT[:], scanT[:], mskT[:]))
                e.nop().then_inc(v_cmp, 1)
                for ti in range(NTI):
                    e.wait_ge(t_ot, ti + 1)
                    VC(e, e.tensor_copy(offn[:, ti, :], ps_tro[ti % 2][:]))
                    if ti > 0:
                        VC(e, e.tensor_single_scalar(tmp8[:], offn[:, ti, :],
                                                     float(PADPOS) - 0.5, ALU.is_ge))
                        VC(e, e.scalar_tensor_tensor(offn[:, ti, :], tmp8[:], float(ti),
                                                     offn[:, ti, :], ALU.mult, ALU.add))
                    e.nop().then_inc(v_ote, 1)
                VC(e, e.tensor_copy(offi[:], offn[:]))
                e.nop().then_inc(v_cmp, 1)

            e.wait_ge(st_wt, 16)
            e.nop().then_inc(ph, 1)

        @b1blk.gpsimd
        def _(e):
            e.dma_start(out=wq_bf[:],
                        in_=wqkvT_d.rearrange("(a p) j -> p a j", p=128)).then_inc(pre_w, 16)
            e.dma_start(out=p2tb[:], in_=p2t_d).then_inc(pre_w, 16)
            e.dma_start(out=identb[:], in_=ident_d).then_inc(pre_w, 16)
            if SPARSE:
                for ti in range(NTI):
                    e.wait_ge(s_xn, ti + 1)
                    e.dma_start(out=xn_hbm[ti * 128:(ti + 1) * 128, :],
                                in_=xnb[ti % 2][:]).then_inc(sp_x[ti % 2], 16)
                e.wait_ge(sp_x[0], 16 * (NTI // 2))
                e.wait_ge(sp_x[1], 16 * (NTI // 2))
            e.nop().then_inc(ph, 1)

        @b1blk.sync
        def _(e):
            e.dma_start(out=Wg_sb[:],
                        in_=Wg_d.rearrange("(a p) e -> p a e", p=128)).then_inc(pre, 16)
            e.dma_start(out=bg_sb[:], in_=bg_d).then_inc(pre, 16)
            e.dma_start(out=b1_sb[:], in_=b1_d.rearrange("e p a -> p e a")).then_inc(pre, 16)
            e.dma_start(out=ident32[:], in_=ident_d).then_inc(pre, 16)
            if SPARSE:
                e.dma_start(out=idcol_sb[:], in_=idcol_d).then_inc(pre, 16)
                e.wait_ge(v_cmp, 1)
                for ee in range(NE):
                    e.dma_start(out=pair_hbm[ee].rearrange("(a p) c -> p a c", p=128)[:, 0:UPE, :],
                                in_=pft[:]).then_inc(ppf, 16)
                for zi in range((T + 128) // 128):
                    e.dma_start(out=acc_hbm[zi * 128:(zi + 1) * 128, :],
                                in_=zt16[:]).then_inc(z_a, 16)
                e.dma_start(out=xn_hbm[T:T + 128, :], in_=zbf[:]).then_inc(z_a, 16)
            for ti in range(NTI):
                W(e, s_xn, ti - 3)
                e.dma_start(out=xt[ti % 4][:],
                            in_=x_d[ti * 128:(ti + 1) * 128, :]).then_inc(ld_x4[ti % 4], 16)
            if SPARSE:
                e.wait_ge(s_tre, NJ * NTI)
                e.dma_start(out=xnt_hbm, in_=xnT[:]).then_inc(st_xnt, 16)
                e.wait_ge(ppf, 16 * NE)
                e.wait_ge(z_a, 16 * ((T + 128) // 128 + 1))
                e.wait_ge(st_xnt, 16)
            e.wait_ge(v_wt, NTI)
            e.dma_start(out=wt_hbm, in_=wT_sb[:]).then_inc(st_wt, 16)
            nst = 16
            if "ph1" in DEBUG:
                e.wait_ge(s_tre, NJ * NTI)
                e.dma_start(out=dbg_xnT, in_=xnT[:]).then_inc(st_wt, 16)
                e.wait_ge(v_w, NTI)
                e.dma_start(out=dbg_w, in_=w_nat[:]).then_inc(st_wt, 16)
                nst = 48
            e.wait_ge(st_wt, nst)
            e.nop().then_inc(ph, 1)

    ctxp1.close()
    ctx1.close()
    PH[0] = 5

    if MAX_PHASE >= 2:
        tab2 = [S("tab0"), S("tab1")]
        t_qkv = S("t_qkv"); s_raw = S("s_raw"); t_rot = S("t_rot")
        v_qkc = S("v_qkc"); t_at = S("t_at"); v_mask = S("v_mask")
        t_y = S("t_y"); s_ye = S("s_ye"); t_ktr = S("t_ktr"); v_kg = S("v_kg")
        t_s = S("t_s"); v_s = S("v_s"); s_st = S("s_st"); v_st = S("v_st")
        sty2 = [S("st_y0"), S("st_y1")]
        ldxt2 = [S("ld_xt0"), S("ld_xt1")]
        v_pb = S("v_pb"); g_scp2 = [S("g_scp0"), S("g_scp1")]

        ctx2 = ExitStack()
        sb2 = lambda name, shape, dt: ctx2.enter_context(nc.sbuf_tensor(name, shape, dt))
        tab_sb = [sb2(f"tab{i}", [128, 4, SLAB], F32) for i in range(2)]
        xnTs = [sb2(f"xnTs{i}", [128, NJ, SLAB], BF16) for i in range(2)]
        qT_sl = [sb2(f"qT{i}", [128, NJ, SLAB], BF16) for i in range(2)]
        kT_sl = [sb2(f"kT{i}", [128, NJ, SLAB], BF16) for i in range(2)]
        v_sl = [sb2(f"v{i}", [128, CPS, C], BF16) for i in range(2)]
        yT_sl = [sb2(f"yT{i}", [128, NJ, SLAB], BF16) for i in range(2)]
        maskr_bf = sb2("maskr_bf", [128, H, CH], BF16)
        gamp_bf = sb2("gamp_bf", [128, NJ, CH], BF16)
        kgam_bf = sb2("kgam_bf", [128, NJ, 128], BF16)
        S_f32 = sb2("S_f32", [128, NJ, 64], F32)
        S_bf = sb2("S_bf", [128, NJ, 128], BF16)
        qraw = [sb2(f"qraw{i}", [128, SLAB], BF16) for i in range(2)]
        rtmp1 = sb2("rtmp1", [128, SLAB], BF16)
        rtmp2 = sb2("rtmp2", [128, SLAB], BF16)
        at_bf = [sb2(f"at_bf{i}", [128, 2, CH], BF16) for i in range(2)]
        qg_sb = [sb2(f"qg{i}", [128, CH], BF16) for i in range(2)]
        kg_sb = [sb2(f"kg{i}", [128, 128], BF16) for i in range(2)]
        S1tmp = sb2("S1tmp", [128, NJ, 1], F32)
        S2col = [sb2(f"S2col{i}", [128, NJ], F32) for i in range(2)]
        sqscr2 = sb2("sqscr2", [128, SLAB], F32)

        ctxp2 = ExitStack()
        pt2 = lambda name, shape, dt: ctxp2.enter_context(nc.psum_tensor(name, shape, dt))
        ps_qkv = pt2("ps_qkv", [128, 512], F32)
        ps_rot = pt2("ps_rot", [128, SLAB], F32)
        ps_at = [pt2("ps_at_e", [128, 128], F32), pt2("ps_at_o", [128, 128], F32)]
        ps_y = [pt2(f"ps_y{i}", [128, 256], F32) for i in range(2)]
        ps_ktr = pt2("ps_ktr", [128, 128], BF16)
        ps_s = pt2("ps_s", [128, 128], F32)

        NG = 16 + 2 * CPS
        TABS_D = [csq_d, ssq_d, csk_d, ssk_d]

        def prod_groups(s):
            gs = []
            for part in range(2):
                for jt in range(NJ):
                    gs.append(("qk", part, jt))
            for ts_ in range(CPS):
                for dmh in range(2):
                    gs.append(("v", ts_, dmh))
            return gs

        with nc.Block() as b2blk:
            def gi_sp(j):
                sj = j // IPS
                ck, p = (j - sj * IPS) // 8, (j - sj * IPS) % 8
                return sj, ck, p

            @b2blk.tensor
            def _(e):
                e.wait_ge(ph, PH[0])
                e.wait_ge(pre_w, 96)

                def emit_prod(s):
                    for gl, (kind, a1, a2) in enumerate(prod_groups(s)):
                        m = s * NG + gl
                        W(e, s_raw, m)
                        if gl == 0:
                            e.wait_ge(ldxt2[s % 2], 16 * (s // 2 + 1))
                        if kind == "qk":
                            part, jt = a1, a2
                            joff = part * C + jt * 128
                            for c in range(NJ):
                                im = e.matmul(ps_qkv[:, 0:SLAB], wq_bf[:, c, joff:joff + 128],
                                              xnTs[s % 2][:, c, :], start=(c == 0),
                                              stop=(c == NJ - 1))
                            im.then_inc(t_qkv, 1)
                            r = s * 16 + part * 8 + jt
                            e.wait_ge(s_raw, m + 1)
                            W(e, v_qkc, r)
                            e.matmul(ps_rot[:, 0:SLAB], p2tb[:], qraw[m % 2][:],
                                     start=True, stop=True).then_inc(t_rot, 1)
                        else:
                            ts_, dmh = a1, a2
                            for c in range(NJ):
                                im = e.matmul(ps_qkv[:, 0:512],
                                              xnTs[s % 2][:, c, ts_ * 128:(ts_ + 1) * 128],
                                              wq_bf[:, c, 2 * C + dmh * 512:2 * C + (dmh + 1) * 512],
                                              start=(c == 0), stop=(c == NJ - 1))
                            im.then_inc(t_qkv, 1)

                def emit_at(j):
                    sj, ck, p = gi_sp(j)
                    ccols = slice(ck * CH, (ck + 1) * CH)
                    e.wait_ge(v_qkc, sj * 16 + 8 + p + 1)
                    W(e, v_mask, j)
                    e.matmul(ps_at[0][:], kT_sl[sj % 2][0:64, p, ccols],
                             qT_sl[sj % 2][0:64, p, ccols], start=True, stop=True)
                    e.matmul(ps_at[1][:], kT_sl[sj % 2][64:128, p, ccols],
                             qT_sl[sj % 2][64:128, p, ccols], start=True,
                             stop=True).then_inc(t_at, 1)

                def emit_supd(j):
                    if j < 0:
                        return
                    sj, ck, p = gi_sp(j)
                    e.wait_ge(v_kg, j + 1)
                    W(e, v_s, j - 1)
                    e.matmul(ps_s[:, 0:64], kg_sb[j % 2][:],
                             v_sl[sj % 2][:, ck, p * 128:p * 128 + 64],
                             start=True, stop=True)
                    e.matmul(ps_s[:, 64:128], kg_sb[j % 2][:],
                             v_sl[sj % 2][:, ck, p * 128 + 64:p * 128 + 128],
                             start=True, stop=True).then_inc(t_s, 1)

                def emit_ret(s):
                    i0 = s * IPS
                    e.wait_ge(s_raw, (s + 1) * NG)
                    emit_at(i0)
                    emit_at(i0 + 1)
                    for i in range(i0, i0 + IPS):
                        sj, ck, p = gi_sp(i)
                        e.wait_ge(v_mask, i + 1)
                        W(e, s_ye, i - 1)
                        W(e, v_s, i - 7)
                        vpair = v_sl[s % 2][:, ck, p * 128:(p + 1) * 128]
                        e.matmul(ps_y[i % 2][:, 0:128], vpair, at_bf[i % 2][:, 0, :],
                                 start=True, stop=False)
                        e.matmul(ps_y[i % 2][:, 0:128], S_bf[:, p, :], qg_sb[i % 2][:],
                                 start=False, stop=True)
                        e.matmul(ps_y[i % 2][:, 128:256], vpair, at_bf[i % 2][:, 1, :],
                                 start=True, stop=False)
                        e.matmul(ps_y[i % 2][:, 128:256], S_bf[:, p, :], qg_sb[i % 2][:],
                                 start=False, stop=True).then_inc(t_y, 1)
                        W(e, v_kg, i)
                        e.transpose(ps_ktr[:], kT_sl[s % 2][:, p, ck * CH:(ck + 1) * CH],
                                    identb[:]).then_inc(t_ktr, 1)
                        if i > i0:
                            emit_supd(i - 1)
                        if i + 2 < i0 + IPS:
                            emit_at(i + 2)
                    emit_supd(i0 + IPS - 1)

                emit_prod(0)
                for s in range(1, NSLAB + 1):
                    if s < NSLAB:
                        emit_prod(s)
                    emit_ret(s - 1)
                e.wait_ge(v_s, NI)
                e.nop().then_inc(ph, 1)

            @b2blk.vector
            def _(e):
                e.wait_ge(ph, PH[0])
                e.wait_ge(pre_w, 96)
                e.memset(S_f32[:], 0.0)
                e.memset(S_bf[:], 0.0)

                def emit_prodv(s):
                    for gl, (kind, a1, a2) in enumerate(prod_groups(s)):
                        m = s * NG + gl
                        if kind == "qk":
                            part, jt = a1, a2
                            r = s * 16 + part * 8 + jt
                            e.wait_ge(s_raw, m + 1)
                            e.wait_ge(t_rot, r + 1)
                            e.wait_ge(tab2[s % 2], (s // 2 + 1) * 64)
                            if s >= 2 and gl == 0:
                                e.wait_ge(t_at, (s - 1) * IPS)
                                e.wait_ge(t_ktr, (s - 1) * IPS)
                                e.wait_ge(t_y, (s - 1) * IPS)
                                e.wait_ge(t_s, (s - 1) * IPS)
                            ci = 0 if part == 0 else 2
                            e.tensor_mul(rtmp1[:], qraw[m % 2][:], tab_sb[s % 2][:, ci, :])
                            VC(e, e.tensor_mul(rtmp2[:], ps_rot[:], tab_sb[s % 2][:, ci + 1, :]))
                            dst = (qT_sl if part == 0 else kT_sl)[s % 2][:, jt, :]
                            e.tensor_add(dst, rtmp1[:], rtmp2[:]).then_inc(v_qkc, 1)

                def emit_maskqg(j):
                    sj, ck, p = gi_sp(j)
                    ccols = slice(ck * CH, (ck + 1) * CH)
                    e.wait_ge(t_at, j + 1)
                    e.tensor_mul(at_bf[j % 2][:, 0, :], ps_at[0][:],
                                 maskr_bf[:, 2 * p, :])
                    e.tensor_mul(at_bf[j % 2][:, 1, :], ps_at[1][:],
                                 maskr_bf[:, 2 * p + 1, :])
                    e.tensor_mul(qg_sb[j % 2][:], qT_sl[sj % 2][:, p, ccols],
                                 gamp_bf[:, p, :]).then_inc(v_mask, 1)

                def emit_tail(j):
                    if j < 0:
                        return
                    sj, ck, p = gi_sp(j)
                    e.wait_ge(t_ktr, j + 1)
                    e.tensor_mul(kg_sb[j % 2][:], ps_ktr[:],
                                 kgam_bf[:, p, :]).then_inc(v_kg, 1)
                    e.wait_ge(t_s, j + 1)
                    ii_ = e.scalar_tensor_tensor(S_f32[0:64, p, :], S_f32[0:64, p, :],
                                                 gam_c[2 * p], ps_s[0:64, 0:64],
                                                 ALU.mult, ALU.add)
                    ii_ = e.scalar_tensor_tensor(S_f32[64:128, p, :], S_f32[64:128, p, :],
                                                 gam_c[2 * p + 1], ps_s[64:128, 64:128],
                                                 ALU.mult, ALU.add)
                    VC(e, ii_)
                    e.tensor_copy(S_bf[0:64, p, 0:64], S_f32[0:64, p, :])
                    e.tensor_copy(S_bf[64:128, p, 64:128],
                                  S_f32[64:128, p, :]).then_inc(v_s, 1)

                def emit_retv(s):
                    i0 = s * IPS
                    for i in range(i0, i0 + IPS):
                        emit_maskqg(i)
                        if i > i0:
                            emit_tail(i - 1)
                    emit_tail(i0 + IPS - 1)
                    e.wait_ge(s_ye, IPS * (s + 1))
                    VC(e, e.tensor_reduce(S1tmp[:], yT_sl[s % 2][:], AX.X, ALU.add))
                    e.tensor_add(S1acc[:], S1acc[:], S1tmp[:, :, 0])
                    e.wait_ge(s_st, s + 1)
                    i2 = e.tensor_add(S2acc[:], S2acc[:], S2col[s % 2][:])
                    if s == NSLAB - 1:
                        VC(e, i2)
                        e.nop().then_inc(v_st, 1)
                    else:
                        i2.then_inc(v_st, 1)

                emit_prodv(0)
                if SPARSE:
                    e.wait_ge(pre, PRE_TOT)
                    VC(e, e.memset(ones8[:], 1.0))
                    for ti in range(NTI):
                        e.tensor_scalar_mul(pairs_all[:, ti, :, 0], ones8[:],
                                            idcol_sb[:, ti:ti + 1])
                    iv = e.tensor_copy(pairs_all[:, :, :, 1], w_nat[:, :, :])
                    VC(e, iv)
                    e.nop().then_inc(v_pb, 1)
                for s in range(1, NSLAB + 1):
                    if s < NSLAB:
                        emit_prodv(s)
                    emit_retv(s - 1)
                e.nop().then_inc(ph, 1)

            @b2blk.scalar
            def _(e):
                e.wait_ge(ph, PH[0])

                def emit_prods(s):
                    for gl, (kind, a1, a2) in enumerate(prod_groups(s)):
                        m = s * NG + gl
                        e.wait_ge(t_qkv, m + 1)
                        if kind == "qk":
                            e.activation(qraw[m % 2][:], ps_qkv[:, 0:SLAB],
                                         AF.Copy).then_inc(s_raw, 1)
                        else:
                            ts_, dmh = a1, a2
                            e.activation(v_sl[s % 2][:, ts_, dmh * 512:(dmh + 1) * 512],
                                         ps_qkv[:, 0:512], AF.Copy).then_inc(s_raw, 1)

                def emit_rets(s):
                    i0 = s * IPS
                    for i in range(i0, i0 + IPS):
                        sj, ck, p = gi_sp(i)
                        ccols = slice(ck * CH, (ck + 1) * CH)
                        e.wait_ge(t_y, i + 1)
                        if s >= 2 and i == i0:
                            e.wait_ge(sty2[s % 2], 16 * (s // 2))
                            e.wait_ge(v_st, s - 1)
                        e.activation(yT_sl[s % 2][0:64, p, ccols],
                                     ps_y[i % 2][0:64, 0:128], AF.Copy)
                        iy = e.activation(yT_sl[s % 2][64:128, p, ccols],
                                          ps_y[i % 2][64:128, 128:256], AF.Copy)
                        if i == i0 + 15:
                            SC(e, iy)
                            e.nop().then_inc(s_ye, 1)
                        else:
                            iy.then_inc(s_ye, 1)
                    W(e, v_st, s - 1)
                    for j in range(NJ):
                        ii_ = e.activation(sqscr2[:], yT_sl[s % 2][:, j, :], AF.Square,
                                           accum_out=S2col[s % 2][:, j:j + 1])
                        if j < NJ - 1:
                            SC(e, ii_)
                    ii_.then_inc(s_st, 1)

                emit_prods(0)
                for s in range(1, NSLAB + 1):
                    if s < NSLAB:
                        emit_prods(s)
                    emit_rets(s - 1)
                e.wait_ge(v_st, NSLAB)
                e.nop().then_inc(ph, 1)

            @b2blk.gpsimd
            def _(e):
                e.wait_ge(ph, PH[0])
                e.dma_start(out=maskr_bf[:], in_=maskr_d).then_inc(pre_w, 16)
                e.dma_start(out=gamp_bf[:], in_=gampair_d).then_inc(pre_w, 16)
                e.dma_start(out=kgam_bf[:], in_=kgam_d).then_inc(pre_w, 16)
                if SPARSE:
                    e.wait_ge(ppf, 16 * NE)
                    e.wait_ge(v_pb, 1)
                    for j in range(NTI * NE):
                        ti, ee = j // NE, j % NE
                        e.indirect_dma_start(
                            out=pair_hbm.rearrange("e r c -> (e r) c"),
                            out_offset=IndirectOffsetOnAxis(offi[:, ti, ee:ee + 1], 0),
                            in_=pairs_all[:, ti, ee, :], in_offset=None,
                            element_offset=ee * (T + 128) * 2,
                        ).then_inc(g_scp2[j % 2], 16)
                    e.wait_ge(g_scp2[0], 16 * NTI * NE // 2)
                    e.wait_ge(g_scp2[1], 16 * NTI * NE // 2)
                e.wait_ge(pre_w, 96)
                e.nop().then_inc(ph, 1)

            @b2blk.sync
            def _(e):
                e.wait_ge(ph, PH[0])
                for s in range(2):
                    e.dma_start(out=xnTs[s][:],
                                in_=xnt_hbm[:, :, s * SLAB:(s + 1) * SLAB]).then_inc(ldxt2[s % 2], 16)
                    for k in range(4):
                        e.dma_start(out=tab_sb[s][:, k, :],
                                    in_=TABS_D[k][:, s * SLAB:(s + 1) * SLAB]).then_inc(tab2[s % 2], 16)
                for s in range(NSLAB):
                    e.wait_ge(s_ye, IPS * (s + 1))
                    e.dma_start(out=yT_hbm[:, :, s * SLAB:(s + 1) * SLAB],
                                in_=yT_sl[s % 2][:]).then_inc(sty2[s % 2], 16)
                    if s + 2 < NSLAB:
                        e.wait_ge(t_qkv, (s + 1) * NG)
                        e.dma_start(out=xnTs[s % 2][:],
                                    in_=xnt_hbm[:, :, (s + 2) * SLAB:(s + 3) * SLAB]).then_inc(ldxt2[s % 2], 16)
                        e.wait_ge(v_qkc, (s + 1) * 16)
                        for k in range(4):
                            e.dma_start(out=tab_sb[s % 2][:, k, :],
                                        in_=TABS_D[k][:, (s + 2) * SLAB:(s + 3) * SLAB]).then_inc(tab2[s % 2], 16)
                e.wait_ge(sty2[0], 16 * (NSLAB // 2))
                e.wait_ge(sty2[1], 16 * (NSLAB // 2))
                if "ph2" in DEBUG:
                    e.wait_ge(v_st, NSLAB)
                    e.dma_start(out=dbg_st[:, 0:NJ], in_=S1acc[:]).then_inc(sty2[0], 16)
                    e.dma_start(out=dbg_st[:, NJ:2 * NJ], in_=S2acc[:]).then_inc(sty2[1], 16)
                    e.wait_ge(sty2[0], 16 * (NSLAB // 2) + 16)
                    e.wait_ge(sty2[1], 16 * (NSLAB // 2) + 16)
                e.nop().then_inc(ph, 1)

        ctxp2.close()
        ctx2.close()
        PH[0] += 5
    ctx12.close()

    if MAX_PHASE >= 3 and SPARSE:
        w1g4 = [S(f"w1g{i}") for i in range(4)]
        w2g4 = [S(f"w2g{i}") for i in range(4)]
        t_h = S("t_h"); s_g = S("s_g")
        t_y3 = S("t_y3"); v_e3 = S("v_e3")
        ldpr2 = [S("ld_pr0"), S("ld_pr1")]
        v_ix = S("v_ix"); g_g2 = [S("g_g0"), S("g_g1")]
        t_tp = S("t_tp"); s_tpe = S("s_tpe")
        g_sc = S("g_sc")

        ctx3 = ExitStack()
        sb3 = lambda name, shape, dt: ctx3.enter_context(nc.sbuf_tensor(name, shape, dt))
        W1sb = sb3("W1sb", [128, NJ, DF], BF16)
        W2sb = sb3("W2sb", [128, NFT, C], BF16)
        hTg = [sb3(f"hTg{i}", [128, TB], BF16) for i in range(2)]
        stage = [sb3(f"stage{i}", [128, 2, C], F16) for i in range(2)]
        pairs_sb = [sb3(f"pairs{i}", [128, UPE, 2], F32) for i in range(2)]
        idxi_sb = [sb3(f"idxi{i}", [128, UPE], I32) for i in range(2)]
        xg = [sb3(f"xg{i}", [128, C], BF16) for i in range(2)]
        xgT = [sb3(f"xgT{i}", [128, NJ, TB], BF16) for i in range(2)]

        ctxp3 = ExitStack()
        pt3 = lambda name, shape, dt: ctxp3.enter_context(nc.psum_tensor(name, shape, dt))
        ps_h = [pt3(f"ps_h{i}", [128, TB], F32) for i in range(2)]
        ps_y3 = [pt3(f"ps_y3{i}", [128, C], F32) for i in range(2)]
        ps_tp = [pt3(f"ps_tp{i}", [128, 128], BF16) for i in range(2)]

        with nc.Block() as b3blk:
            @b3blk.tensor
            def _(e):
                e.wait_ge(ph, PH[0])

                def y_mms(ee, k, bg_, ft):
                    idx = bg_ * NFT + ft
                    e.wait_ge(s_g, idx + 1)
                    if ft == 0:
                        W(e, v_e3, bg_)
                        if k == 0:
                            e.wait_ge(w2g4[0], (ee + 1) * 32)
                    elif k == 0 and ft % 4 == 0:
                        e.wait_ge(w2g4[(ft // 4) // 2], (ee + 1) * 32)
                    im = None
                    for a in range(2):
                        for dmh in range(2):
                            im = e.matmul(ps_y3[a][:, dmh * 512:(dmh + 1) * 512],
                                          hTg[ft % 2][:, a * 128:(a + 1) * 128],
                                          W2sb[:, ft, dmh * 512:(dmh + 1) * 512],
                                          start=(ft == 0), stop=(ft == NFT - 1))
                    return im

                for ee in range(NE):
                    for k in range(NB):
                        bg_ = ee * NB + k
                        for uu in range(2):
                            u = 2 * k + uu
                            ug = ee * UPE + u
                            e.wait_ge(g_g2[ug % 2], 16 * (ug // 2 + 1))
                            if bg_ >= 2 and uu == 0:
                                e.wait_ge(s_g, (bg_ - 1) * NFT)
                            for c in range(NJ):
                                tc = ug * NJ + c
                                W(e, s_tpe, tc - 1)
                                e.transpose(ps_tp[tc % 2][:],
                                            xg[u % 2][:, c * 128:(c + 1) * 128],
                                            identb[:]).then_inc(t_tp, 1)
                        for ft in range(NFT):
                            idx = bg_ * NFT + ft
                            if k == 0:
                                e.wait_ge(w1g4[(ft // 4) // 2], (ee + 1) * 32)
                            W(e, s_g, idx - 1)
                            if ft == 0:
                                e.wait_ge(s_tpe, 16 * (bg_ + 1))
                            for c in range(NJ):
                                im = e.matmul(ps_h[ft % 2][:],
                                              W1sb[:, c, ft * 128:(ft + 1) * 128],
                                              xgT[k % 2][:, c, :], start=(c == 0),
                                              stop=(c == NJ - 1))
                            im.then_inc(t_h, 1)
                            if ft >= 1:
                                y_mms(ee, k, bg_, ft - 1)
                        y_mms(ee, k, bg_, NFT - 1).then_inc(t_y3, 1)
                e.wait_ge(v_e3, NBG)
                e.nop().then_inc(ph, 1)

            @b3blk.scalar
            def _(e):
                e.wait_ge(ph, PH[0])
                for ee in range(NE):
                    for k in range(NB):
                        bg_ = ee * NB + k
                        for uu in range(2):
                            u = 2 * k + uu
                            ug = ee * UPE + u
                            for c in range(NJ):
                                tc = ug * NJ + c
                                e.wait_ge(t_tp, tc + 1)
                                e.activation(xgT[k % 2][:, c, uu * 128:(uu + 1) * 128],
                                             ps_tp[tc % 2][:],
                                             AF.Copy).then_inc(s_tpe, 1)
                        for ft in range(NFT):
                            idx = bg_ * NFT + ft
                            e.wait_ge(t_h, idx + 1)
                            e.activation(hTg[ft % 2][:], ps_h[ft % 2][:], AF.Gelu,
                                         bias=b1_sb[:, ee, ft:ft + 1]).then_inc(s_g, 1)
                e.wait_ge(t_y3, NBG)
                e.nop().then_inc(ph, 1)

            @b3blk.vector
            def _(e):
                e.wait_ge(ph, PH[0])
                for ee in range(NE):
                    e.wait_ge(ldpr2[ee % 2], 16 * (ee // 2 + 1))
                    if ee >= 2:
                        e.wait_ge(g_g2[0], 16 * (ee - 1) * UPE // 2)
                        e.wait_ge(g_g2[1], 16 * (ee - 1) * UPE // 2)
                        e.wait_ge(g_sc, 32 * (ee - 1) * NB)
                    VC(e, e.tensor_copy(idxi_sb[ee % 2][:], pairs_sb[ee % 2][:, :, 0]))
                    e.nop().then_inc(v_ix, 1)
                    for k in range(NB):
                        bg_ = ee * NB + k
                        e.wait_ge(t_y3, bg_ + 1)
                        W(e, g_sc, 32 * (bg_ - 1))
                        ii_ = None
                        for a in range(2):
                            for dmh in range(2):
                                ii_ = e.tensor_scalar(
                                    stage[bg_ % 2][:, a, dmh * 512:(dmh + 1) * 512],
                                    ps_y3[a][:, dmh * 512:(dmh + 1) * 512],
                                    pairs_sb[ee % 2][:, 2 * k + a, 1:2], None, ALU.mult)
                        ii_.then_inc(v_e3, 1)
                e.nop().then_inc(ph, 1)

            @b3blk.gpsimd
            def _(e):
                e.wait_ge(ph, PH[0])

                def loadW(ee):
                    for gi in range(8):
                        e.dma_start(
                            out=W1sb[:, :, gi * 512:(gi + 1) * 512],
                            in_=W1_d[ee].rearrange("(a p) f -> p a f", p=128)[:, :, gi * 512:(gi + 1) * 512],
                        ).then_inc(w1g4[gi // 2], 16)
                        e.dma_start(
                            out=W2sb[:, gi * 4:(gi + 1) * 4, :],
                            in_=W2_d[ee].rearrange("(g p) d -> p g d", p=128)[:, gi * 4:(gi + 1) * 4, :],
                        ).then_inc(w2g4[gi // 2], 16)

                loadW(0)
                events = []
                for ee in range(NE):
                    for k in range(NB):
                        events.append(("g", ee, k))
                ev2 = []
                for idx3, (kind, ee, k) in enumerate(events):
                    ev2.append(("gather", ee, k))
                    if k == 0 and ee > 0:
                        ev2.append(("loadW", ee, 0))
                    if idx3 >= 2:
                        pee, pk = events[idx3 - 2][1], events[idx3 - 2][2]
                        ev2.append(("scatter", pee, pk))
                ev2.append(("scatter", NE - 1, NB - 2))
                ev2.append(("scatter", NE - 1, NB - 1))
                for kind, ee, k in ev2:
                    bg_ = ee * NB + k
                    if kind == "gather":
                        e.wait_ge(v_ix, ee + 1)
                        for uu in range(2):
                            u = 2 * k + uu
                            ug = ee * UPE + u
                            if ug >= 2:
                                e.wait_ge(t_tp, NJ * (ug - 1))
                            e.indirect_dma_start(
                                out=xg[u % 2][:], out_offset=None,
                                in_=xn_hbm,
                                in_offset=IndirectOffsetOnAxis(
                                    idxi_sb[ee % 2][:, u:u + 1], 0),
                            ).then_inc(g_g2[ug % 2], 16)
                    elif kind == "scatter":
                        e.wait_ge(v_e3, bg_ + 1)
                        for a in range(2):
                            W(e, g_sc, 16 * (2 * bg_ + a))
                            e.indirect_dma_start(
                                out=acc_hbm,
                                out_offset=IndirectOffsetOnAxis(
                                    idxi_sb[ee % 2][:, 2 * k + a:2 * k + a + 1], 0),
                                in_=stage[bg_ % 2][:, a, :], in_offset=None,
                                compute_op=ALU.add,
                            ).then_inc(g_sc, 16)
                    else:
                        e.wait_ge(t_h, ee * NB * NFT)
                        e.wait_ge(t_y3, ee * NB)
                        loadW(ee)
                e.wait_ge(g_sc, 16 * 2 * NBG)
                e.nop().then_inc(ph, 1)

            @b3blk.sync
            def _(e):
                e.wait_ge(ph, PH[0])
                for ee in range(NE):
                    W(e, v_ix, ee - 1)
                    W(e, v_e3, (ee - 1) * NB)
                    e.dma_start(
                        out=pairs_sb[ee % 2][:],
                        in_=pair_hbm[ee].rearrange("(a p) c -> p a c", p=128)[:, 0:UPE, :],
                    ).then_inc(ldpr2[ee % 2], 16)
                e.nop().then_inc(ph, 1)

        ctxp3.close()
        ctx3.close()
        PH[0] += 5

    if MAX_PHASE >= 3 and not SPARSE:
        w1g8 = [S(f"w1g{i}") for i in range(8)]
        w2g8 = [S(f"w2g{i}") for i in range(8)]
        t_h = S("t_h"); s_g = S("s_g")
        t_y3 = S("t_y3"); v_e3 = S("v_e3"); st_c = S("st_c")

        ctx3 = ExitStack()
        sb3 = lambda name, shape, dt: ctx3.enter_context(nc.sbuf_tensor(name, shape, dt))
        W1sb = sb3("W1sb", [128, NJ, DF], BF16)
        W2sb = sb3("W2sb", [128, NFT, C], BF16)
        hTg = [sb3(f"hTg{i}", [128, TB], BF16) for i in range(2)]
        stage = sb3("stage", [128, 2, C], F16)

        ctxp3 = ExitStack()
        pt3 = lambda name, shape, dt: ctxp3.enter_context(nc.psum_tensor(name, shape, dt))
        ps_h = [pt3(f"ps_h{i}", [128, TB], F32) for i in range(2)]
        ps_y3 = [pt3(f"ps_y3{i}", [128, C], F32) for i in range(2)]

        with nc.Block() as b3blk:
            @b3blk.tensor
            def _(e):
                e.wait_ge(ph, PH[0])
                def y_mms(ee, tb, tbg, ft):
                    idx = tbg * NFT + ft
                    e.wait_ge(s_g, idx + 1)
                    if ft == 0:
                        W(e, v_e3, tbg)
                        if tb == 0:
                            e.wait_ge(w2g8[0], (ee + 1) * 16)
                    elif tb == 0 and ft % 4 == 0:
                        e.wait_ge(w2g8[ft // 4], (ee + 1) * 16)
                    im = None
                    for a in range(2):
                        for dmh in range(2):
                            im = e.matmul(ps_y3[a][:, dmh * 512:(dmh + 1) * 512],
                                          hTg[ft % 2][:, a * 128:(a + 1) * 128],
                                          W2sb[:, ft, dmh * 512:(dmh + 1) * 512],
                                          start=(ft == 0), stop=(ft == NFT - 1))
                    return im

                for ee in range(NE):
                    for tb in range(NTB):
                        tbg = ee * NTB + tb
                        tcols = slice(tb * TB, (tb + 1) * TB)
                        for ft in range(NFT):
                            idx = tbg * NFT + ft
                            if tb == 0:
                                e.wait_ge(w1g8[ft // 4], (ee + 1) * 16)
                            W(e, s_g, idx - 1)
                            for c in range(NJ):
                                im = e.matmul(ps_h[ft % 2][:], W1sb[:, c, ft * 128:(ft + 1) * 128],
                                              xnT[:, c, tcols], start=(c == 0),
                                              stop=(c == NJ - 1))
                            im.then_inc(t_h, 1)
                            if ft >= 1:
                                y_mms(ee, tb, tbg, ft - 1)
                        y_mms(ee, tb, tbg, NFT - 1).then_inc(t_y3, 1)
                e.wait_ge(v_e3, NTBG)
                e.nop().then_inc(ph, 1)

            @b3blk.scalar
            def _(e):
                e.wait_ge(ph, PH[0])
                for ee in range(NE):
                    for tb in range(NTB):
                        for ft in range(NFT):
                            idx = (ee * NTB + tb) * NFT + ft
                            e.wait_ge(t_h, idx + 1)
                            e.activation(hTg[ft % 2][:], ps_h[ft % 2][:], AF.Gelu,
                                         bias=b1_sb[:, ee, ft:ft + 1]).then_inc(s_g, 1)
                e.wait_ge(t_y3, NTBG)
                e.nop().then_inc(ph, 1)

            @b3blk.vector
            def _(e):
                e.wait_ge(ph, PH[0])
                for ee in range(NE):
                    for tb in range(NTB):
                        tbg = ee * NTB + tb
                        e.wait_ge(t_y3, tbg + 1)
                        W(e, st_c, 16 * tbg)
                        ii_ = None
                        for a in range(2):
                            for dmh in range(2):
                                ii_ = e.tensor_scalar(
                                    stage[:, a, dmh * 512:(dmh + 1) * 512],
                                    ps_y3[a][:, dmh * 512:(dmh + 1) * 512],
                                    w_nat[:, tb * 2 + a, ee:ee + 1], None, ALU.mult)
                        ii_.then_inc(v_e3, 1)
                e.wait_ge(st_c, 16 * NTBG)
                e.nop().then_inc(ph, 1)

            @b3blk.gpsimd
            def _(e):
                e.wait_ge(ph, PH[0])

                def loadW(ee):
                    for gi in range(8):
                        e.dma_start(
                            out=W1sb[:, :, gi * 512:(gi + 1) * 512],
                            in_=W1_d[ee].rearrange("(a p) f -> p a f", p=128)[:, :, gi * 512:(gi + 1) * 512],
                        ).then_inc(w1g4[gi // 2], 16)
                        e.dma_start(
                            out=W2sb[:, gi * 4:(gi + 1) * 4, :],
                            in_=W2_d[ee].rearrange("(g p) d -> p g d", p=128)[:, gi * 4:(gi + 1) * 4, :],
                        ).then_inc(w2g4[gi // 2], 16)

                loadW(0)
                for ee in range(NE):
                    for tb in range(NTB):
                        tbg = ee * NTB + tb
                        e.wait_ge(v_e3, tbg + 1)
                        e.dma_start(
                            out=acc_hbm[tb * TB:(tb + 1) * TB, :].rearrange("(a p) d -> p a d", p=128),
                            in_=stage[:],
                            accum_op=(ALU.add if ee > 0 else ALU.bypass),
                        ).then_inc(st_c, 16)
                    if ee + 1 < NE:
                        e.wait_ge(t_h, (ee + 1) * NTB * NFT)
                        e.wait_ge(t_y3, (ee + 1) * NTB)
                        loadW(ee + 1)
                e.wait_ge(st_c, 16 * NTBG)
                e.nop().then_inc(ph, 1)

        ctxp3.close()
        ctx3.close()
        PH[0] += 4

    if MAX_PHASE >= 4:
        g4w = S("g4w"); ld4 = S("ld4"); t_s4 = S("t_s4")
        s_st4 = S("s_st4"); v_m4 = S("v_m4")
        ld_xnt = S("ld_xnt")
        ldy2 = [S("ld_y40"), S("ld_y41")]
        t_g4 = S("t_g4"); s_si = S("s_si"); v_gy = S("v_gy")
        t_pr = S("t_pr")
        ldxa2 = [S("ld_xa0"), S("ld_xa1")]
        v_mrg = S("v_mrg")
        sto2 = [S("st_o0"), S("st_o1")]

        ctx4 = ExitStack()
        sb4 = lambda name, shape, dt: ctx4.enter_context(nc.sbuf_tensor(name, shape, dt))
        if SPARSE:
            xnT = sb4("xnT4", [128, NJ, T], BF16)
        wgT_bf = sb4("wgT_bf", [128, NJ, C], BF16)
        wpT_bf = sb4("wpT_bf", [128, NJ, C], BF16)
        yt4 = [sb4(f"yt4_{i}", [128, NJ, SLAB], BF16) for i in range(2)]
        gy = [sb4(f"gy{i}", [128, NJ, SLAB], BF16) for i in range(2)]
        sil = [sb4(f"sil{i}", [128, SLAB], BF16) for i in range(2)]
        tmp_yn = sb4("tmp_yn", [128, SLAB], BF16)
        xt4 = [sb4(f"xt4_{i}", [128, C], F32) for i in range(2)]
        at4 = [sb4(f"at4_{i}", [128, C], F16) for i in range(2)]
        ot4 = [sb4(f"ot4_{i}", [128, C], F32) for i in range(2)]
        wT4 = sb4("wT4", [NE, T], BF16)
        b2f = sb4("b2f", [NE, C], F32)
        b2b = sb4("b2b", [NE, C], BF16)
        gnw_sb = sb4("gnw_sb", [128, NJ], F32)
        gnb_sb = sb4("gnb_sb", [128, NJ], F32)
        h2_sb = sb4("h2_sb", [128, 2], F32)
        g2_sb = sb4("g2_sb", [2, 128], F32)
        sums4 = sb4("sums4", [2, 2 * NJ], F32)
        mh4 = sb4("mh4", [2, NJ], F32)
        rstd4 = sb4("rstd4", [2, NJ], F32)
        mr4 = sb4("mr4", [2, NJ], F32)
        alpha = sb4("alpha", [128, NJ], F32)
        beta = sb4("beta", [128, NJ], F32)
        btmp = sb4("btmp", [128, NJ], F32)

        ctxp4 = ExitStack()
        pt4 = lambda name, shape, dt: ctxp4.enter_context(nc.psum_tensor(name, shape, dt))
        ps_st4 = pt4("ps_st4", [128, 2 * NJ], F32)
        ps_bc = pt4("ps_bc", [128, 2 * NJ], F32)
        ps_g4 = [pt4(f"ps_g4{i}", [128, SLAB], F32) for i in range(2)]
        ps_o = [pt4(f"ps_o{i}", [128, C], F32) for i in range(2)]

        def emit_proj(e, ti):
            s = ti // TPS
            e.wait_ge(v_gy, NJ * (s + 1))
            W(e, v_mrg, ti - 1)
            tcols = slice((ti % TPS) * 128, (ti % TPS) * 128 + 128)
            for dmh in range(2):
                for c in range(NJ):
                    e.matmul(ps_o[ti % 2][:, dmh * 512:(dmh + 1) * 512],
                             gy[s % 2][:, c, tcols],
                             wpT_bf[:, c, dmh * 512:(dmh + 1) * 512],
                             start=(c == 0), stop=False)
                im = e.matmul(ps_o[ti % 2][:, dmh * 512:(dmh + 1) * 512],
                              wT4[:, ti * 128:(ti + 1) * 128],
                              b2b[:, dmh * 512:(dmh + 1) * 512],
                              start=False, stop=True)
            im.then_inc(t_pr, 1)

        with nc.Block() as b4blk:
            @b4blk.tensor
            def _(e):
                e.wait_ge(ph, PH[0])
                e.wait_ge(ld4, 16 * 6)
                e.matmul(ps_st4[0:2, 0:NJ], h2_sb[:], S1acc[:], start=True, stop=True)
                e.matmul(ps_st4[0:2, NJ:2 * NJ], h2_sb[:], S2acc[:], start=True,
                         stop=True).then_inc(t_s4, 1)
                e.wait_ge(v_m4, 2)
                e.matmul(ps_bc[:, 0:NJ], g2_sb[:], rstd4[:], start=True, stop=True)
                e.matmul(ps_bc[:, NJ:2 * NJ], g2_sb[:], mr4[:], start=True,
                         stop=True).then_inc(t_s4, 1)
                e.wait_ge(g4w, 32)
                if SPARSE:
                    e.wait_ge(ld_xnt, 16)
                for s in range(NSLAB):
                    cols = slice(s * SLAB, (s + 1) * SLAB)
                    for j in range(NJ):
                        gj = s * NJ + j
                        W(e, s_si, gj - 1)
                        for c in range(NJ):
                            im = e.matmul(ps_g4[gj % 2][:], wgT_bf[:, c, j * 128:(j + 1) * 128],
                                          xnT[:, c, cols], start=(c == 0), stop=(c == NJ - 1))
                        im.then_inc(t_g4, 1)
                    if s >= 1:
                        for ts_ in range(TPS):
                            ti = (s - 1) * TPS + ts_
                            emit_proj(e, ti)
                for ts_ in range(TPS):
                    emit_proj(e, (NSLAB - 1) * TPS + ts_)
                e.nop().then_inc(ph, 1)

            @b4blk.scalar
            def _(e):
                e.wait_ge(ph, PH[0])
                e.wait_ge(v_m4, 1)
                SC(e, e.activation(rstd4[:], sums4[0:2, NJ:2 * NJ], AF.Ln))
                e.activation(rstd4[:], rstd4[:], AF.Exp, scale=-0.5).then_inc(s_st4, 1)
                for s in range(NSLAB):
                    for j in range(NJ):
                        gj = s * NJ + j
                        e.wait_ge(t_g4, gj + 1)
                        W(e, v_gy, gj - 1)
                        e.activation(sil[gj % 2][:], ps_g4[gj % 2][:],
                                     AF.Silu).then_inc(s_si, 1)
                e.nop().then_inc(ph, 1)

            @b4blk.vector
            def _(e):
                e.wait_ge(ph, PH[0])
                e.wait_ge(ld4, 16 * 6)
                e.tensor_copy(b2b[:], b2f[:])
                e.wait_ge(t_s4, 1)
                VC(e, e.tensor_copy(sums4[:], ps_st4[0:2, :]))
                VC(e, e.tensor_scalar_mul(mh4[:], sums4[0:2, 0:NJ], 1.0 / (64.0 * T)))
                VC(e, e.tensor_mul(rstd4[:], mh4[:], mh4[:]))
                VC(e, e.tensor_scalar_mul(sums4[0:2, NJ:2 * NJ], sums4[0:2, NJ:2 * NJ],
                                          1.0 / (64.0 * T)))
                VC(e, e.tensor_sub(sums4[0:2, NJ:2 * NJ], sums4[0:2, NJ:2 * NJ], rstd4[:]))
                VC(e, e.tensor_scalar_add(sums4[0:2, NJ:2 * NJ], sums4[0:2, NJ:2 * NJ],
                                          1e-5))
                e.nop().then_inc(v_m4, 1)
                e.wait_ge(s_st4, 1)
                VC(e, e.tensor_mul(mr4[:], mh4[:], rstd4[:]))
                e.nop().then_inc(v_m4, 1)
                e.wait_ge(t_s4, 2)
                VC(e, e.tensor_mul(alpha[:], ps_bc[:, 0:NJ], gnw_sb[:]))
                VC(e, e.tensor_mul(btmp[:], ps_bc[:, NJ:2 * NJ], gnw_sb[:]))
                VC(e, e.tensor_sub(beta[:], gnb_sb[:], btmp[:]))
                e.nop().then_inc(v_m4, 1)
                def emit_gy(s):
                    for j in range(NJ):
                        gj = s * NJ + j
                        e.wait_ge(ldy2[s % 2], 16 * (s // 2 + 1))
                        e.wait_ge(s_si, gj + 1)
                        W(e, v_gy, gj)
                        VC(e, e.tensor_scalar(tmp_yn[:], yt4[s % 2][:, j, :],
                                              alpha[:, j:j + 1], beta[:, j:j + 1],
                                              ALU.mult, ALU.add))
                        e.tensor_mul(gy[s % 2][:, j, :], sil[gj % 2][:],
                                     tmp_yn[:]).then_inc(v_gy, 1)

                def emit_merge(sm):
                    for ts_ in range(TPS):
                        ti = sm * TPS + ts_
                        e.wait_ge(t_pr, ti + 1)
                        e.wait_ge(ldxa2[ti % 2], 32 * (ti // 2 + 1))
                        W(e, sto2[ti % 2], 16 * (ti // 2))
                        for dmh in range(2):
                            ia = e.tensor_add(ot4[ti % 2][:, dmh * 512:(dmh + 1) * 512],
                                              ps_o[ti % 2][:, dmh * 512:(dmh + 1) * 512],
                                              xt4[ti % 2][:, dmh * 512:(dmh + 1) * 512])
                        VC(e, ia)
                        e.tensor_add(ot4[ti % 2][:], ot4[ti % 2][:],
                                     at4[ti % 2][:]).then_inc(v_mrg, 1)

                emit_gy(0)
                for s in range(1, NSLAB + 1):
                    if s < NSLAB:
                        emit_gy(s)
                    emit_merge(s - 1)
                e.nop().then_inc(ph, 1)

            @b4blk.gpsimd
            def _(e):
                e.wait_ge(ph, PH[0])
                e.dma_start(out=wgT_bf[:],
                            in_=wgT_d.rearrange("(a p) j -> p a j", p=128)).then_inc(g4w, 16)
                e.dma_start(out=wpT_bf[:],
                            in_=wpT_d.rearrange("(a p) j -> p a j", p=128)).then_inc(g4w, 16)

            @b4blk.sync
            def _(e):
                e.wait_ge(ph, PH[0])
                if SPARSE:
                    e.dma_start(out=xnT[:], in_=xnt_hbm).then_inc(ld_xnt, 16)
                e.dma_start(out=wT4[:], in_=wt_hbm).then_inc(ld4, 16)
                e.dma_start(out=b2f[:], in_=b2_d).then_inc(ld4, 16)
                e.dma_start(out=gnw_sb[:], in_=gnw_d).then_inc(ld4, 16)
                e.dma_start(out=gnb_sb[:], in_=gnb_d).then_inc(ld4, 16)
                e.dma_start(out=h2_sb[:], in_=h2_d).then_inc(ld4, 16)
                e.dma_start(out=g2_sb[:], in_=g2_d).then_inc(ld4, 16)
                for s in range(2):
                    e.dma_start(out=yt4[s][:],
                                in_=yT_hbm[:, :, s * SLAB:(s + 1) * SLAB]).then_inc(ldy2[s % 2], 16)
                for ti in range(2):
                    e.dma_start(out=xt4[ti][:],
                                in_=x_d[ti * 128:(ti + 1) * 128, :]).then_inc(ldxa2[ti % 2], 16)
                    e.dma_start(out=at4[ti][:],
                                in_=acc_hbm[ti * 128:(ti + 1) * 128, :]).then_inc(ldxa2[ti % 2], 16)
                for ti in range(NTI):
                    e.wait_ge(v_mrg, ti + 1)
                    e.dma_start(out=out_d[ti * 128:(ti + 1) * 128, :],
                                in_=ot4[ti % 2][:]).then_inc(sto2[ti % 2], 16)
                    if ti + 2 < NTI:
                        e.dma_start(out=xt4[ti % 2][:],
                                    in_=x_d[(ti + 2) * 128:(ti + 3) * 128, :]).then_inc(ldxa2[ti % 2], 16)
                        e.dma_start(out=at4[ti % 2][:],
                                    in_=acc_hbm[(ti + 2) * 128:(ti + 3) * 128, :]).then_inc(ldxa2[ti % 2], 16)
                    if ti % TPS == TPS - 1:
                        s = ti // TPS
                        if s + 2 < NSLAB:
                            e.wait_ge(v_gy, NJ * (s + 1))
                            e.dma_start(out=yt4[s % 2][:],
                                        in_=yT_hbm[:, :, (s + 2) * SLAB:(s + 3) * SLAB]).then_inc(ldy2[s % 2], 16)
                e.wait_ge(sto2[0], 16 * (NTI // 2))
                e.wait_ge(sto2[1], 16 * (NTI // 2))

        ctxp4.close()
        ctx4.close()

    ctx.close()
    return nc


def kernel(**inputs):
    tabs = host_tables()
    x = np.ascontiguousarray(np.asarray(inputs["L0_attn_x"], np.float32))
    w_qkv = np.asarray(inputs["L0_attn_w_qkv"], np.float32)
    w_gated = np.asarray(inputs["L0_attn_w_gated"], np.float32)
    w_proj = np.asarray(inputs["L0_attn_w_proj"], np.float32)
    gnw = np.asarray(inputs["L0_attn_gn_weight"], np.float32)
    gnb = np.asarray(inputs["L0_attn_gn_bias"], np.float32)
    Wg = np.ascontiguousarray(np.asarray(inputs["L0_ffn_Wg"], np.float32))
    bg = np.asarray(inputs["L0_ffn_bg"], np.float32)
    W1 = np.ascontiguousarray(np.asarray(inputs["L0_ffn_W1"], np.float32))
    b1 = np.asarray(inputs["L0_ffn_b1"], np.float32)
    W2 = np.ascontiguousarray(np.asarray(inputs["L0_ffn_W2"], np.float32))
    b2 = np.ascontiguousarray(np.asarray(inputs["L0_ffn_b2"], np.float32))

    common = {
        "wqkvT": np.ascontiguousarray(w_qkv.T),
        "wgT": np.ascontiguousarray(w_gated.T),
        "wpT": np.ascontiguousarray(w_proj.T),
        "gnw_r": np.ascontiguousarray(gnw.reshape(NJ, 128).T),
        "gnb_r": np.ascontiguousarray(gnb.reshape(NJ, 128).T),
        "Wg": Wg,
        "bg": np.ascontiguousarray(bg.reshape(1, NE)),
        "b1r": np.ascontiguousarray(b1.reshape(NE, NFT, 128).transpose(0, 2, 1)),
        "b2": b2,
        "W1": W1,
        "W2": W2,
        "csq": tabs["csq"], "ssq": tabs["ssq"],
        "csk": tabs["csk"], "ssk": tabs["ssk"],
        "maskr": tabs["maskr"], "gampair": tabs["gampair"],
        "kgampair": tabs["kgampair"], "p2t": tabs["p2t"],
        "ident": tabs["ident"], "h2": tabs["h2"], "g2": tabs["g2"],
        "idcol": tabs["idcol"],
    }
    nc = build_program(tabs["gam_c"])
    in_maps = [dict(common, x=np.ascontiguousarray(x[bb])) for bb in range(8)]
    res = run_bass_kernel_spmd(nc, in_maps, core_ids=list(range(8)))
    if DEBUG:
        return res
    return np.stack([np.asarray(r["out"]) for r in res.results], axis=0)


# version 08 (final)
# speedup vs baseline: 1.1498x; 1.0518x over previous
import numpy as np
import ml_dtypes
import concourse.bass as bass
import concourse.mybir as mybir
from concourse.bass_utils import run_bass_kernel_spmd
from concourse.bass_core_rust import IndirectOffsetOnAxis
from contextlib import ExitStack

F32 = mybir.dt.float32
BF16 = mybir.dt.bfloat16
F16 = mybir.dt.float16
I32 = mybir.dt.int32
AF = mybir.ActivationFunctionType
ALU = mybir.AluOpType
AX = mybir.AxisListType

T, C, H, D = 4096, 1024, 16, 64
NJ = 8
CH = 128
SLAB = 512
NSLAB = T // SLAB
CPS = SLAB // CH
NPAIR = NJ
NI = NSLAB * CPS * NPAIR
IPS = CPS * NPAIR
TPS = SLAB // 128
NE, DF = 8, 4096
NFT = DF // 128
TB = 256
NTB = T // TB
NTBG = NE * NTB
NTI = T // 128

DEBUG = set()
MAX_PHASE = 4
SPARSE = True
NB = 6
UPE = 2 * NB
NBG = NE * NB
PADPOS = T
RET_MODE = 0


def host_tables():
    half = np.arange(0, D, 2, dtype=np.float64)
    inv_freq = 1.0 / (10000.0 ** (half / D))
    t = np.arange(T, dtype=np.float64)
    freqs = np.repeat(t[:, None] * inv_freq[None, :], 2, axis=-1)
    scale_vec = (half + 0.4 * D) / (1.4 * D)
    power = (t - T // 2) / 512.0
    scale = np.repeat(scale_vec[None, :] ** power[:, None], 2, axis=-1)
    cos, sin = np.cos(freqs), np.sin(freqs)

    def dup(a):
        a = np.ascontiguousarray(a.T.astype(np.float32))
        return np.concatenate([a, a], axis=0)

    tabs = {"csq": dup(cos * scale), "ssq": dup(sin * scale),
            "csk": dup(cos / scale), "ssk": dup(sin / scale)}
    for _n in ("csq", "ssq", "csk", "ssk"):
        tabs[_n + "b"] = tabs[_n].astype(ml_dtypes.bfloat16)
    gamma = 1.0 - 2.0 ** (-5.0 - np.arange(H, dtype=np.float64))
    ii = np.arange(CH)
    diff = ii[None, :] - ii[:, None]
    m = np.where(diff >= 0, gamma[:, None, None] ** np.maximum(diff, 0)[None], 0.0)
    m = m * (D ** -0.5)
    tabs["maskr"] = np.ascontiguousarray(m.transpose(1, 0, 2).astype(np.float32))
    gp = np.zeros((128, NJ, CH), np.float64)
    kg = np.zeros((128, NJ, 128), np.float64)
    for p in range(NJ):
        for par in range(2):
            gp[par * 64:(par + 1) * 64, p, :] = gamma[2 * p + par] ** (ii + 1.0) * (D ** -0.5)
            kg[:, p, par * 64:(par + 1) * 64] = (gamma[2 * p + par] ** (CH - 1.0 - ii))[:, None]
    tabs["gampair"] = gp.astype(np.float32)
    tabs["kgampair"] = kg.astype(np.float32)
    tabs["gam_c"] = [float(g ** CH) for g in gamma]
    P = np.zeros((64, 64), np.float32)
    for i in range(32):
        P[2 * i, 2 * i + 1] = -1.0
        P[2 * i + 1, 2 * i] = 1.0
    P2 = np.zeros((128, 128), np.float32)
    P2[0:64, 0:64] = P
    P2[64:128, 64:128] = P
    tabs["p2t"] = np.ascontiguousarray(P2.T)
    tabs["ident"] = np.eye(128, dtype=np.float32)
    h2 = np.zeros((128, 2), np.float32)
    h2[0:64, 0] = 1.0
    h2[64:128, 1] = 1.0
    tabs["h2"] = h2
    tabs["g2"] = np.ascontiguousarray(h2.T)
    idc = np.zeros((128, T // 128), np.float32)
    for ti in range(T // 128):
        idc[:, ti] = ti * 128 + np.arange(128)
    tabs["idcol"] = idc
    return tabs


def build_program(gam_c):
    nc = bass.Bass("TRN2", target_bir_lowering=False, debug=False)

    def din(name, shape):
        return nc.dram_tensor(name, shape, F32, kind="ExternalInput").ap()

    x_d = din("x", [T, C])
    wqkvT_d = din("wqkvT", [C, 3 * C])
    wgT_d = din("wgT", [C, C])
    wpT_d = din("wpT", [C, C])
    gnw_d = din("gnw_r", [128, NJ])
    gnb_d = din("gnb_r", [128, NJ])
    Wg_d = din("Wg", [C, NE])
    bg_d = din("bg", [1, NE])
    b1_d = din("b1r", [NE, 128, NFT])
    b2_d = din("b2", [NE, C])
    W1_d = din("W1", [NE, C, DF])
    W2_d = din("W2", [NE, DF, C])
    def dinb(name, shape):
        return nc.dram_tensor(name, shape, BF16, kind="ExternalInput").ap()
    csq_d = dinb("csqb", [128, T]); ssq_d = dinb("ssqb", [128, T])
    csk_d = dinb("cskb", [128, T]); ssk_d = dinb("sskb", [128, T])
    maskr_d = din("maskr", [128, H, CH])
    gampair_d = din("gampair", [128, NJ, CH])
    kgam_d = din("kgampair", [128, NJ, 128])
    p2t_d = din("p2t", [128, 128])
    ident_d = din("ident", [128, 128])
    h2_d = din("h2", [128, 2])
    g2_d = din("g2", [2, 128])
    idcol_d = din("idcol", [128, NTI])

    out_d = nc.dram_tensor("out", [T, C], F32, kind="ExternalOutput").ap()
    ytk = "ExternalOutput" if "ph2" in DEBUG else "Internal"
    yT_hbm = nc.dram_tensor("yT_hbm", [128, NJ, T], BF16, kind=ytk).ap()
    acck = "ExternalOutput" if "ph3" in DEBUG else "Internal"
    acc_hbm = nc.dram_tensor("acc_hbm", [T + 128, C], F16, kind=acck).ap()
    xn_hbm = nc.dram_tensor("xn_hbm", [T + 128, C], BF16, kind="Internal").ap()
    xnt_hbm = nc.dram_tensor("xnt_hbm", [128, NJ, T], BF16, kind="Internal").ap()
    pair_hbm = nc.dram_tensor("pair_hbm", [NE, T + 128, 2], F32, kind="Internal").ap()
    wt_hbm = nc.dram_tensor("wt_hbm", [NE, T], BF16, kind="Internal").ap()
    if "ph1" in DEBUG:
        dbg_xnT = nc.dram_tensor("dbg_xnT", [128, NJ, T], BF16, kind="ExternalOutput").ap()
        dbg_w = nc.dram_tensor("dbg_w", [128, NTI, NE], F32, kind="ExternalOutput").ap()
    if "ph2" in DEBUG:
        dbg_st = nc.dram_tensor("dbg_st", [128, 2 * NJ], F32, kind="ExternalOutput").ap()

    ctx = ExitStack()
    sbp = lambda name, shape, dt: ctx.enter_context(nc.sbuf_tensor(name, shape, dt))
    w_nat = sbp("w_nat", [128, NTI, NE], F32)
    b1_sb = sbp("b1_sb", [128, NE, NFT], F32)
    S1acc = sbp("S1acc", [128, NJ], F32)
    S2acc = sbp("S2acc", [128, NJ], F32)
    identb = sbp("identb", [128, 128], BF16)

    sems = {}
    def S(name):
        sems[name] = Sem = nc.alloc_semaphore(name)
        return Sem
    pre = S("pre"); pre_w = S("pre_w"); pre_v = S("pre_v")
    ph = S("ph"); vch = S("vch")
    PH = [0]
    _vchn = [0]

    def VC(v, inst):
        inst.then_inc(vch, 1)
        _vchn[0] += 1
        v.wait_ge(vch, _vchn[0])

    def W(e, sem, val):
        if val > 0:
            e.wait_ge(sem, val)

    sch = S("sch")
    _schn = [0]

    def SC(eng, inst):
        inst.then_inc(sch, 1)
        _schn[0] += 1
        eng.wait_ge(sch, _schn[0])

    PRE_TOT = 16 * (5 if SPARSE else 4)
    ctx12 = ExitStack()
    sb12 = lambda name, shape, dt: ctx12.enter_context(nc.sbuf_tensor(name, shape, dt))
    wq_bf = sb12("wq_bf", [128, NJ, 3 * C], BF16)
    xnT = sb12("xnT", [128, NJ, T], BF16)
    p2tb = sb12("p2tb", [128, 128], BF16)
    if SPARSE:
        offi = sb12("offi", [128, NTI, NE], I32)
        idcol_sb = sb12("idcol_sb", [128, NTI], F32)
        pairs_all = sb12("pairs_all", [128, NTI, NE, 2], F32)
        ones8 = sb12("ones8", [128, NE], F32)

    ld_x4 = [S(f"ld_x{i}") for i in range(4)]
    s_sq = S("s_sq"); v_va = S("v_va"); s_rstd = S("s_rstd")
    s_xn = S("s_xn"); t_tr = S("t_tr"); s_tre = S("s_tre")
    v_tre = S("v_tre"); t_gate = S("t_gate"); v_g = S("v_g"); v_sm1 = S("v_sm1")
    s_e = S("s_e"); v_w = S("v_w"); t_wt = S("t_wt"); v_wt = S("v_wt")
    st_wt = S("st_wt")
    sp_x = [S("sp_x0"), S("sp_x1")]
    ppf = S("ppf"); z_a = S("z_a")
    t_ot = S("t_ot"); v_ote = S("v_ote"); v_cmp = S("v_cmp")

    ctx1 = ExitStack()
    sb1 = lambda name, shape, dt: ctx1.enter_context(nc.sbuf_tensor(name, shape, dt))
    xt = [sb1(f"xt{i}", [128, C], F32) for i in range(4)]
    ident32 = sb1("ident32", [128, 128], F32)
    if SPARSE:
        offn = sb1("offn", [128, NTI, NE], F32)
    sqscr = [sb1(f"sqscr{i}", [128, C], F32) for i in range(2)]
    xnf = [sb1(f"xnf{i}", [128, C], F32) for i in range(2)]
    xnb = [sb1(f"xnb{i}", [128, C], BF16) for i in range(2)]
    sx = [sb1(f"sx{i}", [128, 1], F32) for i in range(2)]
    sxx = [sb1(f"sxx{i}", [128, 1], F32) for i in range(2)]
    msc = [sb1(f"msc{i}", [128, 1], F32) for i in range(4)]
    msq = sb1("msq", [128, 1], F32)
    veps = [sb1(f"veps{i}", [128, 1], F32) for i in range(2)]
    rstdc = [sb1(f"rstdc{i}", [128, 1], F32) for i in range(2)]
    nbc = [sb1(f"nbc{i}", [128, 1], F32) for i in range(2)]
    Wg_sb = sb1("Wg_sb", [128, NJ, NE], F32)
    Wg_bf = sb1("Wg_bf", [128, NJ, NE], BF16)
    bg_sb = sb1("bg_sb", [1, NE], F32)
    bg_bf = sb1("bg_bf", [1, NE], BF16)
    ones1 = sb1("ones1", [1, 128], BF16)
    lg = [sb1(f"lg{i}", [128, NE], F32) for i in range(2)]
    pe_ = [sb1(f"pe{i}", [128, NE], F32) for i in range(2)]
    lmn = [sb1(f"lmn{i}", [128, 1], F32) for i in range(2)]
    sm = sb1("sm", [128, NE], F32)
    sm2 = sb1("sm2", [128, NE], F32)
    sme = sb1("sme", [128, NE], F32)
    smw = sb1("smw", [128, NE], F32)
    col_a = sb1("col_a", [128, 1], F32)
    col_b = sb1("col_b", [128, 1], F32)
    wT_sb = sb1("wT_sb", [NE, T], BF16)
    if SPARSE:
        mskT = sb1("mskT", [NE, T], F32)
        scanT = sb1("scanT", [NE, T], F32)
        tmp8 = sb1("tmp8", [128, NE], F32)
        pft = sb1("pft", [128, UPE, 2], F32)
        zt16 = sb1("zt16", [128, C], F16)
        zbf = sb1("zbf", [128, C], BF16)

    ctxp1 = ExitStack()
    pt1 = lambda name, shape, dt: ctxp1.enter_context(nc.psum_tensor(name, shape, dt))
    ps_tr = [pt1(f"ps_tr{i}", [128, 128], BF16) for i in range(2)]
    ps_tro = [pt1(f"ps_tro{i}", [128, NE], F32) for i in range(2)]
    ps_gate = [pt1(f"ps_gate{i}", [128, NE], F32) for i in range(2)]
    ps_wt = [pt1(f"ps_wt{i}", [NE, 128], F32) for i in range(2)]

    with nc.Block() as b1blk:
        @b1blk.tensor
        def _(e):
            e.wait_ge(pre, PRE_TOT)
            e.wait_ge(pre_v, 1)
            e.wait_ge(pre_w, 48)

            def peD(t):
                e.wait_ge(s_xn, t + 1)
                for c in range(NJ):
                    g = NJ * t + c
                    W(e, s_tre, g - 1)
                    e.transpose(ps_tr[g % 2][:], xnb[t % 2][:, c * 128:(c + 1) * 128],
                                identb[:]).then_inc(t_tr, 1)

            def peE(t):
                W(e, v_g, t - 1)
                e.wait_ge(s_tre, NJ * t + NJ)
                for c in range(NJ):
                    e.matmul(ps_gate[t % 2][:], xnT[:, c, t * 128:(t + 1) * 128],
                             Wg_bf[:, c, :], start=(c == 0), stop=False)
                e.matmul(ps_gate[t % 2][:], ones1[:], bg_bf[:],
                         start=False, stop=True).then_inc(t_gate, 1)

            for k in range(NTI + 3):
                if 0 <= k - 2 < NTI:
                    peD(k - 2)
                if 0 <= k - 3 < NTI:
                    peE(k - 3)
            for ti in range(NTI):
                e.wait_ge(v_w, ti + 1)
                W(e, v_wt, ti - 1)
                e.transpose(ps_wt[ti % 2][:], w_nat[:, ti, :], ident32[:]).then_inc(t_wt, 1)
            e.wait_ge(v_wt, NTI)
            if SPARSE:
                e.wait_ge(v_cmp, 2)
                e.wait_ge(s_tre, NJ * NTI)
                for ti in range(NTI):
                    W(e, v_ote, ti - 1)
                    e.transpose(ps_tro[ti % 2][:], scanT[:, ti * 128:(ti + 1) * 128],
                                ident32[0:NE, 0:NE]).then_inc(t_ot, 1)
                e.wait_ge(v_cmp, 3)
            e.nop().then_inc(ph, 1)

        @b1blk.scalar
        def _(e):
            def sA(t):
                e.wait_ge(ld_x4[t % 4], 16 * (t // 4 + 1))
                W(e, v_va, t - 1)
                e.activation(sqscr[t % 2][:], xt[t % 4][:], AF.Square,
                             accum_out=sxx[t % 2][:]).then_inc(s_sq, 1)

            def sB(t):
                e.wait_ge(v_va, t + 1)
                W(e, s_xn, t - 1)
                SC(e, e.activation(veps[t % 2][:], veps[t % 2][:], AF.Ln))
                e.activation(rstdc[t % 2][:], veps[t % 2][:], AF.Exp,
                             scale=-0.5).then_inc(s_rstd, 1)

            def sD(t):
                for c in range(NJ):
                    g = NJ * t + c
                    e.wait_ge(t_tr, g + 1)
                    e.activation(xnT[:, c, t * 128:(t + 1) * 128],
                                 ps_tr[g % 2][:], AF.Copy).then_inc(s_tre, 1)

            def sC(t):
                e.wait_ge(v_sm1, t + 1)
                W(e, v_w, t - 1)
                e.activation(pe_[t % 2][:], lg[t % 2][:], AF.Exp,
                             bias=lmn[t % 2][:]).then_inc(s_e, 1)

            for k in range(NTI + 4):
                if 0 <= k < NTI:
                    sA(k)
                if 0 <= k - 1 < NTI:
                    sB(k - 1)
                if 0 <= k - 2 < NTI:
                    sD(k - 2)
                if 0 <= k - 4 < NTI:
                    sC(k - 4)
            e.wait_ge(v_w, NTI)
            e.nop().then_inc(ph, 1)

        @b1blk.vector
        def _(e):
            e.wait_ge(pre, PRE_TOT)
            e.memset(ones1[:], 1.0)
            e.wait_ge(pre, PRE_TOT)
            e.tensor_copy(Wg_bf[:], Wg_sb[:])
            VC(e, e.tensor_copy(bg_bf[:], bg_sb[:]))
            e.memset(S1acc[:], 0.0)
            e.memset(S2acc[:], 0.0).then_inc(pre_v, 1)
            if SPARSE:
                e.memset(pft[:, :, 0:1], float(PADPOS))
                e.memset(pft[:, :, 1:2], 0.0)
                e.memset(zt16[:], 0.0)
                e.memset(zbf[:], 0.0).then_inc(v_cmp, 1)
            def vA(t):
                e.wait_ge(ld_x4[t % 4], 16 * (t // 4 + 1))
                VC(e, e.reduce_sum(sx[t % 2][:], xt[t % 4][:], AX.X))
                VC(e, e.tensor_scalar_mul(msc[t % 4][:], sx[t % 2][:], 1.0 / C))

            def vB(t):
                e.wait_ge(s_sq, t + 1)
                W(e, s_rstd, t - 1)
                VC(e, e.tensor_mul(msq[:], msc[t % 4][:], msc[t % 4][:]))
                VC(e, e.tensor_scalar_mul(veps[t % 2][:], sxx[t % 2][:], 1.0 / C))
                VC(e, e.tensor_sub(veps[t % 2][:], veps[t % 2][:], msq[:]))
                VC(e, e.tensor_scalar_add(veps[t % 2][:], veps[t % 2][:], 1e-5))
                e.nop().then_inc(v_va, 1)

            def vCst(t):
                e.wait_ge(s_rstd, t + 1)
                VC(e, e.scalar_tensor_tensor(nbc[t % 2][:], msc[t % 4][:], -1.0,
                                             rstdc[t % 2][:], ALU.mult, ALU.mult))
                if SPARSE:
                    W(e, sp_x[t % 2], 16 * (t // 2))
                W(e, t_tr, NJ * (t - 1))
                VC(e, e.tensor_scalar(xnf[t % 2][:], xt[t % 4][:], rstdc[t % 2][:],
                                      nbc[t % 2][:], ALU.mult, ALU.add))
                e.tensor_copy(xnb[t % 2][:], xnf[t % 2][:]).then_inc(s_xn, 1)

            def vE(t):
                e.wait_ge(t_gate, t + 1)
                W(e, s_e, t - 1)
                VC(e, e.tensor_copy(lg[t % 2][:], ps_gate[t % 2][:]))
                e.nop().then_inc(v_g, 1)
                VC(e, e.reduce_max(col_a[:], lg[t % 2][:], AX.X))
                VC(e, e.tensor_scalar_mul(lmn[t % 2][:], col_a[:], -1.0))
                e.nop().then_inc(v_sm1, 1)

            def vF(t):
                e.wait_ge(s_e, t + 1)
                VC(e, e.reduce_sum(col_a[:], pe_[t % 2][:], AX.X))
                VC(e, e.reciprocal(col_b[:], col_a[:]))
                VC(e, e.tensor_scalar_mul(sm[:], pe_[t % 2][:], col_b[:]))
                VC(e, e.reduce_max(col_a[:], sm[:], AX.X))
                VC(e, e.tensor_scalar(sme[:], sm[:], col_a[:], None, ALU.is_equal))
                VC(e, e.scalar_tensor_tensor(sm2[:], sme[:], -1e9, sm[:], ALU.mult, ALU.add))
                VC(e, e.reduce_max(col_a[:], sm2[:], AX.X))
                VC(e, e.tensor_scalar(sme[:], sm[:], col_a[:], None, ALU.is_ge))
                VC(e, e.tensor_mul(smw[:], sm[:], sme[:]))
                VC(e, e.reduce_sum(col_a[:], smw[:], AX.X))
                VC(e, e.reciprocal(col_b[:], col_a[:]))
                e.tensor_scalar_mul(w_nat[:, t, :], smw[:], col_b[:]).then_inc(v_w, 1)

            for k in range(NTI + 6):
                if 0 <= k < NTI:
                    vA(k)
                if 0 <= k - 1 < NTI:
                    vB(k - 1)
                if 0 <= k - 2 < NTI:
                    vCst(k - 2)
                if 0 <= k - 5 < NTI:
                    vE(k - 5)
                if 0 <= k - 6 < NTI:
                    vF(k - 6)
            for ti in range(NTI):
                e.wait_ge(t_wt, ti + 1)
                iw = e.tensor_copy(wT_sb[:, ti * 128:(ti + 1) * 128],
                                   ps_wt[ti % 2][:])
                if ti == NTI - 1:
                    VC(e, iw)
                    e.nop().then_inc(v_wt, 1)
                else:
                    iw.then_inc(v_wt, 1)
            if SPARSE:
                VC(e, e.tensor_single_scalar(mskT[:], wT_sb[:], 0.0, ALU.is_gt))
                VC(e, e.tensor_tensor_scan(scanT[:], mskT[:], mskT[:], 0.0,
                                           ALU.add, ALU.bypass))
                VC(e, e.tensor_scalar_add(scanT[:], scanT[:], -1.0))
                VC(e, e.tensor_mul(scanT[:], scanT[:], mskT[:]))
                VC(e, e.tensor_scalar(mskT[:], mskT[:], -float(PADPOS), float(PADPOS),
                                      ALU.mult, ALU.add))
                VC(e, e.tensor_add(scanT[:], scanT[:], mskT[:]))
                e.nop().then_inc(v_cmp, 1)
                for ti in range(NTI):
                    e.wait_ge(t_ot, ti + 1)
                    VC(e, e.tensor_copy(offn[:, ti, :], ps_tro[ti % 2][:]))
                    if ti > 0:
                        VC(e, e.tensor_single_scalar(tmp8[:], offn[:, ti, :],
                                                     float(PADPOS) - 0.5, ALU.is_ge))
                        VC(e, e.scalar_tensor_tensor(offn[:, ti, :], tmp8[:], float(ti),
                                                     offn[:, ti, :], ALU.mult, ALU.add))
                    e.nop().then_inc(v_ote, 1)
                VC(e, e.tensor_copy(offi[:], offn[:]))
                e.nop().then_inc(v_cmp, 1)

            e.wait_ge(st_wt, 16)
            e.nop().then_inc(ph, 1)

        @b1blk.gpsimd
        def _(e):
            e.dma_start(out=wq_bf[:],
                        in_=wqkvT_d.rearrange("(a p) j -> p a j", p=128)).then_inc(pre_w, 16)
            e.dma_start(out=p2tb[:], in_=p2t_d).then_inc(pre_w, 16)
            e.dma_start(out=identb[:], in_=ident_d).then_inc(pre_w, 16)
            if SPARSE:
                for ti in range(NTI):
                    e.wait_ge(s_xn, ti + 1)
                    e.dma_start(out=xn_hbm[ti * 128:(ti + 1) * 128, :],
                                in_=xnb[ti % 2][:]).then_inc(sp_x[ti % 2], 16)
                e.wait_ge(sp_x[0], 16 * (NTI // 2))
                e.wait_ge(sp_x[1], 16 * (NTI // 2))
            e.nop().then_inc(ph, 1)

        @b1blk.sync
        def _(e):
            e.dma_start(out=Wg_sb[:],
                        in_=Wg_d.rearrange("(a p) e -> p a e", p=128)).then_inc(pre, 16)
            e.dma_start(out=bg_sb[:], in_=bg_d).then_inc(pre, 16)
            e.dma_start(out=b1_sb[:], in_=b1_d.rearrange("e p a -> p e a")).then_inc(pre, 16)
            e.dma_start(out=ident32[:], in_=ident_d).then_inc(pre, 16)
            if SPARSE:
                e.dma_start(out=idcol_sb[:], in_=idcol_d).then_inc(pre, 16)
                e.wait_ge(v_cmp, 1)
                for ee in range(NE):
                    e.dma_start(out=pair_hbm[ee].rearrange("(a p) c -> p a c", p=128)[:, 0:UPE, :],
                                in_=pft[:]).then_inc(ppf, 16)
                for zi in range((T + 128) // 128):
                    e.dma_start(out=acc_hbm[zi * 128:(zi + 1) * 128, :],
                                in_=zt16[:]).then_inc(z_a, 16)
                e.dma_start(out=xn_hbm[T:T + 128, :], in_=zbf[:]).then_inc(z_a, 16)
            for ti in range(NTI):
                W(e, s_xn, ti - 3)
                e.dma_start(out=xt[ti % 4][:],
                            in_=x_d[ti * 128:(ti + 1) * 128, :]).then_inc(ld_x4[ti % 4], 16)
            if SPARSE:
                e.wait_ge(ppf, 16 * NE)
                e.wait_ge(z_a, 16 * ((T + 128) // 128 + 1))
            e.wait_ge(v_wt, NTI)
            e.dma_start(out=wt_hbm, in_=wT_sb[:]).then_inc(st_wt, 16)
            nst = 16
            if "ph1" in DEBUG:
                e.wait_ge(s_tre, NJ * NTI)
                e.dma_start(out=dbg_xnT, in_=xnT[:]).then_inc(st_wt, 16)
                e.wait_ge(v_w, NTI)
                e.dma_start(out=dbg_w, in_=w_nat[:]).then_inc(st_wt, 16)
                nst = 48
            e.wait_ge(st_wt, nst)
            e.nop().then_inc(ph, 1)

    ctxp1.close()
    ctx1.close()
    PH[0] = 5

    if MAX_PHASE >= 2:
        tab2 = [S("tab0"), S("tab1")]
        t_qkv = S("t_qkv"); s_raw = S("s_raw"); t_rot = S("t_rot")
        v_qkc = S("v_qkc"); t_at = S("t_at"); v_mask = S("v_mask")
        t_y = S("t_y"); s_ye = S("s_ye"); t_ktr = S("t_ktr"); v_kg = S("v_kg")
        t_s = S("t_s"); v_s = S("v_s"); s_st = S("s_st"); v_st = S("v_st")
        sty2 = [S("st_y0"), S("st_y1")]
        st_xnt = S("st_xnt")
        v_pb = S("v_pb"); g_scp2 = [S("g_scp0"), S("g_scp1")]

        ctx2 = ExitStack()
        sb2 = lambda name, shape, dt: ctx2.enter_context(nc.sbuf_tensor(name, shape, dt))
        tab_sb = [sb2(f"tab{i}", [128, 4, SLAB], BF16) for i in range(2)]
        qT_sl = [sb2(f"qT{i}", [128, NJ, SLAB], BF16) for i in range(2)]
        kT_sl = [sb2(f"kT{i}", [128, NJ, SLAB], BF16) for i in range(2)]
        v_sl = [sb2(f"v{i}", [128, CPS, C], BF16) for i in range(2)]
        yT_sl = [sb2(f"yT{i}", [128, NJ, SLAB], BF16) for i in range(2)]
        maskr_bf = sb2("maskr_bf", [128, H, CH], BF16)
        gamp_bf = sb2("gamp_bf", [128, NJ, CH], BF16)
        kgam_bf = sb2("kgam_bf", [128, NJ, 128], BF16)
        S_f32 = sb2("S_f32", [128, NJ, 64], F32)
        S_bf = sb2("S_bf", [128, NJ, 128], BF16)
        qraw = [sb2(f"qraw{i}", [128, SLAB], BF16) for i in range(2)]
        rtmp1 = sb2("rtmp1", [128, SLAB], BF16)
        at_bf = [sb2(f"at_bf{i}", [128, 2, CH], BF16) for i in range(2)]
        qg_sb = [sb2(f"qg{i}", [128, CH], BF16) for i in range(2)]
        kg_sb = [sb2(f"kg{i}", [128, 128], BF16) for i in range(2)]
        S1tmp = sb2("S1tmp", [128, NJ, 1], F32)
        S2col = [sb2(f"S2col{i}", [128, NJ], F32) for i in range(2)]
        sqscr2 = sb2("sqscr2", [128, SLAB], BF16)

        ctxp2 = ExitStack()
        pt2 = lambda name, shape, dt: ctxp2.enter_context(nc.psum_tensor(name, shape, dt))
        ps_qkv = pt2("ps_qkv", [128, 512], F32)
        ps_rot = pt2("ps_rot", [128, SLAB], F32)
        ps_at = [pt2("ps_at_e", [128, 128], F32), pt2("ps_at_o", [128, 128], F32)]
        ps_y = [pt2(f"ps_y{i}", [128, 256], F32) for i in range(2)]
        ps_ktr = pt2("ps_ktr", [128, 128], BF16)
        ps_s = pt2("ps_s", [128, 128], F32)

        NG = 16 + 2 * CPS
        TABS_D = [csq_d, ssq_d, csk_d, ssk_d]

        def prod_groups(s):
            gs = []
            for part in range(2):
                for jt in range(NJ):
                    gs.append(("qk", part, jt))
            for ts_ in range(CPS):
                for dmh in range(2):
                    gs.append(("v", ts_, dmh))
            return gs

        with nc.Block() as b2blk:
            def gi_sp(j):
                sj = j // IPS
                ck, p = (j - sj * IPS) // 8, (j - sj * IPS) % 8
                return sj, ck, p

            @b2blk.tensor
            def _(e):
                e.wait_ge(ph, PH[0])
                e.wait_ge(pre_w, 96)

                def emit_prod(s):
                    for gl, (kind, a1, a2) in enumerate(prod_groups(s)):
                        m = s * NG + gl
                        W(e, s_raw, m)
                        if kind == "qk":
                            part, jt = a1, a2
                            joff = part * C + jt * 128
                            for c in range(NJ):
                                im = e.matmul(ps_qkv[:, 0:SLAB], wq_bf[:, c, joff:joff + 128],
                                              xnT[:, c, s * SLAB:(s + 1) * SLAB],
                                              start=(c == 0),
                                              stop=(c == NJ - 1))
                            im.then_inc(t_qkv, 1)
                            r = s * 16 + part * 8 + jt
                            e.wait_ge(s_raw, m + 1)
                            W(e, v_qkc, r)
                            e.matmul(ps_rot[:, 0:SLAB], p2tb[:], qraw[m % 2][:],
                                     start=True, stop=True).then_inc(t_rot, 1)
                        else:
                            ts_, dmh = a1, a2
                            for c in range(NJ):
                                im = e.matmul(ps_qkv[:, 0:512],
                                              xnT[:, c, s * SLAB + ts_ * 128:
                                                  s * SLAB + (ts_ + 1) * 128],
                                              wq_bf[:, c, 2 * C + dmh * 512:2 * C + (dmh + 1) * 512],
                                              start=(c == 0), stop=(c == NJ - 1))
                            im.then_inc(t_qkv, 1)

                def emit_at(j):
                    sj, ck, p = gi_sp(j)
                    ccols = slice(ck * CH, (ck + 1) * CH)
                    e.wait_ge(v_qkc, sj * 16 + 8 + p + 1)
                    W(e, v_mask, j)
                    e.matmul(ps_at[0][:], kT_sl[sj % 2][0:64, p, ccols],
                             qT_sl[sj % 2][0:64, p, ccols], start=True, stop=True)
                    e.matmul(ps_at[1][:], kT_sl[sj % 2][64:128, p, ccols],
                             qT_sl[sj % 2][64:128, p, ccols], start=True,
                             stop=True).then_inc(t_at, 1)

                def emit_supd(j):
                    if j < 0:
                        return
                    sj, ck, p = gi_sp(j)
                    e.wait_ge(v_kg, j + 1)
                    W(e, v_s, j - 1)
                    e.matmul(ps_s[:, 0:64], kg_sb[j % 2][:],
                             v_sl[sj % 2][:, ck, p * 128:p * 128 + 64],
                             start=True, stop=True)
                    e.matmul(ps_s[:, 64:128], kg_sb[j % 2][:],
                             v_sl[sj % 2][:, ck, p * 128 + 64:p * 128 + 128],
                             start=True, stop=True).then_inc(t_s, 1)

                def emit_ret(s):
                    i0 = s * IPS
                    e.wait_ge(s_raw, (s + 1) * NG)
                    emit_at(i0)
                    emit_at(i0 + 1)
                    for i in range(i0, i0 + IPS):
                        sj, ck, p = gi_sp(i)
                        e.wait_ge(v_mask, i + 1)
                        W(e, s_ye, i - 1)
                        W(e, v_s, i - 7)
                        vpair = v_sl[s % 2][:, ck, p * 128:(p + 1) * 128]
                        e.matmul(ps_y[i % 2][:, 0:128], vpair, at_bf[i % 2][:, 0, :],
                                 start=True, stop=False)
                        e.matmul(ps_y[i % 2][:, 0:128], S_bf[:, p, :], qg_sb[i % 2][:],
                                 start=False, stop=True)
                        e.matmul(ps_y[i % 2][:, 128:256], vpair, at_bf[i % 2][:, 1, :],
                                 start=True, stop=False)
                        e.matmul(ps_y[i % 2][:, 128:256], S_bf[:, p, :], qg_sb[i % 2][:],
                                 start=False, stop=True).then_inc(t_y, 1)
                        W(e, v_kg, i)
                        e.transpose(ps_ktr[:], kT_sl[s % 2][:, p, ck * CH:(ck + 1) * CH],
                                    identb[:]).then_inc(t_ktr, 1)
                        if i > i0:
                            emit_supd(i - 1)
                        if i + 2 < i0 + IPS:
                            emit_at(i + 2)
                    emit_supd(i0 + IPS - 1)

                emit_prod(0)
                for s in range(1, NSLAB + 1):
                    if s < NSLAB:
                        emit_prod(s)
                    emit_ret(s - 1)
                e.wait_ge(v_s, NI)
                e.nop().then_inc(ph, 1)

            @b2blk.vector
            def _(e):
                e.wait_ge(ph, PH[0])
                e.wait_ge(pre_w, 96)
                e.memset(S_f32[:], 0.0)
                e.memset(S_bf[:], 0.0)

                def emit_prodv(s):
                    for gl, (kind, a1, a2) in enumerate(prod_groups(s)):
                        m = s * NG + gl
                        if kind == "qk":
                            part, jt = a1, a2
                            r = s * 16 + part * 8 + jt
                            e.wait_ge(s_raw, m + 1)
                            e.wait_ge(t_rot, r + 1)
                            e.wait_ge(tab2[s % 2], (s // 2 + 1) * 64)
                            if s >= 2 and gl == 0:
                                e.wait_ge(t_at, (s - 1) * IPS)
                                e.wait_ge(t_ktr, (s - 1) * IPS)
                                e.wait_ge(t_y, (s - 1) * IPS)
                                e.wait_ge(t_s, (s - 1) * IPS)
                            ci = 0 if part == 0 else 2
                            dst = (qT_sl if part == 0 else kT_sl)[s % 2][:, jt, :]
                            e.tensor_mul(rtmp1[:], qraw[m % 2][:], tab_sb[s % 2][:, ci, :])
                            VC(e, e.tensor_mul(dst, ps_rot[:], tab_sb[s % 2][:, ci + 1, :]))
                            e.tensor_add(dst, dst, rtmp1[:]).then_inc(v_qkc, 1)

                def emit_maskqg(j):
                    sj, ck, p = gi_sp(j)
                    ccols = slice(ck * CH, (ck + 1) * CH)
                    e.wait_ge(t_at, j + 1)
                    e.tensor_mul(at_bf[j % 2][:, 0, :], ps_at[0][:],
                                 maskr_bf[:, 2 * p, :])
                    e.tensor_mul(at_bf[j % 2][:, 1, :], ps_at[1][:],
                                 maskr_bf[:, 2 * p + 1, :])
                    e.tensor_mul(qg_sb[j % 2][:], qT_sl[sj % 2][:, p, ccols],
                                 gamp_bf[:, p, :]).then_inc(v_mask, 1)

                def emit_tail(j):
                    if j < 0:
                        return
                    sj, ck, p = gi_sp(j)
                    e.wait_ge(t_ktr, j + 1)
                    e.tensor_mul(kg_sb[j % 2][:], ps_ktr[:],
                                 kgam_bf[:, p, :]).then_inc(v_kg, 1)
                    e.wait_ge(t_s, j + 1)
                    ii_ = e.scalar_tensor_tensor(S_f32[0:64, p, :], S_f32[0:64, p, :],
                                                 gam_c[2 * p], ps_s[0:64, 0:64],
                                                 ALU.mult, ALU.add)
                    ii_ = e.scalar_tensor_tensor(S_f32[64:128, p, :], S_f32[64:128, p, :],
                                                 gam_c[2 * p + 1], ps_s[64:128, 64:128],
                                                 ALU.mult, ALU.add)
                    VC(e, ii_)
                    e.tensor_copy(S_bf[0:64, p, 0:64], S_f32[0:64, p, :])
                    e.tensor_copy(S_bf[64:128, p, 64:128],
                                  S_f32[64:128, p, :]).then_inc(v_s, 1)

                def emit_retv(s):
                    i0 = s * IPS
                    for i in range(i0, i0 + IPS):
                        emit_maskqg(i)
                        if i > i0:
                            emit_tail(i - 1)
                    emit_tail(i0 + IPS - 1)
                    e.wait_ge(s_ye, IPS * (s + 1))
                    VC(e, e.tensor_reduce(S1tmp[:], yT_sl[s % 2][:], AX.X, ALU.add))
                    e.tensor_add(S1acc[:], S1acc[:], S1tmp[:, :, 0])
                    e.wait_ge(s_st, s + 1)
                    i2 = e.tensor_add(S2acc[:], S2acc[:], S2col[s % 2][:])
                    if s == NSLAB - 1:
                        VC(e, i2)
                        e.nop().then_inc(v_st, 1)
                    else:
                        i2.then_inc(v_st, 1)

                emit_prodv(0)
                if SPARSE:
                    e.wait_ge(pre, PRE_TOT)
                    VC(e, e.memset(ones8[:], 1.0))
                    for ti in range(NTI):
                        e.tensor_scalar_mul(pairs_all[:, ti, :, 0], ones8[:],
                                            idcol_sb[:, ti:ti + 1])
                    iv = e.tensor_copy(pairs_all[:, :, :, 1], w_nat[:, :, :])
                    VC(e, iv)
                    e.nop().then_inc(v_pb, 1)
                for s in range(1, NSLAB + 1):
                    if s < NSLAB:
                        emit_prodv(s)
                    emit_retv(s - 1)
                e.nop().then_inc(ph, 1)

            @b2blk.scalar
            def _(e):
                e.wait_ge(ph, PH[0])

                def emit_prods(s):
                    for gl, (kind, a1, a2) in enumerate(prod_groups(s)):
                        m = s * NG + gl
                        e.wait_ge(t_qkv, m + 1)
                        if kind == "qk":
                            e.activation(qraw[m % 2][:], ps_qkv[:, 0:SLAB],
                                         AF.Copy).then_inc(s_raw, 1)
                        else:
                            ts_, dmh = a1, a2
                            e.activation(v_sl[s % 2][:, ts_, dmh * 512:(dmh + 1) * 512],
                                         ps_qkv[:, 0:512], AF.Copy).then_inc(s_raw, 1)

                def emit_rets(s):
                    i0 = s * IPS
                    for i in range(i0, i0 + IPS):
                        sj, ck, p = gi_sp(i)
                        ccols = slice(ck * CH, (ck + 1) * CH)
                        e.wait_ge(t_y, i + 1)
                        if s >= 2 and i == i0:
                            e.wait_ge(sty2[s % 2], 16 * (s // 2))
                            e.wait_ge(v_st, s - 1)
                        e.activation(yT_sl[s % 2][0:64, p, ccols],
                                     ps_y[i % 2][0:64, 0:128], AF.Copy)
                        iy = e.activation(yT_sl[s % 2][64:128, p, ccols],
                                          ps_y[i % 2][64:128, 128:256], AF.Copy)
                        if i == i0 + 15:
                            SC(e, iy)
                            e.nop().then_inc(s_ye, 1)
                        else:
                            iy.then_inc(s_ye, 1)
                    W(e, v_st, s - 1)
                    for j in range(NJ):
                        ii_ = e.activation(sqscr2[:], yT_sl[s % 2][:, j, :], AF.Square,
                                           accum_out=S2col[s % 2][:, j:j + 1])
                        if j < NJ - 1:
                            SC(e, ii_)
                    ii_.then_inc(s_st, 1)

                emit_prods(0)
                for s in range(1, NSLAB + 1):
                    if s < NSLAB:
                        emit_prods(s)
                    emit_rets(s - 1)
                e.wait_ge(v_st, NSLAB)
                e.nop().then_inc(ph, 1)

            @b2blk.gpsimd
            def _(e):
                e.wait_ge(ph, PH[0])
                e.dma_start(out=maskr_bf[:], in_=maskr_d).then_inc(pre_w, 16)
                e.dma_start(out=gamp_bf[:], in_=gampair_d).then_inc(pre_w, 16)
                e.dma_start(out=kgam_bf[:], in_=kgam_d).then_inc(pre_w, 16)
                if SPARSE:
                    e.wait_ge(ppf, 16 * NE)
                    e.wait_ge(v_pb, 1)
                    for j in range(NTI * NE):
                        ti, ee = j // NE, j % NE
                        e.indirect_dma_start(
                            out=pair_hbm.rearrange("e r c -> (e r) c"),
                            out_offset=IndirectOffsetOnAxis(offi[:, ti, ee:ee + 1], 0),
                            in_=pairs_all[:, ti, ee, :], in_offset=None,
                            element_offset=ee * (T + 128) * 2,
                        ).then_inc(g_scp2[j % 2], 16)
                    e.wait_ge(g_scp2[0], 16 * NTI * NE // 2)
                    e.wait_ge(g_scp2[1], 16 * NTI * NE // 2)
                e.wait_ge(pre_w, 96)
                e.nop().then_inc(ph, 1)

            @b2blk.sync
            def _(e):
                e.wait_ge(ph, PH[0])
                for s in range(2):
                    for k in range(4):
                        e.dma_start(out=tab_sb[s][:, k, :],
                                    in_=TABS_D[k][:, s * SLAB:(s + 1) * SLAB]).then_inc(tab2[s % 2], 16)
                e.dma_start(out=xnt_hbm, in_=xnT[:]).then_inc(st_xnt, 16)
                for s in range(NSLAB):
                    e.wait_ge(s_ye, IPS * (s + 1))
                    e.dma_start(out=yT_hbm[:, :, s * SLAB:(s + 1) * SLAB],
                                in_=yT_sl[s % 2][:]).then_inc(sty2[s % 2], 16)
                    if s + 2 < NSLAB:
                        e.wait_ge(v_qkc, (s + 1) * 16)
                        for k in range(4):
                            e.dma_start(out=tab_sb[s % 2][:, k, :],
                                        in_=TABS_D[k][:, (s + 2) * SLAB:(s + 3) * SLAB]).then_inc(tab2[s % 2], 16)
                e.wait_ge(sty2[0], 16 * (NSLAB // 2))
                e.wait_ge(sty2[1], 16 * (NSLAB // 2))
                e.wait_ge(st_xnt, 16)
                if "ph2" in DEBUG:
                    e.wait_ge(v_st, NSLAB)
                    e.dma_start(out=dbg_st[:, 0:NJ], in_=S1acc[:]).then_inc(sty2[0], 16)
                    e.dma_start(out=dbg_st[:, NJ:2 * NJ], in_=S2acc[:]).then_inc(sty2[1], 16)
                    e.wait_ge(sty2[0], 16 * (NSLAB // 2) + 16)
                    e.wait_ge(sty2[1], 16 * (NSLAB // 2) + 16)
                e.nop().then_inc(ph, 1)

        ctxp2.close()
        ctx2.close()
        PH[0] += 5
    ctx12.close()

    if MAX_PHASE >= 3 and SPARSE:
        w1g4 = [S(f"w1g{i}") for i in range(4)]
        w2g4 = [S(f"w2g{i}") for i in range(4)]
        t_h = S("t_h"); s_g = S("s_g")
        t_y3 = S("t_y3"); v_e3 = S("v_e3")
        ldpr2 = [S("ld_pr0"), S("ld_pr1")]
        v_ix = S("v_ix"); g_g2 = [S("g_g0"), S("g_g1")]
        t_tp = S("t_tp"); s_tpe = S("s_tpe")
        g_sc = S("g_sc")

        ctx3 = ExitStack()
        sb3 = lambda name, shape, dt: ctx3.enter_context(nc.sbuf_tensor(name, shape, dt))
        W1sb = sb3("W1sb", [128, NJ, DF], BF16)
        W2sb = sb3("W2sb", [128, NFT, C], BF16)
        hTg = [sb3(f"hTg{i}", [128, TB], BF16) for i in range(2)]
        stage = [sb3(f"stage{i}", [128, 2, C], F16) for i in range(2)]
        pairs_sb = [sb3(f"pairs{i}", [128, UPE, 2], F32) for i in range(2)]
        idxi_sb = [sb3(f"idxi{i}", [128, UPE], I32) for i in range(2)]
        xg = [sb3(f"xg{i}", [128, C], BF16) for i in range(2)]
        xgT = [sb3(f"xgT{i}", [128, NJ, TB], BF16) for i in range(2)]

        ctxp3 = ExitStack()
        pt3 = lambda name, shape, dt: ctxp3.enter_context(nc.psum_tensor(name, shape, dt))
        ps_h = [pt3(f"ps_h{i}", [128, TB], F32) for i in range(2)]
        ps_y3 = [pt3(f"ps_y3{i}", [128, C], F32) for i in range(2)]
        ps_tp = [pt3(f"ps_tp{i}", [128, 128], BF16) for i in range(2)]

        with nc.Block() as b3blk:
            @b3blk.tensor
            def _(e):
                e.wait_ge(ph, PH[0])

                def y_mms(ee, k, bg_, ft):
                    idx = bg_ * NFT + ft
                    e.wait_ge(s_g, idx + 1)
                    if ft == 0:
                        W(e, v_e3, bg_)
                        if k == 0:
                            e.wait_ge(w2g4[0], (ee + 1) * 32)
                    elif k == 0 and ft % 4 == 0:
                        e.wait_ge(w2g4[(ft // 4) // 2], (ee + 1) * 32)
                    im = None
                    for a in range(2):
                        for dmh in range(2):
                            im = e.matmul(ps_y3[a][:, dmh * 512:(dmh + 1) * 512],
                                          hTg[ft % 2][:, a * 128:(a + 1) * 128],
                                          W2sb[:, ft, dmh * 512:(dmh + 1) * 512],
                                          start=(ft == 0), stop=(ft == NFT - 1))
                    return im

                for ee in range(NE):
                    for k in range(NB):
                        bg_ = ee * NB + k
                        for uu in range(2):
                            u = 2 * k + uu
                            ug = ee * UPE + u
                            e.wait_ge(g_g2[ug % 2], 16 * (ug // 2 + 1))
                            if bg_ >= 2 and uu == 0:
                                e.wait_ge(s_g, (bg_ - 1) * NFT)
                            for c in range(NJ):
                                tc = ug * NJ + c
                                W(e, s_tpe, tc - 1)
                                e.transpose(ps_tp[tc % 2][:],
                                            xg[u % 2][:, c * 128:(c + 1) * 128],
                                            identb[:]).then_inc(t_tp, 1)
                        for ft in range(NFT):
                            idx = bg_ * NFT + ft
                            if k == 0:
                                e.wait_ge(w1g4[(ft // 4) // 2], (ee + 1) * 32)
                            W(e, s_g, idx - 1)
                            if ft == 0:
                                e.wait_ge(s_tpe, 16 * (bg_ + 1))
                            for c in range(NJ):
                                im = e.matmul(ps_h[ft % 2][:],
                                              W1sb[:, c, ft * 128:(ft + 1) * 128],
                                              xgT[k % 2][:, c, :], start=(c == 0),
                                              stop=(c == NJ - 1))
                            im.then_inc(t_h, 1)
                            if ft >= 1:
                                y_mms(ee, k, bg_, ft - 1)
                        y_mms(ee, k, bg_, NFT - 1).then_inc(t_y3, 1)
                e.wait_ge(v_e3, NBG)
                e.nop().then_inc(ph, 1)

            @b3blk.scalar
            def _(e):
                e.wait_ge(ph, PH[0])
                for ee in range(NE):
                    for k in range(NB):
                        bg_ = ee * NB + k
                        for uu in range(2):
                            u = 2 * k + uu
                            ug = ee * UPE + u
                            for c in range(NJ):
                                tc = ug * NJ + c
                                e.wait_ge(t_tp, tc + 1)
                                e.activation(xgT[k % 2][:, c, uu * 128:(uu + 1) * 128],
                                             ps_tp[tc % 2][:],
                                             AF.Copy).then_inc(s_tpe, 1)
                        for ft in range(NFT):
                            idx = bg_ * NFT + ft
                            e.wait_ge(t_h, idx + 1)
                            e.activation(hTg[ft % 2][:], ps_h[ft % 2][:], AF.Gelu,
                                         bias=b1_sb[:, ee, ft:ft + 1]).then_inc(s_g, 1)
                e.wait_ge(t_y3, NBG)
                e.nop().then_inc(ph, 1)

            @b3blk.vector
            def _(e):
                e.wait_ge(ph, PH[0])
                for ee in range(NE):
                    e.wait_ge(ldpr2[ee % 2], 16 * (ee // 2 + 1))
                    if ee >= 2:
                        e.wait_ge(g_g2[0], 16 * (ee - 1) * UPE // 2)
                        e.wait_ge(g_g2[1], 16 * (ee - 1) * UPE // 2)
                        e.wait_ge(g_sc, 32 * (ee - 1) * NB)
                    VC(e, e.tensor_copy(idxi_sb[ee % 2][:], pairs_sb[ee % 2][:, :, 0]))
                    e.nop().then_inc(v_ix, 1)
                    for k in range(NB):
                        bg_ = ee * NB + k
                        e.wait_ge(t_y3, bg_ + 1)
                        W(e, g_sc, 32 * (bg_ - 1))
                        ii_ = None
                        for a in range(2):
                            for dmh in range(2):
                                ii_ = e.tensor_scalar(
                                    stage[bg_ % 2][:, a, dmh * 512:(dmh + 1) * 512],
                                    ps_y3[a][:, dmh * 512:(dmh + 1) * 512],
                                    pairs_sb[ee % 2][:, 2 * k + a, 1:2], None, ALU.mult)
                        ii_.then_inc(v_e3, 1)
                e.nop().then_inc(ph, 1)

            @b3blk.gpsimd
            def _(e):
                e.wait_ge(ph, PH[0])

                def loadW(ee):
                    for gi in range(8):
                        e.dma_start(
                            out=W1sb[:, :, gi * 512:(gi + 1) * 512],
                            in_=W1_d[ee].rearrange("(a p) f -> p a f", p=128)[:, :, gi * 512:(gi + 1) * 512],
                        ).then_inc(w1g4[gi // 2], 16)
                        e.dma_start(
                            out=W2sb[:, gi * 4:(gi + 1) * 4, :],
                            in_=W2_d[ee].rearrange("(g p) d -> p g d", p=128)[:, gi * 4:(gi + 1) * 4, :],
                        ).then_inc(w2g4[gi // 2], 16)

                loadW(0)
                events = []
                for ee in range(NE):
                    for k in range(NB):
                        events.append(("g", ee, k))
                ev2 = []
                for idx3, (kind, ee, k) in enumerate(events):
                    ev2.append(("gather", ee, k))
                    if k == 0 and ee > 0:
                        ev2.append(("loadW", ee, 0))
                    if idx3 >= 2:
                        pee, pk = events[idx3 - 2][1], events[idx3 - 2][2]
                        ev2.append(("scatter", pee, pk))
                ev2.append(("scatter", NE - 1, NB - 2))
                ev2.append(("scatter", NE - 1, NB - 1))
                for kind, ee, k in ev2:
                    bg_ = ee * NB + k
                    if kind == "gather":
                        e.wait_ge(v_ix, ee + 1)
                        for uu in range(2):
                            u = 2 * k + uu
                            ug = ee * UPE + u
                            if ug >= 2:
                                e.wait_ge(t_tp, NJ * (ug - 1))
                            e.indirect_dma_start(
                                out=xg[u % 2][:], out_offset=None,
                                in_=xn_hbm,
                                in_offset=IndirectOffsetOnAxis(
                                    idxi_sb[ee % 2][:, u:u + 1], 0),
                            ).then_inc(g_g2[ug % 2], 16)
                    elif kind == "scatter":
                        e.wait_ge(v_e3, bg_ + 1)
                        for a in range(2):
                            W(e, g_sc, 16 * (2 * bg_ + a))
                            e.indirect_dma_start(
                                out=acc_hbm,
                                out_offset=IndirectOffsetOnAxis(
                                    idxi_sb[ee % 2][:, 2 * k + a:2 * k + a + 1], 0),
                                in_=stage[bg_ % 2][:, a, :], in_offset=None,
                                compute_op=ALU.add,
                            ).then_inc(g_sc, 16)
                    else:
                        e.wait_ge(t_h, ee * NB * NFT)
                        e.wait_ge(t_y3, ee * NB)
                        loadW(ee)
                e.wait_ge(g_sc, 16 * 2 * NBG)
                e.nop().then_inc(ph, 1)

            @b3blk.sync
            def _(e):
                e.wait_ge(ph, PH[0])
                for ee in range(NE):
                    W(e, v_ix, ee - 1)
                    W(e, v_e3, (ee - 1) * NB)
                    e.dma_start(
                        out=pairs_sb[ee % 2][:],
                        in_=pair_hbm[ee].rearrange("(a p) c -> p a c", p=128)[:, 0:UPE, :],
                    ).then_inc(ldpr2[ee % 2], 16)
                e.nop().then_inc(ph, 1)

        ctxp3.close()
        ctx3.close()
        PH[0] += 5

    if MAX_PHASE >= 3 and not SPARSE:
        w1g8 = [S(f"w1g{i}") for i in range(8)]
        w2g8 = [S(f"w2g{i}") for i in range(8)]
        t_h = S("t_h"); s_g = S("s_g")
        t_y3 = S("t_y3"); v_e3 = S("v_e3"); st_c = S("st_c")

        ctx3 = ExitStack()
        sb3 = lambda name, shape, dt: ctx3.enter_context(nc.sbuf_tensor(name, shape, dt))
        W1sb = sb3("W1sb", [128, NJ, DF], BF16)
        W2sb = sb3("W2sb", [128, NFT, C], BF16)
        hTg = [sb3(f"hTg{i}", [128, TB], BF16) for i in range(2)]
        stage = sb3("stage", [128, 2, C], F16)

        ctxp3 = ExitStack()
        pt3 = lambda name, shape, dt: ctxp3.enter_context(nc.psum_tensor(name, shape, dt))
        ps_h = [pt3(f"ps_h{i}", [128, TB], F32) for i in range(2)]
        ps_y3 = [pt3(f"ps_y3{i}", [128, C], F32) for i in range(2)]

        with nc.Block() as b3blk:
            @b3blk.tensor
            def _(e):
                e.wait_ge(ph, PH[0])
                def y_mms(ee, tb, tbg, ft):
                    idx = tbg * NFT + ft
                    e.wait_ge(s_g, idx + 1)
                    if ft == 0:
                        W(e, v_e3, tbg)
                        if tb == 0:
                            e.wait_ge(w2g8[0], (ee + 1) * 16)
                    elif tb == 0 and ft % 4 == 0:
                        e.wait_ge(w2g8[ft // 4], (ee + 1) * 16)
                    im = None
                    for a in range(2):
                        for dmh in range(2):
                            im = e.matmul(ps_y3[a][:, dmh * 512:(dmh + 1) * 512],
                                          hTg[ft % 2][:, a * 128:(a + 1) * 128],
                                          W2sb[:, ft, dmh * 512:(dmh + 1) * 512],
                                          start=(ft == 0), stop=(ft == NFT - 1))
                    return im

                for ee in range(NE):
                    for tb in range(NTB):
                        tbg = ee * NTB + tb
                        tcols = slice(tb * TB, (tb + 1) * TB)
                        for ft in range(NFT):
                            idx = tbg * NFT + ft
                            if tb == 0:
                                e.wait_ge(w1g8[ft // 4], (ee + 1) * 16)
                            W(e, s_g, idx - 1)
                            for c in range(NJ):
                                im = e.matmul(ps_h[ft % 2][:], W1sb[:, c, ft * 128:(ft + 1) * 128],
                                              xnT[:, c, tcols], start=(c == 0),
                                              stop=(c == NJ - 1))
                            im.then_inc(t_h, 1)
                            if ft >= 1:
                                y_mms(ee, tb, tbg, ft - 1)
                        y_mms(ee, tb, tbg, NFT - 1).then_inc(t_y3, 1)
                e.wait_ge(v_e3, NTBG)
                e.nop().then_inc(ph, 1)

            @b3blk.scalar
            def _(e):
                e.wait_ge(ph, PH[0])
                for ee in range(NE):
                    for tb in range(NTB):
                        for ft in range(NFT):
                            idx = (ee * NTB + tb) * NFT + ft
                            e.wait_ge(t_h, idx + 1)
                            e.activation(hTg[ft % 2][:], ps_h[ft % 2][:], AF.Gelu,
                                         bias=b1_sb[:, ee, ft:ft + 1]).then_inc(s_g, 1)
                e.wait_ge(t_y3, NTBG)
                e.nop().then_inc(ph, 1)

            @b3blk.vector
            def _(e):
                e.wait_ge(ph, PH[0])
                for ee in range(NE):
                    for tb in range(NTB):
                        tbg = ee * NTB + tb
                        e.wait_ge(t_y3, tbg + 1)
                        W(e, st_c, 16 * tbg)
                        ii_ = None
                        for a in range(2):
                            for dmh in range(2):
                                ii_ = e.tensor_scalar(
                                    stage[:, a, dmh * 512:(dmh + 1) * 512],
                                    ps_y3[a][:, dmh * 512:(dmh + 1) * 512],
                                    w_nat[:, tb * 2 + a, ee:ee + 1], None, ALU.mult)
                        ii_.then_inc(v_e3, 1)
                e.wait_ge(st_c, 16 * NTBG)
                e.nop().then_inc(ph, 1)

            @b3blk.gpsimd
            def _(e):
                e.wait_ge(ph, PH[0])

                def loadW(ee):
                    for gi in range(8):
                        e.dma_start(
                            out=W1sb[:, :, gi * 512:(gi + 1) * 512],
                            in_=W1_d[ee].rearrange("(a p) f -> p a f", p=128)[:, :, gi * 512:(gi + 1) * 512],
                        ).then_inc(w1g4[gi // 2], 16)
                        e.dma_start(
                            out=W2sb[:, gi * 4:(gi + 1) * 4, :],
                            in_=W2_d[ee].rearrange("(g p) d -> p g d", p=128)[:, gi * 4:(gi + 1) * 4, :],
                        ).then_inc(w2g4[gi // 2], 16)

                loadW(0)
                for ee in range(NE):
                    for tb in range(NTB):
                        tbg = ee * NTB + tb
                        e.wait_ge(v_e3, tbg + 1)
                        e.dma_start(
                            out=acc_hbm[tb * TB:(tb + 1) * TB, :].rearrange("(a p) d -> p a d", p=128),
                            in_=stage[:],
                            accum_op=(ALU.add if ee > 0 else ALU.bypass),
                        ).then_inc(st_c, 16)
                    if ee + 1 < NE:
                        e.wait_ge(t_h, (ee + 1) * NTB * NFT)
                        e.wait_ge(t_y3, (ee + 1) * NTB)
                        loadW(ee + 1)
                e.wait_ge(st_c, 16 * NTBG)
                e.nop().then_inc(ph, 1)

        ctxp3.close()
        ctx3.close()
        PH[0] += 4

    if MAX_PHASE >= 4:
        g4w = S("g4w"); ld4 = S("ld4"); t_s4 = S("t_s4")
        s_st4 = S("s_st4"); v_m4 = S("v_m4"); ld_xnt = S("ld_xnt")
        ldy2 = [S("ld_y40"), S("ld_y41")]
        t_g4 = S("t_g4"); s_si = S("s_si"); v_gy = S("v_gy")
        t_pr = S("t_pr")
        ldxa2 = [S("ld_xa0"), S("ld_xa1")]
        v_mrg = S("v_mrg")
        sto2 = [S("st_o0"), S("st_o1")]

        ctx4 = ExitStack()
        sb4 = lambda name, shape, dt: ctx4.enter_context(nc.sbuf_tensor(name, shape, dt))
        xnT = sb4("xnT4", [128, NJ, T], BF16)
        wgT_bf = sb4("wgT_bf", [128, NJ, C], BF16)
        wpT_bf = sb4("wpT_bf", [128, NJ, C], BF16)
        yt4 = [sb4(f"yt4_{i}", [128, NJ, SLAB], BF16) for i in range(2)]
        gy = [sb4(f"gy{i}", [128, NJ, SLAB], BF16) for i in range(2)]
        sil = [sb4(f"sil{i}", [128, SLAB], BF16) for i in range(2)]
        tmp_yn = sb4("tmp_yn", [128, SLAB], BF16)
        xt4 = [sb4(f"xt4_{i}", [128, C], F32) for i in range(2)]
        at4 = [sb4(f"at4_{i}", [128, C], F16) for i in range(2)]
        ot4 = [sb4(f"ot4_{i}", [128, C], F32) for i in range(2)]
        wT4 = sb4("wT4", [NE, T], BF16)
        b2f = sb4("b2f", [NE, C], F32)
        b2b = sb4("b2b", [NE, C], BF16)
        gnw_sb = sb4("gnw_sb", [128, NJ], F32)
        gnb_sb = sb4("gnb_sb", [128, NJ], F32)
        h2_sb = sb4("h2_sb", [128, 2], F32)
        g2_sb = sb4("g2_sb", [2, 128], F32)
        sums4 = sb4("sums4", [2, 2 * NJ], F32)
        mh4 = sb4("mh4", [2, NJ], F32)
        rstd4 = sb4("rstd4", [2, NJ], F32)
        mr4 = sb4("mr4", [2, NJ], F32)
        alpha = sb4("alpha", [128, NJ], F32)
        beta = sb4("beta", [128, NJ], F32)
        btmp = sb4("btmp", [128, NJ], F32)

        ctxp4 = ExitStack()
        pt4 = lambda name, shape, dt: ctxp4.enter_context(nc.psum_tensor(name, shape, dt))
        ps_st4 = pt4("ps_st4", [128, 2 * NJ], F32)
        ps_bc = pt4("ps_bc", [128, 2 * NJ], F32)
        ps_g4 = [pt4(f"ps_g4{i}", [128, SLAB], F32) for i in range(2)]
        ps_o = [pt4(f"ps_o{i}", [128, C], F32) for i in range(2)]

        def emit_proj(e, ti):
            s = ti // TPS
            e.wait_ge(v_gy, NJ * (s + 1))
            W(e, v_mrg, ti - 1)
            tcols = slice((ti % TPS) * 128, (ti % TPS) * 128 + 128)
            for dmh in range(2):
                for c in range(NJ):
                    e.matmul(ps_o[ti % 2][:, dmh * 512:(dmh + 1) * 512],
                             gy[s % 2][:, c, tcols],
                             wpT_bf[:, c, dmh * 512:(dmh + 1) * 512],
                             start=(c == 0), stop=False)
                im = e.matmul(ps_o[ti % 2][:, dmh * 512:(dmh + 1) * 512],
                              wT4[:, ti * 128:(ti + 1) * 128],
                              b2b[:, dmh * 512:(dmh + 1) * 512],
                              start=False, stop=True)
            im.then_inc(t_pr, 1)

        with nc.Block() as b4blk:
            @b4blk.tensor
            def _(e):
                e.wait_ge(ph, PH[0])
                e.wait_ge(ld4, 16 * 6)
                e.matmul(ps_st4[0:2, 0:NJ], h2_sb[:], S1acc[:], start=True, stop=True)
                e.matmul(ps_st4[0:2, NJ:2 * NJ], h2_sb[:], S2acc[:], start=True,
                         stop=True).then_inc(t_s4, 1)
                e.wait_ge(v_m4, 2)
                e.matmul(ps_bc[:, 0:NJ], g2_sb[:], rstd4[:], start=True, stop=True)
                e.matmul(ps_bc[:, NJ:2 * NJ], g2_sb[:], mr4[:], start=True,
                         stop=True).then_inc(t_s4, 1)
                e.wait_ge(g4w, 32)
                e.wait_ge(ld_xnt, 16)
                for s in range(NSLAB):
                    cols = slice(s * SLAB, (s + 1) * SLAB)
                    for j in range(NJ):
                        gj = s * NJ + j
                        W(e, s_si, gj - 1)
                        for c in range(NJ):
                            im = e.matmul(ps_g4[gj % 2][:], wgT_bf[:, c, j * 128:(j + 1) * 128],
                                          xnT[:, c, cols], start=(c == 0), stop=(c == NJ - 1))
                        im.then_inc(t_g4, 1)
                    if s >= 1:
                        for ts_ in range(TPS):
                            ti = (s - 1) * TPS + ts_
                            emit_proj(e, ti)
                for ts_ in range(TPS):
                    emit_proj(e, (NSLAB - 1) * TPS + ts_)
                e.nop().then_inc(ph, 1)

            @b4blk.scalar
            def _(e):
                e.wait_ge(ph, PH[0])
                e.wait_ge(v_m4, 1)
                SC(e, e.activation(rstd4[:], sums4[0:2, NJ:2 * NJ], AF.Ln))
                e.activation(rstd4[:], rstd4[:], AF.Exp, scale=-0.5).then_inc(s_st4, 1)
                for s in range(NSLAB):
                    for j in range(NJ):
                        gj = s * NJ + j
                        e.wait_ge(t_g4, gj + 1)
                        W(e, v_gy, gj - 1)
                        e.activation(sil[gj % 2][:], ps_g4[gj % 2][:],
                                     AF.Silu).then_inc(s_si, 1)
                e.nop().then_inc(ph, 1)

            @b4blk.vector
            def _(e):
                e.wait_ge(ph, PH[0])
                e.wait_ge(ld4, 16 * 6)
                e.tensor_copy(b2b[:], b2f[:])
                e.wait_ge(t_s4, 1)
                VC(e, e.tensor_copy(sums4[:], ps_st4[0:2, :]))
                VC(e, e.tensor_scalar_mul(mh4[:], sums4[0:2, 0:NJ], 1.0 / (64.0 * T)))
                VC(e, e.tensor_mul(rstd4[:], mh4[:], mh4[:]))
                VC(e, e.tensor_scalar_mul(sums4[0:2, NJ:2 * NJ], sums4[0:2, NJ:2 * NJ],
                                          1.0 / (64.0 * T)))
                VC(e, e.tensor_sub(sums4[0:2, NJ:2 * NJ], sums4[0:2, NJ:2 * NJ], rstd4[:]))
                VC(e, e.tensor_scalar_add(sums4[0:2, NJ:2 * NJ], sums4[0:2, NJ:2 * NJ],
                                          1e-5))
                e.nop().then_inc(v_m4, 1)
                e.wait_ge(s_st4, 1)
                VC(e, e.tensor_mul(mr4[:], mh4[:], rstd4[:]))
                e.nop().then_inc(v_m4, 1)
                e.wait_ge(t_s4, 2)
                VC(e, e.tensor_mul(alpha[:], ps_bc[:, 0:NJ], gnw_sb[:]))
                VC(e, e.tensor_mul(btmp[:], ps_bc[:, NJ:2 * NJ], gnw_sb[:]))
                VC(e, e.tensor_sub(beta[:], gnb_sb[:], btmp[:]))
                e.nop().then_inc(v_m4, 1)
                def emit_gy(s):
                    for j in range(NJ):
                        gj = s * NJ + j
                        e.wait_ge(ldy2[s % 2], 16 * (s // 2 + 1))
                        e.wait_ge(s_si, gj + 1)
                        W(e, v_gy, gj)
                        VC(e, e.tensor_scalar(tmp_yn[:], yt4[s % 2][:, j, :],
                                              alpha[:, j:j + 1], beta[:, j:j + 1],
                                              ALU.mult, ALU.add))
                        e.tensor_mul(gy[s % 2][:, j, :], sil[gj % 2][:],
                                     tmp_yn[:]).then_inc(v_gy, 1)

                def emit_merge(sm):
                    for ts_ in range(TPS):
                        ti = sm * TPS + ts_
                        e.wait_ge(t_pr, ti + 1)
                        e.wait_ge(ldxa2[ti % 2], 32 * (ti // 2 + 1))
                        W(e, sto2[ti % 2], 16 * (ti // 2))
                        for dmh in range(2):
                            ia = e.tensor_add(ot4[ti % 2][:, dmh * 512:(dmh + 1) * 512],
                                              ps_o[ti % 2][:, dmh * 512:(dmh + 1) * 512],
                                              xt4[ti % 2][:, dmh * 512:(dmh + 1) * 512])
                        VC(e, ia)
                        e.tensor_add(ot4[ti % 2][:], ot4[ti % 2][:],
                                     at4[ti % 2][:]).then_inc(v_mrg, 1)

                emit_gy(0)
                for s in range(1, NSLAB + 1):
                    if s < NSLAB:
                        emit_gy(s)
                    emit_merge(s - 1)
                e.nop().then_inc(ph, 1)

            @b4blk.gpsimd
            def _(e):
                e.wait_ge(ph, PH[0])
                e.dma_start(out=wgT_bf[:],
                            in_=wgT_d.rearrange("(a p) j -> p a j", p=128)).then_inc(g4w, 16)
                e.dma_start(out=wpT_bf[:],
                            in_=wpT_d.rearrange("(a p) j -> p a j", p=128)).then_inc(g4w, 16)

            @b4blk.sync
            def _(e):
                e.wait_ge(ph, PH[0])
                e.wait_ge(st_xnt, 16)
                e.dma_start(out=xnT[:], in_=xnt_hbm).then_inc(ld_xnt, 16)
                e.dma_start(out=wT4[:], in_=wt_hbm).then_inc(ld4, 16)
                e.dma_start(out=b2f[:], in_=b2_d).then_inc(ld4, 16)
                e.dma_start(out=gnw_sb[:], in_=gnw_d).then_inc(ld4, 16)
                e.dma_start(out=gnb_sb[:], in_=gnb_d).then_inc(ld4, 16)
                e.dma_start(out=h2_sb[:], in_=h2_d).then_inc(ld4, 16)
                e.dma_start(out=g2_sb[:], in_=g2_d).then_inc(ld4, 16)
                for s in range(2):
                    e.dma_start(out=yt4[s][:],
                                in_=yT_hbm[:, :, s * SLAB:(s + 1) * SLAB]).then_inc(ldy2[s % 2], 16)
                for ti in range(2):
                    e.dma_start(out=xt4[ti][:],
                                in_=x_d[ti * 128:(ti + 1) * 128, :]).then_inc(ldxa2[ti % 2], 16)
                    e.dma_start(out=at4[ti][:],
                                in_=acc_hbm[ti * 128:(ti + 1) * 128, :]).then_inc(ldxa2[ti % 2], 16)
                for ti in range(NTI):
                    e.wait_ge(v_mrg, ti + 1)
                    e.dma_start(out=out_d[ti * 128:(ti + 1) * 128, :],
                                in_=ot4[ti % 2][:]).then_inc(sto2[ti % 2], 16)
                    if ti + 2 < NTI:
                        e.dma_start(out=xt4[ti % 2][:],
                                    in_=x_d[(ti + 2) * 128:(ti + 3) * 128, :]).then_inc(ldxa2[ti % 2], 16)
                        e.dma_start(out=at4[ti % 2][:],
                                    in_=acc_hbm[(ti + 2) * 128:(ti + 3) * 128, :]).then_inc(ldxa2[ti % 2], 16)
                    if ti % TPS == TPS - 1:
                        s = ti // TPS
                        if s + 2 < NSLAB:
                            e.wait_ge(v_gy, NJ * (s + 1))
                            e.dma_start(out=yt4[s % 2][:],
                                        in_=yT_hbm[:, :, (s + 2) * SLAB:(s + 3) * SLAB]).then_inc(ldy2[s % 2], 16)
                e.wait_ge(sto2[0], 16 * (NTI // 2))
                e.wait_ge(sto2[1], 16 * (NTI // 2))

        ctxp4.close()
        ctx4.close()

    ctx.close()
    return nc


def kernel(**inputs):
    tabs = host_tables()
    x = np.ascontiguousarray(np.asarray(inputs["L0_attn_x"], np.float32))
    w_qkv = np.asarray(inputs["L0_attn_w_qkv"], np.float32)
    w_gated = np.asarray(inputs["L0_attn_w_gated"], np.float32)
    w_proj = np.asarray(inputs["L0_attn_w_proj"], np.float32)
    gnw = np.asarray(inputs["L0_attn_gn_weight"], np.float32)
    gnb = np.asarray(inputs["L0_attn_gn_bias"], np.float32)
    Wg = np.ascontiguousarray(np.asarray(inputs["L0_ffn_Wg"], np.float32))
    bg = np.asarray(inputs["L0_ffn_bg"], np.float32)
    W1 = np.ascontiguousarray(np.asarray(inputs["L0_ffn_W1"], np.float32))
    b1 = np.asarray(inputs["L0_ffn_b1"], np.float32)
    W2 = np.ascontiguousarray(np.asarray(inputs["L0_ffn_W2"], np.float32))
    b2 = np.ascontiguousarray(np.asarray(inputs["L0_ffn_b2"], np.float32))

    common = {
        "wqkvT": np.ascontiguousarray(w_qkv.T),
        "wgT": np.ascontiguousarray(w_gated.T),
        "wpT": np.ascontiguousarray(w_proj.T),
        "gnw_r": np.ascontiguousarray(gnw.reshape(NJ, 128).T),
        "gnb_r": np.ascontiguousarray(gnb.reshape(NJ, 128).T),
        "Wg": Wg,
        "bg": np.ascontiguousarray(bg.reshape(1, NE)),
        "b1r": np.ascontiguousarray(b1.reshape(NE, NFT, 128).transpose(0, 2, 1)),
        "b2": b2,
        "W1": W1,
        "W2": W2,
        "csqb": tabs["csqb"], "ssqb": tabs["ssqb"],
        "cskb": tabs["cskb"], "sskb": tabs["sskb"],
        "maskr": tabs["maskr"], "gampair": tabs["gampair"],
        "kgampair": tabs["kgampair"], "p2t": tabs["p2t"],
        "ident": tabs["ident"], "h2": tabs["h2"], "g2": tabs["g2"],
        "idcol": tabs["idcol"],
    }
    nc = build_program(tabs["gam_c"])
    in_maps = [dict(common, x=np.ascontiguousarray(x[bb])) for bb in range(8)]
    res = run_bass_kernel_spmd(nc, in_maps, core_ids=list(range(8)))
    if DEBUG:
        return res
    return np.stack([np.asarray(r["out"]) for r in res.results], axis=0)
